# version 05
# speedup vs baseline: 1.1081x; 1.0086x over previous
import numpy as np
import ml_dtypes
from contextlib import ExitStack

import concourse.bass as bass
import concourse.mybir as mybir
from concourse.bass_utils import run_bass_kernel_spmd

F32 = mybir.dt.float32
BF16 = mybir.dt.bfloat16
AF = mybir.ActivationFunctionType
ALU = mybir.AluOpType
AX = mybir.AxisListType
BF = ml_dtypes.bfloat16

NC = 8
B = 2
SQ = 8192
D = 1024
NBLK = 64
TPC = 2048
NT = 16
H = 16
HP = 8
DH = 64
LN_EPS = 1e-5



def shard_rows(x):
    xb = np.ascontiguousarray(x).reshape(B, NBLK, 128, x.shape[-1])
    return [np.ascontiguousarray(xb[:, i::8].reshape(TPC, x.shape[-1]))
            for i in range(NC)]


def unshard_rows(parts):
    Dd = parts[0].shape[-1]
    out = np.empty((B, NBLK, 128, Dd), parts[0].dtype)
    for i in range(NC):
        out[:, i::8] = np.asarray(parts[i]).reshape(B, 8, 128, Dd)
    return out.reshape(B, NBLK * 128, Dd)



def emit_ln_plan(nc, bodies, xw_tile, lnw_tile, stat, sq, nt, x_wait, app_sem,
                 rstd_sem, nm_sem, eps_ap=None, out_f32_tile=None):
    def v_ops(v, t):
        x_wait(v, t)
        p = t % 2
        inst = v.reduce_sum(stat[:, 4 + p:5 + p], xw_tile(t), axis=AX.X)
        inst.then_inc(nm_sem.h, 1)
        v.wait_ge(nm_sem.h, 2 * t + 1)
        inst = v.tensor_scalar_mul(stat[:, 0 + p:1 + p], stat[:, 4 + p:5 + p],
                                   -1.0 / D)
        inst.then_inc(nm_sem.h, 1)

    def s_ops(sc, t):
        p = t % 2
        sc.wait_ge(nm_sem.h, 2 * t + 2)
        if t >= 2:
            sc.wait_ge(app_sem.h, t - 1)
        inst = sc.activation(sq[:], xw_tile(t), AF.Square,
                             bias=stat[:, 0 + p:1 + p],
                             accum_out=stat[:, 6 + p:7 + p])
        inst.then_inc(rstd_sem.h, 1)
        sc.wait_ge(rstd_sem.h, 3 * t + 1)
        inst = sc.activation(stat[:, 6 + p:7 + p], stat[:, 6 + p:7 + p], AF.Ln,
                             scale=1.0 / D, bias=eps_ap)
        inst.then_inc(rstd_sem.h, 1)
        sc.wait_ge(rstd_sem.h, 3 * t + 2)
        inst = sc.activation(stat[:, 2 + p:3 + p], stat[:, 6 + p:7 + p], AF.Exp,
                             scale=-0.5)
        inst.then_inc(rstd_sem.h, 1)

    def v2_ops(v, t):
        p = t % 2
        v.wait_ge(rstd_sem.h, 3 * t + 3)
        inst = v.tensor_scalar(lnw_tile(t), xw_tile(t), stat[:, 0 + p:1 + p],
                               stat[:, 2 + p:3 + p], ALU.add, ALU.mult)
        if out_f32_tile is not None:
            inst = v.tensor_scalar(out_f32_tile(t), xw_tile(t),
                                   stat[:, 0 + p:1 + p], stat[:, 2 + p:3 + p],
                                   ALU.add, ALU.mult)
        inst.then_inc(app_sem.h, 1)

    return v_ops, s_ops, v2_ops



def emit_A(nc, io, px="A"):
    xs = io["xs"]
    wqp = io["wqp"]
    wkp = io["wkp"]
    wvf = io["wvf"]
    bqT = io["bqT"]
    bkT = io["bkT"]
    bvf = io["bvf"]
    o_qT = io["qS"]
    o_kT = io["agk_in"]
    o_v = io["agv_in"]

    es = ExitStack()
    sb = lambda n, s, d=F32: es.enter_context(nc.sbuf_tensor(px + n, s, d))
    psal = lambda n, s, d=F32: es.enter_context(nc.psum_tensor(px + n, s, d))
    sem = lambda n: nc.alloc_semaphore(px + n)
    with es:
        xw = sb("xw", [128, 2 * D])
        lnw = sb("lnw", [128, 2 * D], BF16)
        stat = sb("stat", [128, 8])
        sq = sb("sq", [128, D])
        xlnT = sb("xlnT", [128, 8 * TPC], BF16)
        ident = sb("ident", [128, 128], BF16)
        wq_sb = sb("wq_sb", [128, HP * 1024], BF16)
        wk_sb = sb("wk_sb", [128, HP * 1024], BF16)
        wv_sb = sb("wv_sb", [128, 8 * D], BF16)
        bq_sb = sb("bq_sb", [DH, H])
        bk_sb = sb("bk_sb", [DH, H])
        bqp = sb("bqp", [128, HP])
        bkp = sb("bkp", [128, HP])
        bv_sb = sb("bv_sb", [1, D], BF16)
        ones_row = sb("ones_row", [1, 128], BF16)
        eps_sb = sb("eps_sb", [128, 1])
        qT_sb = sb("qT_sb", [128, HP * TPC], BF16)
        kT_sb = sb("kT_sb", [128, HP * TPC], BF16)
        v_sb = sb("v_sb", [128, NT * D], BF16)
        pt0 = psal("pt0", [128, 128], BF16)
        pt1 = psal("pt1", [128, 128], BF16)
        pqk = psal("pqk", [128, 1024])
        pqk2 = psal("pqk2", [128, 1024])

        s_ld = sem("s_ld")
        s_x = sem("s_x")
        s_nm = sem("s_nm")
        s_rstd = sem("s_rstd")
        s_app = sem("s_app")
        s_tr = sem("s_tr")
        s_trv = sem("s_trv")
        s_mm = sem("s_mm")
        s_ev = sem("s_ev")
        s_st = sem("s_st")
        s_cc = sem("s_cc")

        cid = nc.inline_tensor(np.eye(128, dtype=BF), px + "cident")
        cones = nc.inline_tensor(np.ones((1, 128), dtype=BF), px + "cones")

        N_WLOADS = 2 + 2 * HP + 1 + 3
        TRT = 8 * NT
        groups = ([("k", hp, tc) for hp in range(HP) for tc in range(4)]
                  + [("v", tau, hc) for tau in range(NT) for hc in range(2)]
                  + [("q", hp, tc) for hp in range(HP) for tc in range(4)])
        NG = len(groups)
        pslot = [(pqk, 0), (pqk, 512), (pqk2, 0), (pqk2, 512)]

        with nc.Block() as blk:
            @blk.gpsimd
            def _(g):
                g.dma_start(out=ident[:], in_=cid.ap()).then_inc(s_ld, 16)
                g.dma_start(out=ones_row[:], in_=cones.ap()).then_inc(s_ld, 16)
                for hp in range(HP):
                    g.dma_start(out=wq_sb[:, hp * 1024:(hp + 1) * 1024]
                                .rearrange("p (ct e) -> p ct e", e=128),
                                in_=wqp[hp].rearrange("(ct p) e -> p ct e", p=128)
                                ).then_inc(s_ld, 16)
                    g.dma_start(out=wk_sb[:, hp * 1024:(hp + 1) * 1024]
                                .rearrange("p (ct e) -> p ct e", e=128),
                                in_=wkp[hp].rearrange("(ct p) e -> p ct e", p=128)
                                ).then_inc(s_ld, 16)
                g.dma_start(out=wv_sb[:].rearrange("p (ct n) -> p ct n", n=D),
                            in_=wvf.rearrange("(ct p) n -> p ct n", p=128)
                            ).then_inc(s_ld, 16)
                g.dma_start(out=bq_sb[:], in_=bqT).then_inc(s_ld, 16)
                g.dma_start(out=bk_sb[:], in_=bkT).then_inc(s_ld, 16)
                g.dma_start(out=bv_sb[:], in_=bvf).then_inc(s_ld, 16)
                for t in range(NT):
                    if t >= 2:
                        g.wait_ge(s_rstd, 3 * (t - 1))
                        g.wait_ge(s_app, t - 1)
                    g.dma_start(out=xw[:, (t % 2) * D:(t % 2 + 1) * D],
                                in_=xs[t * 128:(t + 1) * 128, :]).then_inc(s_x, 16)

            xw_tile = lambda t: xw[:, (t % 2) * D:(t % 2 + 1) * D]
            lnw_tile = lambda t: lnw[:, (t % 2) * D:(t % 2 + 1) * D]

            def x_wait(v, t):
                v.wait_ge(s_x, (t + 1) * 16)

            v_ops, s_ops, v2_ops = emit_ln_plan(
                nc, None, xw_tile, lnw_tile, stat, sq, NT, x_wait,
                SemRef(s_app), SemRef(s_rstd), SemRef(s_nm), eps_ap=eps_sb[:])

            @blk.vector
            def _(v):
                v.memset(eps_sb[:], LN_EPS)
                v.wait_ge(s_ld, N_WLOADS * 16)
                for hp in range(HP):
                    v.tensor_scalar_mul(bqp[0:64, hp:hp + 1],
                                        bq_sb[:, 2 * hp:2 * hp + 1], 1.0 / 256.0)
                    v.tensor_scalar_mul(bqp[64:128, hp:hp + 1],
                                        bq_sb[:, 2 * hp + 1:2 * hp + 2], 1.0 / 256.0)
                    v.tensor_scalar_mul(bkp[0:64, hp:hp + 1],
                                        bk_sb[:, 2 * hp:2 * hp + 1], 1.0 / 32.0)
                    v.tensor_scalar_mul(bkp[64:128, hp:hp + 1],
                                        bk_sb[:, 2 * hp + 1:2 * hp + 2], 1.0 / 32.0)
                for t in range(NT):
                    if t >= 2:
                        v.wait_ge(s_tr, 8 * (t - 1))
                    v_ops(v, t)
                    v2_ops(v, t)
                    if t >= 1:
                        for ct in range(8):
                            j = 8 * (t - 1) + ct
                            v.wait_ge(s_tr, j + 1)
                            v.tensor_copy(
                                xlnT[:, ct * TPC + (t - 1) * 128:
                                     ct * TPC + t * 128],
                                (pt0 if j % 2 == 0 else pt1)[:]
                            ).then_inc(s_trv, 1)
                for ct in range(8):
                    j = 8 * (NT - 1) + ct
                    v.wait_ge(s_tr, j + 1)
                    v.tensor_copy(
                        xlnT[:, ct * TPC + (NT - 1) * 128:ct * TPC + NT * 128],
                        (pt0 if j % 2 == 0 else pt1)[:]).then_inc(s_trv, 1)

            @blk.scalar
            def _(sc):
                for t in range(NT):
                    s_ops(sc, t)
                for gi, grp in enumerate(groups):
                    pbuf, off = pslot[gi % 4]
                    sc.wait_ge(s_mm, gi + 1)
                    kind = grp[0]
                    if kind == "q":
                        _, hp, tc = grp
                        sc.activation(qT_sb[:, hp * TPC + tc * 512:
                                            hp * TPC + (tc + 1) * 512],
                                      pbuf[:, off:off + 512], AF.Identity,
                                      bias=bqp[:, hp:hp + 1], scale=1.0 / 256.0
                                      ).then_inc(s_ev, 1)
                    elif kind == "k":
                        _, hp, tc = grp
                        sc.activation(kT_sb[:, hp * TPC + tc * 512:
                                            hp * TPC + (tc + 1) * 512],
                                      pbuf[:, off:off + 512], AF.Identity,
                                      bias=bkp[:, hp:hp + 1], scale=1.0 / 32.0
                                      ).then_inc(s_ev, 1)
                    else:
                        _, tau, hc = grp
                        sc.activation(v_sb[:, tau * D + hc * 512:
                                           tau * D + (hc + 1) * 512],
                                      pbuf[:, off:off + 512], AF.Copy,
                                      scale=1.0 / 32.0).then_inc(s_ev, 1)

            @blk.tensor
            def _(t):
                for tt in range(NT):
                    t.wait_ge(s_app, tt + 1)
                    for ct in range(8):
                        j = 8 * tt + ct
                        if j >= 2:
                            t.wait_ge(s_trv, j - 1)
                        t.transpose((pt0 if j % 2 == 0 else pt1)[:],
                                    lnw[:, (tt % 2) * D + ct * 128:
                                        (tt % 2) * D + (ct + 1) * 128],
                                    ident[:]).then_inc(s_tr, 1)
                t.wait_ge(s_trv, TRT)
                t.wait_ge(s_ld, N_WLOADS * 16)
                for gi, grp in enumerate(groups):
                    pbuf, off = pslot[gi % 4]
                    if gi >= 4:
                        t.wait_ge(s_ev, gi - 3)
                    kind = grp[0]
                    if kind in ("q", "k"):
                        _, hp, tc = grp
                        wsb = wq_sb if kind == "q" else wk_sb
                        for ct in range(8):
                            inst = t.matmul(
                                pbuf[:, off:off + 512],
                                wsb[:, hp * 1024 + ct * 128:
                                    hp * 1024 + (ct + 1) * 128],
                                xlnT[:, ct * TPC + tc * 512:
                                     ct * TPC + (tc + 1) * 512],
                                start=(ct == 0), stop=(ct == 7))
                        inst.then_inc(s_mm, 1)
                    else:
                        _, tau, hc = grp
                        t.matmul(pbuf[:, off:off + 512], ones_row[:],
                                 bv_sb[0:1, hc * 512:(hc + 1) * 512],
                                 start=True, stop=False)
                        for ct in range(8):
                            inst = t.matmul(
                                pbuf[:, off:off + 512],
                                xlnT[:, ct * TPC + tau * 128:
                                     ct * TPC + (tau + 1) * 128],
                                wv_sb[:, ct * D + hc * 512:
                                      ct * D + (hc + 1) * 512],
                                start=False, stop=(ct == 7))
                        inst.then_inc(s_mm, 1)

            @blk.sync
            def _(sy):
                for hp in range(HP):
                    sy.wait_ge(s_ev, 4 * hp + 4)
                    sy.dma_start(out=o_kT[hp], in_=kT_sb[:, hp * TPC:(hp + 1) * TPC]
                                 ).then_inc(s_st, 16)
                for tau in range(NT):
                    sy.wait_ge(s_ev, 32 + 2 * tau + 2)
                    sy.dma_start(out=o_v[tau * 128:(tau + 1) * 128, :],
                                 in_=v_sb[:, tau * D:(tau + 1) * D]
                                 ).then_inc(s_st, 16)
                for hp in range(HP):
                    sy.wait_ge(s_ev, 64 + 4 * hp + 4)
                    sy.dma_start(out=o_qT[hp], in_=qT_sb[:, hp * TPC:(hp + 1) * TPC]
                                 ).then_inc(s_st, 16)
                sy.wait_ge(s_st, 32 * 16)

            @blk.gpsimd
            def _(g):
                g.wait_ge(s_st, 8 * 16)
                g.collective_compute(
                    "AllGather", ALU.bypass,
                    replica_groups=[list(range(NC))],
                    ins=[io["agk_in"]], outs=[io["agk_out"]]).then_inc(s_cc)
                g.wait_ge(s_st, 24 * 16)
                g.collective_compute(
                    "AllGather", ALU.bypass,
                    replica_groups=[list(range(NC))],
                    ins=[io["agv_in"]], outs=[io["agv_out"]]).then_inc(s_cc)
                g.wait_ge(s_cc, 2)




def attn_plan():
    items = []
    st = sa = spv = svd = sz = 0
    for b in range(B):
        for hp in range(HP):
            for h2 in range(2):
                for qc in range(2):
                    kts = 32 * qc + 32
                    G = (kts + 1) // 2
                    qlo = []
                    for kt in range(kts):
                        a = kt // 8
                        qlo.append(128 * max(0, a - 4 * qc))
                    diag_of_group = {}
                    for gi in range(G):
                        dk = []
                        for kt in (2 * gi, 2 * gi + 1):
                            if kt >= kts:
                                continue
                            a = kt // 8
                            if a >= 4 * qc:
                                dk.append((kt, a - 4 * qc, kt % 8))
                        if dk:
                            diag_of_group[gi] = dk
                    items.append(dict(b=b, hp=hp, h2=h2, qc=qc,
                                      kts=kts, G=G, qlo=qlo,
                                      diag=diag_of_group,
                                      st0=st, sa0=sa, spv0=spv, svd0=svd,
                                      sz0=sz))
                    st += G
                    sa += G
                    spv += G
                    svd += len(diag_of_group)
                    sz += 1
    return items, st, sa, spv, svd, sz


SERIAL_ATTN = False
WO_BIAS_MM = True


def emit_B(nc, io, px="B", dbg=False):
    xs = io["xs"]
    qTi = io["qS"]
    agk_out = io["agk_out"]
    agv_out = io["agv_out"]
    wop = io["wop"]
    bo = io["bo"]
    wqT = io["wqT"]
    rtT = io["rtT"]
    lv = io["lv"]
    woutT = io["woutT"]
    wob = io["wob"]
    mk = io["mk"]
    o_x2 = io["x2S"]

    cid = nc.inline_tensor(np.eye(128, dtype=BF), px + "cident")
    cones = nc.inline_tensor(np.ones((1, 128), dtype=BF), px + "cones")
    c8 = nc.inline_tensor(np.full((1, 64), 0.125, dtype=np.float32), px + "c8")
    pmat_np = np.zeros((512, 256), dtype=BF)
    for m in range(256):
        for lev in range(8):
            node = (2 ** lev - 1) + (m >> (8 - lev))
            branch = (m >> (7 - lev)) & 1
            pmat_np[node if branch else 256 + node, m] = 1.0
    cpm = nc.inline_tensor(pmat_np, px + "cpmat")

    items, ST, SA, SPV, SVD, SZ = attn_plan()

    es = ExitStack()
    sb = lambda n, s, d=F32: es.enter_context(nc.sbuf_tensor(px + n, s, d))
    sem = lambda n: nc.alloc_semaphore(px + n)
    with es:
        resid = sb("resid", [128, NT * D])
        ident = sb("ident", [128, 128], BF16)
        mk_sb = sb("mk_sb", [128, 8 * 128], BF16)
        ones_row = sb("ones_row", [1, 128], BF16)
        oneseighth = sb("oneseighth", [1, 64])
        bo8 = sb("bo8", [128, D])
        wob_sb = sb("wob_sb", [128, D])
        wtmp = sb("wtmp", [128, D])
        stat = sb("stat", [128, 8])
        sqs = sb("sqs", [128, D])
        eps_sb = sb("eps_sb", [128, 1])
        one_sb = sb("one_sb", [128, 1])
        rl_sb = sb("rl_sb", [1, 512])
        bc_sb = sb("bc_sb", [64, 512])

        s_ld = sem("s_ld")
        s_ldg = sem("s_ldg")
        s_x = sem("s_x")
        s_kl = sem("s_kl")
        s_vl = sem("s_vl")
        s_t = sem("s_t")
        s_a = sem("s_a")
        s_vd = sem("s_vd")
        s_pv = sem("s_pv")
        s_zr = sem("s_zr")
        s_zb = sem("s_zb")
        s_zc = sem("s_zc")
        s_ze = sem("s_ze")
        s_pre = sem("s_pre")
        s_wo = sem("s_wo")
        s_woe = sem("s_woe")
        s_nm = sem("s_nm")
        s_rstd = sem("s_rstd")
        s_app = sem("s_app")
        s_tr = sem("s_tr")
        s_trv = sem("s_trv")
        s_mm = sem("s_mm")
        s_ev = sem("s_ev")
        s_sc = sem("s_sc")
        s_st = sem("s_st")
        s_vf = sem("s_vf")
        s_cc = sem("s_cc")
        VF0 = 0

        with nc.Block() as blk:
            @blk.sync
            def _(sy):
                sy.dma_start(out=ident[:], in_=cid.ap()).then_inc(s_ld, 16)
                sy.dma_start(out=mk_sb[:].rearrange("p (r n) -> p r n", n=128),
                             in_=mk.rearrange("r p n -> p r n")).then_inc(s_ld, 16)
                sy.dma_start(out=ones_row[:], in_=cones.ap()).then_inc(s_ld, 16)
                sy.dma_start(out=oneseighth[:], in_=c8.ap()).then_inc(s_ld, 16)
                sy.dma_start(out=wtmp[:], in_=bo).then_inc(s_ld, 16)
                for t in range(NT):
                    sy.dma_start(out=resid[:, t * D:(t + 1) * D],
                                 in_=xs[t * 128:(t + 1) * 128, :]).then_inc(s_x, 16)
            NLD0 = 5

            esz = ExitStack()
            z2T = esz.enter_context(nc.sbuf_tensor(px + "z2T", [128, NT * D], BF16))
            with ExitStack() as es1:
                sb1 = lambda n, s, d=F32: es1.enter_context(nc.sbuf_tensor(px + n, s, d))
                q_sb = sb1("q_sb", [128, HP * TPC], BF16)
                kp_sb = sb1("kp_sb", [128, 2 * SQ], BF16)
                vh_sb = sb1("vh_sb", [128, 2 * 64 * 65], BF16)
                esb0 = sb1("esb0", [128, 1024], BF16)
                esb1 = sb1("esb1", [128, 1024], BF16)
                ps_s0 = es1.enter_context(nc.psum_tensor(px + "ps_s0", [128, 1024], F32))
                ps_s1 = es1.enter_context(nc.psum_tensor(px + "ps_s1", [128, 1024], F32))
                ps_z = es1.enter_context(nc.psum_tensor(px + "ps_z", [65, 512], F32))
                ps_b = es1.enter_context(nc.psum_tensor(px + "ps_b", [64, 512], F32))
                psl = [ps_s0, ps_s1]
                esl = [esb0, esb1]

                @blk.vector
                def _(v):
                    v.memset(eps_sb[:], LN_EPS)
                    v.memset(one_sb[:], 1.0)
                    v.wait_ge(s_ld, NLD0 * 16)
                    v.tensor_scalar_mul(bo8[:], wtmp[:], 0.125).then_inc(s_pre, 1)

                @blk.gpsimd
                def _(g):
                    for hp in range(HP):
                        g.dma_start(out=q_sb[:, hp * TPC:(hp + 1) * TPC],
                                    in_=qTi[hp]).then_inc(s_ldg, 16)
                    g.memset(vh_sb[:].rearrange("p (s kt e) -> p s kt e", s=2, e=65)
                             [:, :, :, 64:65], 1.0).then_inc(s_pre, 1)

                @blk.sync
                def _(sy):
                    loads = ([("k", m, 4 * m) for m in range(B * HP)]
                             + [("v", n, 2 * n) for n in range(B * H)])
                    loads.sort(key=lambda x: (x[2], x[0]))
                    kcnt = vcnt = 0
                    for typ, idx, _need in loads:
                        if typ == "k":
                            b, hp = divmod(idx, HP)
                            if idx >= 2:
                                e_it = items[(idx - 2) * 4 + 3]
                                sy.wait_ge(s_t, e_it["st0"] + e_it["G"])
                            dstk = kp_sb[:, (idx % 2) * SQ:(idx % 2 + 1) * SQ] \
                                .rearrange("p (k c t) -> p k c t", c=NC, t=128)
                            for c in range(NC):
                                sy.dma_start(
                                    out=dstk[:, :, c, :],
                                    in_=agk_out[c * HP + hp]
                                    [:, b * 1024:(b + 1) * 1024]
                                    .rearrange("p (k t) -> p k t", t=128)
                                ).then_inc(s_kl, 16)
                            kcnt += 1
                        else:
                            b, h = divmod(idx, H)
                            if idx >= 2:
                                e_it = items[(idx - 2) * 2 + 1]
                                sy.wait_ge(s_pv, e_it["spv0"] + e_it["G"])
                            dstv = vh_sb[:, (idx % 2) * 4160:(idx % 2 + 1) * 4160] \
                                .rearrange("p (k c e) -> p k c e", c=NC, e=65)
                            for c in range(NC):
                                sy.dma_start(
                                    out=dstv[:, :, c, 0:64],
                                    in_=agv_out[c * TPC + b * 1024:
                                                c * TPC + (b + 1) * 1024,
                                                h * DH:(h + 1) * DH]
                                    .rearrange("(k p) e -> p k e", p=128)
                                ).then_inc(s_vl, 16)
                            vcnt += 1

                @blk.scalar
                def _(sc):
                    for it in items:
                        for g in range(it["G"]):
                            ga = it["sa0"] + g
                            sc.wait_ge(s_t, it["st0"] + g + 1)
                            if ga >= 2:
                                sc.wait_ge(s_pv, ga - 1)
                            lo = min((kt % 2) * 512 + it["qlo"][kt]
                                     for kt in (2 * g, 2 * g + 1) if kt < it["kts"])
                            hi = 512 if (2 * g + 1 >= it["kts"]) else 1024
                            sc.activation(esl[g % 2][:, lo:hi],
                                          psl[g % 2][:, lo:hi], AF.Exp
                                          ).then_inc(s_a, 1)

                @blk.tensor
                def _(t):
                    t.wait_ge(s_ld, NLD0 * 16)
                    t.wait_ge(s_ldg, HP * 16)
                    t.wait_ge(s_pre, 2)
                    for it in items:
                        b, hp, h2, qc = it["b"], it["hp"], it["h2"], it["qc"]
                        m = b * HP + hp
                        n = m * 2 + h2
                        G, kts, qlo = it["G"], it["kts"], it["qlo"]
                        t.wait_ge(s_kl, (m + 1) * NC * 16)
                        t.wait_ge(s_vl, (n + 1) * NC * 16)
                        kbase = (m % 2) * SQ
                        vbase = (n % 2) * 4160
                        qbase = hp * TPC + b * 1024 + qc * 512
                        if it["sz0"] >= 1:
                            t.wait_ge(s_ze, it["sz0"])
                        if it["sa0"] >= 1:
                            t.wait_ge(s_a, it["sa0"])

                        def pv_group(gp):
                            final = (gp == G - 1)
                            t.wait_ge(s_a, it["sa0"] + gp + 1)
                            nd = sum(1 for gg in it["diag"] if gg <= gp)
                            if nd:
                                t.wait_ge(s_vd, it["svd0"] + nd)
                            inst = None
                            for kt in (2 * gp, 2 * gp + 1):
                                if kt >= kts:
                                    continue
                                inst = t.matmul(
                                    ps_z[:, qlo[kt]:512],
                                    vh_sb[:, vbase + kt * 65:vbase + (kt + 1) * 65],
                                    esl[gp % 2][:, (kt % 2) * 512 + qlo[kt]:
                                                (kt % 2) * 512 + 512],
                                    start=(kt == 0), stop=final)
                            inst.then_inc(s_pv, 1)

                        for g in range(G):
                            if SERIAL_ATTN:
                                t.wait_ge(s_a, it["sa0"] + g)
                            elif g >= 2:
                                t.wait_ge(s_a, it["sa0"] + g - 1)
                            inst = None
                            for kt in (2 * g, 2 * g + 1):
                                if kt >= kts:
                                    continue
                                inst = t.matmul(
                                    psl[g % 2][:, (kt % 2) * 512 + qlo[kt]:
                                               (kt % 2) * 512 + 512],
                                    kp_sb[h2 * 64:(h2 + 1) * 64,
                                          kbase + kt * 128:kbase + (kt + 1) * 128],
                                    q_sb[h2 * 64:(h2 + 1) * 64,
                                         qbase + qlo[kt]:qbase + 512],
                                    start=True, stop=True)
                            inst.then_inc(s_t, 1)
                            if SERIAL_ATTN:
                                pv_group(g)
                            elif g >= 1:
                                pv_group(g - 1)
                        if not SERIAL_ATTN:
                            pv_group(G - 1)
                        t.wait_ge(s_zr, it["sz0"] + 1)
                        if it["sz0"] >= 1:
                            t.wait_ge(s_zc, it["sz0"])
                        t.matmul(ps_b[:], oneseighth[:], rl_sb[:],
                                 start=True, stop=True).then_inc(s_zb, 1)

                @blk.vector
                def _(v):
                    for it in items:
                        b, hp, h2, qc = it["b"], it["hp"], it["h2"], it["qc"]
                        for gi, dks in sorted(it["diag"].items()):
                            v.wait_ge(s_a, it["sa0"] + gi + 1)
                            inst = None
                            for kt, koff, r in dks:
                                pos = (kt % 2) * 512 + koff * 128
                                inst = v.tensor_tensor(
                                    esl[gi % 2][:, pos:pos + 128],
                                    esl[gi % 2][:, pos:pos + 128],
                                    mk_sb[:, r * 128:(r + 1) * 128],
                                    ALU.mult)
                            inst.then_inc(s_vd, 1)
                        v.wait_ge(s_pv, it["spv0"] + it["G"])
                        v.reciprocal(rl_sb[:], ps_z[64:65, :]).then_inc(s_zr, 1)
                        v.wait_ge(s_zb, it["sz0"] + 1)
                        v.tensor_copy(bc_sb[:], ps_b[:]).then_inc(s_zc, 1)
                        v.wait_ge(s_zc, it["sz0"] + 1)
                        zc = (b * 8 + hp) * 1024 + qc * 512
                        v.tensor_tensor(z2T[h2 * 64:(h2 + 1) * 64, zc:zc + 512],
                                        ps_z[0:64, :], bc_sb[:],
                                        ALU.mult).then_inc(s_ze, 1)

            with ExitStack() as es2:
                wo_sb = es2.enter_context(nc.sbuf_tensor(px + "wo_sb", [128, HP * D], BF16))
                ps_o0 = es2.enter_context(nc.psum_tensor(px + "ps_o0", [128, 1024], F32))
                ps_o1 = es2.enter_context(nc.psum_tensor(px + "ps_o1", [128, 1024], F32))

                @blk.gpsimd
                def _(g):
                    g.wait_ge(s_ze, SZ)
                    for hp in range(HP):
                        g.dma_start(out=wo_sb[:, hp * D:(hp + 1) * D],
                                    in_=wop[hp]).then_inc(s_ldg, 16)
                NLDG1 = HP + HP

                @blk.tensor
                def _(t):
                    t.wait_ge(s_ldg, NLDG1 * 16)
                    t.wait_ge(s_pre, 2)
                    for tau in range(NT):
                        b, k = divmod(tau, 8)
                        pso = [ps_o0, ps_o1][tau % 2]
                        if tau >= 2:
                            t.wait_ge(s_woe, tau - 1)
                        last = items[(b * HP + HP - 1) * 4 + 3]
                        t.wait_ge(s_ze, last["sz0"] + 1)
                        inst = None
                        for half in range(2):
                            for hp in range(HP):
                                inst = t.matmul(
                                    pso[:, half * 512:(half + 1) * 512],
                                    z2T[:, (b * 8 + hp) * 1024 + k * 128:
                                        (b * 8 + hp) * 1024 + (k + 1) * 128],
                                    wo_sb[:, hp * D + half * 512:
                                          hp * D + (half + 1) * 512],
                                    start=(hp == 0), stop=(hp == HP - 1))
                        inst.then_inc(s_wo, 1)

                @blk.vector
                def _(v):
                    v.wait_ge(s_pre, 1)
                    for tau in range(NT):
                        pso = [ps_o0, ps_o1][tau % 2]
                        v.wait_ge(s_wo, tau + 1)
                        v.wait_ge(s_x, (tau + 1) * 16)
                        if tau >= 1:
                            v.wait_ge(s_woe, tau)
                        v.tensor_tensor(wtmp[:], pso[:], bo8[:],
                                        ALU.add).then_inc(s_vf, 1)
                        v.wait_ge(s_vf, VF0 + tau + 1)
                        v.tensor_tensor(
                            resid[:, tau * D:(tau + 1) * D],
                            resid[:, tau * D:(tau + 1) * D], wtmp[:],
                            ALU.add).then_inc(s_woe, 1)

            if dbg:
                @blk.gpsimd
                def _(g):
                    g.wait_ge(s_ze, SZ)
                    g.dma_start(out=o_z2, in_=z2T[:]).then_inc(s_st, 16)
                    g.wait_ge(s_woe, NT)
                    for tau in range(NT):
                        g.dma_start(out=o_x1[tau * 128:(tau + 1) * 128, :],
                                    in_=resid[:, tau * D:(tau + 1) * D]
                                    ).then_inc(s_st, 16)
                    g.wait_ge(s_st, 17 * 16)
            esz.close()

            with ExitStack() as es3:
                sb3 = lambda n, s, d=F32: es3.enter_context(nc.sbuf_tensor(px + n, s, d))
                lnw = sb3("lnw", [128, 2 * D], BF16)
                xlnT = sb3("xlnT", [128, 8 * TPC], BF16)
                pt0 = es3.enter_context(nc.psum_tensor(px + "pt0", [128, 128], BF16))
                pt1 = es3.enter_context(nc.psum_tensor(px + "pt1", [128, 128], BF16))
                pm0 = es3.enter_context(nc.psum_tensor(px + "pm0", [128, 512], F32))
                pm1 = es3.enter_context(nc.psum_tensor(px + "pm1", [128, 512], F32))
                pm2 = es3.enter_context(nc.psum_tensor(px + "pm2", [128, 512], F32))
                pm3 = es3.enter_context(nc.psum_tensor(px + "pm3", [128, 512], F32))
                pms = [pm0, pm1, pm2, pm3]

                xw_tile = lambda t: resid[:, t * D:(t + 1) * D]
                lnw_tile = lambda t: lnw[:, (t % 2) * D:(t % 2 + 1) * D]

                def x_wait(v, t):
                    v.wait_ge(s_wo, NT)
                    if dbg:
                        v.wait_ge(s_st, 16)
                    v.wait_ge(s_woe, t + 1)

                v_ops, s_ops, v2_ops = emit_ln_plan(
                    nc, None, xw_tile, lnw_tile, stat, sqs, NT, x_wait,
                    SemRef(s_app), SemRef(s_rstd), SemRef(s_nm),
                    eps_ap=eps_sb[:])

                @blk.vector
                def _(v):
                    for t in range(NT):
                        if t >= 2:
                            v.wait_ge(s_tr, 8 * (t - 1))
                        v_ops(v, t)
                        v2_ops(v, t)
                        if t >= 1:
                            for ct in range(8):
                                j = 8 * (t - 1) + ct
                                v.wait_ge(s_tr, j + 1)
                                v.tensor_copy(
                                    xlnT[:, ct * TPC + (t - 1) * 128:
                                         ct * TPC + t * 128],
                                    (pt0 if j % 2 == 0 else pt1)[:]
                                ).then_inc(s_trv, 1)
                    for ct in range(8):
                        j = 8 * (NT - 1) + ct
                        v.wait_ge(s_tr, j + 1)
                        v.tensor_copy(xlnT[:, ct * TPC + (NT - 1) * 128:
                                           ct * TPC + NT * 128],
                                      (pt0 if j % 2 == 0 else pt1)[:]
                                      ).then_inc(s_trv, 1)

                @blk.scalar
                def _(sc):
                    for t in range(NT):
                        s_ops(sc, t)

                @blk.tensor
                def _(t):
                    for tt in range(NT):
                        t.wait_ge(s_app, tt + 1)
                        for ct in range(8):
                            j = 8 * tt + ct
                            if j >= 2:
                                t.wait_ge(s_trv, j - 1)
                            t.transpose((pt0 if j % 2 == 0 else pt1)[:],
                                        lnw[:, (tt % 2) * D + ct * 128:
                                            (tt % 2) * D + (ct + 1) * 128],
                                        ident[:]).then_inc(s_tr, 1)

                with ExitStack() as es4:
                    sb4 = lambda n, s, d=F32: es4.enter_context(nc.sbuf_tensor(px + n, s, d))
                    lgT = sb4("lgT", [128, 4 * TPC], BF16)
                    lpT = sb4("lpT", [128, 2 * TPC], BF16)
                    tmp_e = sb4("tmp_e", [128, 512])
                    pm_sb = sb4("pm_sb", [128, 4 * 256], BF16)
                    lv_sb = sb4("lv_sb", [128, 2 * D], BF16)

                    fgroups = ([("q0", ec, tc) for ec in range(8) for tc in range(4)]
                               + [("gl", rc, tc) for rc in range(2) for tc in range(4)]
                               + [("lp", mc, tc) for mc in range(2) for tc in range(4)]
                               + [("ret", dc, tc) for dc in range(8) for tc in range(4)]
                               + [("out", tau, half) for tau in range(NT)
                                  for half in range(2)])

                    def scalar_evac(sc, gi, grp):
                        pm = pms[gi % 4]
                        kind = grp[0]
                        sc.wait_ge(s_mm, gi + 1)
                        if kind == "q0":
                            _, ec, tc = grp
                            sc.activation(qT0[:, ec * TPC + tc * 512:
                                              ec * TPC + (tc + 1) * 512],
                                          pm[:], AF.Copy).then_inc(s_ev, 1)
                        elif kind == "gl":
                            _, rc, tc = grp
                            scc[0] += 1
                            sc.activation(tmp_e[:], pm[:], AF.Exp,
                                          scale=-1.0).then_inc(s_sc, 1)
                            sc.wait_ge(s_sc, scc[0])
                            scc[0] += 1
                            sc.activation(
                                lgT[:, rc * TPC + tc * 512:
                                    rc * TPC + (tc + 1) * 512],
                                tmp_e[:], AF.Ln, bias=one_sb[:]).then_inc(s_sc, 1)
                            sc.wait_ge(s_sc, scc[0])
                            scc[0] += 1
                            sc.activation(tmp_e[:], pm[:], AF.Exp,
                                          scale=1.0).then_inc(s_sc, 1)
                            sc.wait_ge(s_sc, scc[0])
                            sc.activation(
                                lgT[:, (2 + rc) * TPC + tc * 512:
                                    (2 + rc) * TPC + (tc + 1) * 512],
                                tmp_e[:], AF.Ln, bias=one_sb[:]
                            ).then_inc(s_ev, 1)
                        elif kind == "lp":
                            _, mc, tc = grp
                            sc.activation(lpT[:, mc * TPC + tc * 512:
                                              mc * TPC + (tc + 1) * 512],
                                          pm[:], AF.Exp, scale=-1.0
                                          ).then_inc(s_ev, 1)
                        elif kind == "ret":
                            _, dc, tc = grp
                            sc.activation(retT[:, dc * TPC + tc * 512:
                                               dc * TPC + (tc + 1) * 512],
                                          pm[:], AF.Copy).then_inc(s_ev, 1)

                    scc = [0]

                    with ExitStack() as es5:
                        sb5 = lambda n, s, d=F32: es5.enter_context(
                            nc.sbuf_tensor(px + n, s, d))
                        wq0 = sb5("wq0", [128, 8 * D], BF16)
                        rt_sb = sb5("rt_sb", [128, 8 * 256], BF16)
                        qT0 = sb5("qT0", [128, 8 * TPC], BF16)

                        @blk.gpsimd
                        def _(g):
                            g.wait_ge(s_wo, NT)
                            g.dma_start(out=wq0[:].rearrange("p (c n) -> p c n", n=D),
                                        in_=wqT.rearrange("(c p) n -> p c n", p=128)
                                        ).then_inc(s_ldg, 16)
                            g.memset(rt_sb[:], 0.0).then_inc(s_pre, 1)
                            g.wait_ge(s_pre, 3)
                            g.dma_start(out=rt_sb[:]
                                        .rearrange("p (c n) -> p c n", n=256)
                                        [:, :, 0:255],
                                        in_=rtT.rearrange("(c p) n -> p c n", p=128)
                                        ).then_inc(s_ldg, 16)
                            g.dma_start(out=pm_sb[:]
                                        .rearrange("p (c n) -> p c n", n=256),
                                        in_=cpm.ap().rearrange("(c p) n -> p c n",
                                                               p=128)
                                        ).then_inc(s_ldg, 16)
                            g.dma_start(out=lv_sb[:]
                                        .rearrange("p (c n) -> p c n", n=D),
                                        in_=lv.rearrange("(c p) n -> p c n", p=128)
                                        ).then_inc(s_ldg, 16)
                            g.dma_start(out=wob_sb[:], in_=wob).then_inc(s_ldg, 16)
                        NLDG2 = NLDG1 + 5

                        @blk.tensor
                        def _(t):
                            t.wait_ge(s_ldg, NLDG2 * 16)
                            t.wait_ge(s_trv, 8 * NT)
                            for gi, grp in enumerate(fgroups[:40]):
                                pm = pms[gi % 4]
                                if gi >= 4:
                                    t.wait_ge(s_ev, gi - 3)
                                kind = grp[0]
                                inst = None
                                if kind == "q0":
                                    _, ec, tc = grp
                                    for ct in range(8):
                                        inst = t.matmul(
                                            pm[:],
                                            wq0[:, ct * D + ec * 128:
                                                ct * D + (ec + 1) * 128],
                                            xlnT[:, ct * TPC + tc * 512:
                                                 ct * TPC + (tc + 1) * 512],
                                            start=(ct == 0), stop=(ct == 7))
                                else:
                                    _, rc, tc = grp
                                    t.wait_ge(s_ev, 32)
                                    for ct in range(8):
                                        inst = t.matmul(
                                            pm[:],
                                            rt_sb[:, ct * 256 + rc * 128:
                                                  ct * 256 + (rc + 1) * 128],
                                            qT0[:, ct * TPC + tc * 512:
                                                ct * TPC + (tc + 1) * 512],
                                            start=(ct == 0), stop=(ct == 7))
                                inst.then_inc(s_mm, 1)

                        @blk.scalar
                        def _(sc):
                            for gi, grp in enumerate(fgroups[:40]):
                                scalar_evac(sc, gi, grp)

                    with ExitStack() as es6:
                        sb6 = lambda n, s, d=F32: es6.enter_context(
                            nc.sbuf_tensor(px + n, s, d))
                        retT = sb6("retT", [128, 8 * TPC], BF16)
                        wout_sb = sb6("wout_sb", [128, 8 * D], BF16)

                        @blk.gpsimd
                        def _(g):
                            g.wait_ge(s_mm, 40)
                            g.dma_start(out=wout_sb[:]
                                        .rearrange("p (c n) -> p c n", n=D),
                                        in_=woutT.rearrange("(c p) n -> p c n",
                                                            p=128)
                                        ).then_inc(s_ldg, 16)
                        NLDG3 = NLDG2 + 1

                        @blk.tensor
                        def _(t):
                            t.wait_ge(s_ldg, NLDG3 * 16)
                            for gj, grp in enumerate(fgroups[40:]):
                                gi = 40 + gj
                                pm = pms[gi % 4]
                                t.wait_ge(s_ev, gi - 3)
                                kind = grp[0]
                                inst = None
                                if kind == "lp":
                                    _, mc, tc = grp
                                    t.wait_ge(s_ev, 40)
                                    for ct in range(4):
                                        inst = t.matmul(
                                            pm[:],
                                            pm_sb[:, ct * 256 + mc * 128:
                                                  ct * 256 + (mc + 1) * 128],
                                            lgT[:, ct * TPC + tc * 512:
                                                ct * TPC + (tc + 1) * 512],
                                            start=(ct == 0), stop=(ct == 3))
                                elif kind == "ret":
                                    _, dc, tc = grp
                                    t.wait_ge(s_ev, 48)
                                    for ct in range(2):
                                        inst = t.matmul(
                                            pm[:],
                                            lv_sb[:, ct * D + dc * 128:
                                                  ct * D + (dc + 1) * 128],
                                            lpT[:, ct * TPC + tc * 512:
                                                ct * TPC + (tc + 1) * 512],
                                            start=(ct == 0), stop=(ct == 1))
                                else:
                                    _, tau, half = grp
                                    t.wait_ge(s_ev, 80)
                                    for ct in range(8):
                                        inst = t.matmul(
                                            pm[:],
                                            retT[:, ct * TPC + tau * 128:
                                                 ct * TPC + (tau + 1) * 128],
                                            wout_sb[:, ct * D + half * 512:
                                                    ct * D + (half + 1) * 512],
                                            start=(ct == 0), stop=(ct == 7))
                                inst.then_inc(s_mm, 1)

                        @blk.scalar
                        def _(sc):
                            for gj, grp in enumerate(fgroups[40:]):
                                if grp[0] == "out":
                                    break
                                scalar_evac(sc, 40 + gj, grp)

                        @blk.vector
                        def _(v):
                            for oi, (tau, half) in enumerate(
                                    [(tau, half) for tau in range(NT)
                                     for half in range(2)]):
                                gi = 80 + oi
                                pm = pms[gi % 4]
                                v.wait_ge(s_mm, gi + 1)
                                if dbg:
                                    v.wait_ge(s_st, 17 * 16)
                                if oi >= 1:
                                    v.wait_ge(s_ev, 80 + oi)
                                v.tensor_tensor(
                                    wtmp[:, 0:512], pm[:],
                                    wob_sb[:, half * 512:(half + 1) * 512],
                                    ALU.add).then_inc(s_vf, 1)
                                v.wait_ge(s_vf, NT + oi + 1)
                                v.tensor_tensor(
                                    resid[:, tau * D + half * 512:
                                          tau * D + (half + 1) * 512],
                                    resid[:, tau * D + half * 512:
                                          tau * D + (half + 1) * 512],
                                    wtmp[:, 0:512],
                                    ALU.add).then_inc(s_ev, 1)

            @blk.sync
            def _(sy):
                for tau in range(NT):
                    sy.wait_ge(s_ev, 80 + 2 * tau + 2)
                    sy.dma_start(out=o_x2[tau * 128:(tau + 1) * 128, :],
                                 in_=resid[:, tau * D:(tau + 1) * D]
                                 ).then_inc(s_st, 16)
                sy.dma_start(out=io["agb_in"].rearrange("t d -> () t d"),
                             in_=resid[127:128, :]
                             .rearrange("q (t d) -> q t d", d=D)
                             ).then_inc(s_st, 16)
                sy.wait_ge(s_st, (NT + 1 + (17 if dbg else 0)) * 16)

            @blk.gpsimd
            def _(g):
                g.wait_ge(s_st, (NT + 1) * 16)
                g.collective_compute(
                    "AllGather", ALU.bypass,
                    replica_groups=[list(range(NC))],
                    ins=[io["agb_in"]], outs=[io["agb_out"]]).then_inc(s_cc)
                g.wait_ge(s_cc, 1)



def emit_C(nc, io, px="C"):
    xs = io["x2S"]
    agb_out = io["agb_out"]
    bsel = io["bsel"]
    wvT = io["wvT"]
    wkT = io["wkT"]
    wrT = io["wrT"]
    tmkT = io["tmkT"]
    tmvT = io["tmvT"]
    tmrT = io["tmrT"]
    tdT = io["tdT"]
    o_kv = io["kvS"]
    o_kc = io["kcS"]
    o_sr = io["srS"]

    cid = nc.inline_tensor(np.eye(128, dtype=BF), px + "cident")
    cidf = nc.inline_tensor(np.eye(128, dtype=np.float32), px + "cidentf")

    es = ExitStack()
    sb = lambda n, s, d=F32: es.enter_context(nc.sbuf_tensor(px + n, s, d))
    psal = lambda n, s, d=F32: es.enter_context(nc.psum_tensor(px + n, s, d))
    sem = lambda n: nc.alloc_semaphore(px + n)
    with es:
        ident = sb("ident", [128, 128], BF16)
        stat = sb("stat", [128, 8])
        sqs = sb("sqs", [128, D])
        eps_sb = sb("eps_sb", [128, 1])
        xw = sb("xw", [128, 2 * D])
        lnw = sb("lnw", [128, 2 * D], BF16)
        xlnT = sb("xlnT", [128, 8 * TPC], BF16)
        xxT = sb("xxT", [128, 8 * TPC], BF16)
        bndw = sb("bndw", [16, D])
        bndln = sb("bndln", [16, D], BF16)
        bndT = sb("bndT", [128, 8 * NT], BF16)
        tmL = {}
        tmL["v"] = sb("tmv_sb", [128, 8])
        tmL["k"] = sb("tmk_sb", [128, 8])
        tmL["r"] = sb("tmr_sb", [128, 8])
        tm1L = {}
        tm1L["v"] = sb("tm1v_sb", [128, 8])
        tm1L["k"] = sb("tm1k_sb", [128, 8])
        tm1L["r"] = sb("tm1r_sb", [128, 8])
        td_sb = sb("td_sb", [128, 8])
        dcol = sb("dcol", [128, 1])
        etd = sb("etd", [128, 1])
        dfull = sb("dfull", [128, 128])
        onecol = sb("onecol", [128, 128])
        mixT = sb("mixT", [128, 8 * TPC], BF16)
        p1buf = sb("p1buf", [128, TPC], BF16)
        vT_sb = sb("vT_sb", [128, 8 * TPC], BF16)
        w_sb = sb("w_sb", [128, 8 * D], BF16)
        kstage = sb("kstage", [128, TPC])
        kcst = sb("kcst", [128, TPC])
        kvst = sb("kvst", [128, TPC])
        srst = sb("srst", [128, 2 * 512])
        sexp = sb("sexp", [128, 2 * 512])
        scano = sb("scano", [128, 128])
        carr_sb = sb("carr_sb", [128, 8 * 32])
        identf = sb("identf", [128, 128])
        bsel_sb = sb("bsel_sb", [128, NT])
        agbS = kstage
        carr2 = kvst
        pt0 = psal("pt0", [128, 128], BF16)
        pt1 = psal("pt1", [128, 128], BF16)
        pm0 = psal("pm0", [128, 512])
        pm1 = psal("pm1", [128, 512])
        pm2 = psal("pm2", [128, 512])
        pm3 = psal("pm3", [128, 512])
        pcar = psal("pcar", [128, 128])
        pms = [pm0, pm1, pm2, pm3]

        s_ld = sem("s_ld")
        s_ldg = sem("s_ldg")
        s_x = sem("s_x")
        s_nm = sem("s_nm")
        s_rstd = sem("s_rstd")
        s_app = sem("s_app")
        s_tr = sem("s_tr")
        s_trv = sem("s_trv")
        s_spre = sem("s_spre")
        s_mx = sem("s_mx")
        s_xx = sem("s_xx")
        s_mm = sem("s_mm")
        s_ev = sem("s_ev")
        s_vf = sem("s_vf")
        s_sf = sem("s_sf")
        s_kc = sem("s_kc")
        s_kv = sem("s_kv")
        s_dc = sem("s_dc")
        s_scn = sem("s_scn")
        s_sg = sem("s_sg")
        s_st = sem("s_st")
        s_wol = sem("s_wol")
        s_rtc = sem("s_rtc")
        s_bm = sem("s_bm")
        s_bnde = sem("s_bnde")
        s_cat = sem("s_cat")
        s_care = sem("s_care")
        s_cc = sem("s_cc")

        vf = [0]
        sf = [0]

        def vfence(v, inst):
            vf[0] += 1
            inst.then_inc(s_vf, 1)
            v.wait_ge(s_vf, vf[0])

        def sfence(sc, inst):
            sf[0] += 1
            inst.then_inc(s_sf, 1)
            sc.wait_ge(s_sf, sf[0])

        projs = ["v", "k", "r"]
        TRT_C = 8 * NT

        with nc.Block() as blk:
            @blk.sync
            def _(sy):
                sy.dma_start(out=ident[:], in_=cid.ap()).then_inc(s_ld, 16)
                sy.dma_start(out=identf[:], in_=cidf.ap()).then_inc(s_ld, 16)
                sy.dma_start(out=agbS[:, 0:D], in_=agb_out).then_inc(s_ld, 16)
                sy.dma_start(out=bsel_sb[:], in_=bsel).then_inc(s_ld, 16)
                sy.dma_start(out=tmL["k"][:], in_=tmkT).then_inc(s_ld, 16)
                sy.dma_start(out=tmL["v"][:], in_=tmvT).then_inc(s_ld, 16)
                sy.dma_start(out=tmL["r"][:], in_=tmrT).then_inc(s_ld, 16)
                sy.dma_start(out=td_sb[:], in_=tdT).then_inc(s_ld, 16)
                for t in range(NT):
                    if t >= 2:
                        sy.wait_ge(s_rstd, 3 * (t - 1))
                        sy.wait_ge(s_app, t - 1)
                    sy.dma_start(out=xw[:, (t % 2) * D:(t % 2 + 1) * D],
                                 in_=xs[t * 128:(t + 1) * 128, :]).then_inc(s_x, 16)
                for dt in range(8):
                    sy.wait_ge(s_kc, dt + 1)
                    sy.dma_start(out=o_kc[dt], in_=kcst[:]).then_inc(s_st, 16)
                    sy.wait_ge(s_kv, dt + 1)
                    sy.dma_start(out=o_kv[dt], in_=kvst[:]).then_inc(s_st, 16)
                for dt in range(8):
                    for tc in range(4):
                        sy.wait_ge(s_sg, dt * 4 + tc + 1)
                        cc = (dt * 4 + tc) % 2
                        sy.dma_start(out=o_sr[dt, :, tc * 512:(tc + 1) * 512],
                                     in_=srst[:, cc * 512:(cc + 1) * 512]
                                     ).then_inc(s_st, 16)
                sy.wait_ge(s_care, 8)
                sy.dma_start(out=io["agc_in"], in_=carr2[0:32, 0:1024]
                             ).then_inc(s_st, 16)
                sy.wait_ge(s_st, (16 + 32 + 1) * 16)
            NLD = 8

            @blk.gpsimd
            def _(g):
                for pi, w in enumerate((wvT, wkT, wrT)):
                    if pi >= 1:
                        g.wait_ge(s_mm, pi * 32)
                    g.dma_start(out=w_sb[:].rearrange("p (c n) -> p c n", n=D),
                                in_=w.rearrange("(c p) n -> p c n", p=128)
                                ).then_inc(s_ldg, 16)
                g.wait_ge(s_st, (16 + 32 + 1) * 16)
                g.collective_compute(
                    "AllGather", ALU.bypass,
                    replica_groups=[list(range(NC))],
                    ins=[io["agc_in"]], outs=[io["agc_out"]]).then_inc(s_cc)
                g.wait_ge(s_cc, 1)

            xw_tile = lambda t: xw[:, (t % 2) * D:(t % 2 + 1) * D]
            lnw_tile = lambda t: lnw[:, (t % 2) * D:(t % 2 + 1) * D]

            def x_wait(v, t):
                v.wait_ge(s_x, (t + 1) * 16)

            v_ops, s_ops, v2_ops = emit_ln_plan(
                nc, None, xw_tile, lnw_tile, stat, sqs, NT, x_wait,
                SemRef(s_app), SemRef(s_rstd), SemRef(s_nm), eps_ap=eps_sb[:])

            @blk.vector
            def _(v):
                v.memset(eps_sb[:], LN_EPS)
                v.memset(onecol[:], 1.0)
                for t in range(NT):
                    if t >= 2:
                        v.wait_ge(s_tr, 8 * (t - 1))
                    v_ops(v, t)
                    v2_ops(v, t)
                    if t >= 1:
                        for ct in range(8):
                            j = 8 * (t - 1) + ct
                            v.wait_ge(s_tr, j + 1)
                            v.tensor_copy(
                                xlnT[:, ct * TPC + (t - 1) * 128:ct * TPC + t * 128],
                                (pt0 if j % 2 == 0 else pt1)[:]).then_inc(s_trv, 1)
                for ct in range(8):
                    j = 8 * (NT - 1) + ct
                    v.wait_ge(s_tr, j + 1)
                    v.tensor_copy(xlnT[:, ct * TPC + (NT - 1) * 128:
                                       ct * TPC + NT * 128],
                                  (pt0 if j % 2 == 0 else pt1)[:]).then_inc(s_trv, 1)
                v.wait_ge(s_bm, 2)
                for ch in range(2):
                    vfence(v, v.tensor_copy(
                        bndw[0:16, ch * 512:(ch + 1) * 512],
                        pms[ch][0:16, :]))
                v.engine_nop().then_inc(s_bnde, 1)
                v.wait_ge(s_ld, NLD * 16)
                vfence(v, v.reduce_sum(stat[0:16, 4:5], bndw[:], axis=AX.X))
                v.tensor_scalar_mul(stat[0:16, 0:1], stat[0:16, 4:5],
                                    -1.0 / D).then_inc(s_mx, 1)
                v.wait_ge(s_spre, 1)
                v.tensor_scalar(bndln[:], bndw[:], stat[0:16, 0:1],
                                stat[0:16, 2:3], ALU.add,
                                ALU.mult).then_inc(s_mx, 1)
                for ct in range(8):
                    j2 = TRT_C + ct
                    v.wait_ge(s_tr, j2 + 1)
                    v.tensor_copy(bndT[:, ct * NT:(ct + 1) * NT],
                                  (pt0 if j2 % 2 == 0 else pt1)[:, 0:16]
                                  ).then_inc(s_trv, 1)
                for p in projs:
                    inst = v.tensor_scalar(tm1L[p][:], tmL[p][:], -1.0, 1.0,
                                           ALU.mult, ALU.add)
                inst.then_inc(s_mx, 1)
                v.wait_ge(s_trv, TRT_C + 8)
                inst = None
                for ct in range(8):
                    for tau in range(NT):
                        v.tensor_copy(
                            xxT[:, ct * TPC + tau * 128 + 1:ct * TPC + (tau + 1) * 128],
                            xlnT[:, ct * TPC + tau * 128:ct * TPC + (tau + 1) * 128 - 1])
                        inst = v.tensor_copy(
                            xxT[:, ct * TPC + tau * 128:ct * TPC + tau * 128 + 1],
                            bndT[:, ct * NT + tau:ct * NT + tau + 1])
                inst.then_inc(s_xx, 1)
                for pi, p in enumerate(projs):
                    for ct in range(8):
                        if pi > 0 and ct == 0:
                            v.wait_ge(s_mm, pi * 32)
                        vfence(v, v.tensor_scalar(
                            p1buf[:], xxT[:, ct * TPC:(ct + 1) * TPC],
                            tm1L[p][:, ct:ct + 1], None, ALU.mult))
                        vfence(v, v.scalar_tensor_tensor(
                            mixT[:, ct * TPC:(ct + 1) * TPC],
                            xlnT[:, ct * TPC:(ct + 1) * TPC],
                            tmL[p][:, ct:ct + 1], p1buf[:],
                            ALU.mult, ALU.add))
                        v.engine_nop().then_inc(s_mx, 1)
                    for dt in range(8):
                        for tc in range(4):
                            gi = pi * 32 + dt * 4 + tc
                            pm = pms[gi % 4]
                            if p != "r":
                                v.wait_ge(s_mm, gi + 1)
                            if p == "v":
                                v.tensor_copy(
                                    vT_sb[:, dt * TPC + tc * 512:
                                          dt * TPC + (tc + 1) * 512],
                                    pm[:]).then_inc(s_ev, 1)
                            elif p == "k":
                                if dt >= 1 and tc == 0:
                                    v.wait_ge(s_st, (2 * dt) * 16)
                                v.tensor_single_scalar(
                                    kstage[:, tc * 512:(tc + 1) * 512],
                                    pm[:], 60.0, ALU.min).then_inc(s_ev, 1)
                            else:
                                pass
                        if p == "k":
                            v.wait_ge(s_kc, dt + 1)
                            v.tensor_tensor(
                                kvst[:], kcst[:],
                                vT_sb[:, dt * TPC:(dt + 1) * TPC],
                                ALU.mult).then_inc(s_kv, 1)
                            v.wait_ge(s_dc, dt + 1)
                            vfence(v, v.tensor_scalar_mul(dfull[:], onecol[:],
                                                          dcol[:]))
                            for tau in range(NT):
                                for si, srcbuf in enumerate((kvst, kcst)):
                                    vfence(v, v.tensor_tensor_scan(
                                        scano[:], dfull[:],
                                        srcbuf[:, tau * 128:(tau + 1) * 128],
                                        0.0, ALU.mult, ALU.add))
                                    vfence(v, v.tensor_copy(
                                        carr_sb[:, dt * 32 + tau * 2 + si:
                                                dt * 32 + tau * 2 + si + 1],
                                        scano[:, 127:128]))
                            v.engine_nop().then_inc(s_scn, 1)
                    if p == "r":
                        for dt in range(8):
                            for tc in range(4):
                                ci = dt * 4 + tc
                                cc = ci % 2
                                v.wait_ge(s_ev, 64 + ci + 1)
                                vfence(v, v.tensor_single_scalar(
                                    sexp[:, cc * 512:(cc + 1) * 512],
                                    sexp[:, cc * 512:(cc + 1) * 512],
                                    1.0, ALU.add))
                                if ci >= 2:
                                    v.wait_ge(s_st, (16 + ci - 1) * 16)
                                vfence(v, v.reciprocal(
                                    srst[:, cc * 512:(cc + 1) * 512],
                                    sexp[:, cc * 512:(cc + 1) * 512]))
                                v.engine_nop().then_inc(s_sg, 1)

            @blk.scalar
            def _(sc):
                for t in range(NT):
                    s_ops(sc, t)
                sc.wait_ge(s_mx, 1)
                sfence(sc, sc.activation(sqs[0:16, :], bndw[:], AF.Square,
                                         bias=stat[0:16, 0:1],
                                         accum_out=stat[0:16, 6:7]))
                sfence(sc, sc.activation(stat[0:16, 6:7], stat[0:16, 6:7], AF.Ln,
                                         scale=1.0 / D, bias=eps_sb[0:16, :]))
                sc.activation(stat[0:16, 2:3], stat[0:16, 6:7], AF.Exp,
                              scale=-0.5).then_inc(s_spre, 1)
                sc.wait_ge(s_ld, NLD * 16)
                for dt in range(8):
                    if dt >= 1:
                        sc.wait_ge(s_scn, dt)
                    sfence(sc, sc.activation(etd[:], td_sb[:, dt:dt + 1], AF.Exp))
                    sc.activation(dcol[:], etd[:], AF.Exp,
                                  scale=-1.0).then_inc(s_dc, 1)
                    sc.wait_ge(s_ev, 32 + dt * 4 + 4)
                    if dt >= 1:
                        sc.wait_ge(s_st, (2 * dt - 1) * 16)
                    sc.activation(kcst[:], kstage[:], AF.Exp).then_inc(s_kc, 1)
                for dt in range(8):
                    for tc in range(4):
                        ci = dt * 4 + tc
                        cc = ci % 2
                        gi = 64 + ci
                        pm = pms[gi % 4]
                        sc.wait_ge(s_mm, gi + 1)
                        if ci >= 2:
                            sc.wait_ge(s_sg, ci - 1)
                        sc.activation(sexp[:, cc * 512:(cc + 1) * 512], pm[:],
                                      AF.Exp, scale=-1.0).then_inc(s_ev, 1)

            @blk.tensor
            def _(t):
                t.wait_ge(s_ld, NLD * 16)
                for ch in range(2):
                    t.matmul(pms[ch][0:16, :], bsel_sb[:],
                             agbS[:, ch * 512:(ch + 1) * 512],
                             start=True, stop=True).then_inc(s_bm, 1)
                for tt in range(NT):
                    t.wait_ge(s_app, tt + 1)
                    for ct in range(8):
                        j = 8 * tt + ct
                        if j >= 2:
                            t.wait_ge(s_trv, j - 1)
                        t.transpose((pt0 if j % 2 == 0 else pt1)[:],
                                    lnw[:, (tt % 2) * D + ct * 128:
                                        (tt % 2) * D + (ct + 1) * 128],
                                    ident[:]).then_inc(s_tr, 1)
                t.wait_ge(s_mx, 2)
                for ct in range(8):
                    j2 = TRT_C + ct
                    t.wait_ge(s_trv, j2 - 1)
                    t.transpose((pt0 if j2 % 2 == 0 else pt1)[0:128, 0:16],
                                bndln[:, ct * 128:(ct + 1) * 128],
                                ident[0:16, 0:16]).then_inc(s_tr, 1)
                t.wait_ge(s_xx, 1)
                t.wait_ge(s_bnde, 1)
                for pi in range(3):
                    t.wait_ge(s_ldg, (pi + 1) * 16)
                    t.wait_ge(s_mx, 3 + (pi + 1) * 8)
                    for dt in range(8):
                        for tc in range(4):
                            gi = pi * 32 + dt * 4 + tc
                            pm = pms[gi % 4]
                            if gi >= 4:
                                t.wait_ge(s_ev, gi - 3)
                            inst = None
                            for ct in range(8):
                                inst = t.matmul(
                                    pm[:],
                                    w_sb[:, ct * D + dt * 128:ct * D + (dt + 1) * 128],
                                    mixT[:, ct * TPC + tc * 512:
                                         ct * TPC + (tc + 1) * 512],
                                    start=(ct == 0), stop=(ct == 7))
                            inst.then_inc(s_mm, 1)
                t.wait_ge(s_scn, 8)
                for dt in range(8):
                    if dt >= 1:
                        t.wait_ge(s_care, dt)
                    t.transpose(pcar[0:32, :], carr_sb[:, dt * 32:(dt + 1) * 32],
                                identf[:]).then_inc(s_cat, 1)

            @blk.vector
            def _(v):
                v.wait_ge(s_st, 16 * 16)
                for dt in range(8):
                    v.wait_ge(s_cat, dt + 1)
                    v.tensor_copy(carr2[0:32, dt * 128:(dt + 1) * 128],
                                  pcar[0:32, :]).then_inc(s_care, 1)



def emit_D(nc, io, px="D"):
    xs = io["x2S"]
    kvi = io["kvS"]
    kci = io["kcS"]
    sri = io["srS"]
    agc_out = io["agc_out"]
    tdT = io["tdT"]
    tfT = io["tfT"]
    woT = io["woT"]
    rwT = io["rwT"]
    selci = io["selci"]
    o_x3 = io["x3S"]
    o_ln3 = io["ln3S"]
    o_w = io["wdS"]
    o_sel = io["selS"]

    cidf = nc.inline_tensor(np.eye(128, dtype=np.float32), px + "cidentf")
    ciota = nc.inline_tensor(
        np.tile(np.arange(8, dtype=np.float32), (128, 1)), px + "ciota8")
    selmm_np = np.zeros((128, 8 * 64), np.float32)
    for t_ in range(2):
        for si_ in range(2):
            for b_ in range(B):
                for j_ in range(64):
                    r = (j_ % 8) * 32 + (b_ * 8 + j_ // 8) * 2 + si_
                    if r // 128 == t_:
                        selmm_np[r % 128,
                                 (t_ * 4 + si_ * 2 + b_) * 64 + j_] = 1.0
    cselmm = nc.inline_tensor(selmm_np, px + "cselmm")
    RMS_EPS = float(np.finfo(np.float32).eps)
    KEPS = 1e-8

    es = ExitStack()
    sb = lambda n, s, d=F32: es.enter_context(nc.sbuf_tensor(px + n, s, d))
    psal = lambda n, s, d=F32: es.enter_context(nc.psum_tensor(px + n, s, d))
    sem = lambda n: nc.alloc_semaphore(px + n)
    with es:
        resid = sb("resid", [128, NT * D])
        identf = sb("identf", [128, 128])
        iota8 = sb("iota8", [128, 8])
        stat = sb("stat", [128, 8])
        sqs = sb("sqs", [128, D])
        eps_sb = sb("eps_sb", [128, 1])
        eps2_sb = sb("eps2_sb", [128, 1])
        xln3 = sb("xln3", [128, NT * D], BF16)
        td_sb = sb("td_sb", [128, 8])
        tf_sb = sb("tf_sb", [128, 8])
        etd = sb("etd", [128, 1])
        dcol = sb("dcol", [128, 1])
        d128col = sb("d128col", [128, 1])
        tfe = sb("tfe", [128, 1])
        dfull = sb("dfull", [128, 128])
        d128full = sb("d128full", [128, 64])
        onecol = sb("onecol", [128, 128])
        cin = sb("cin", [128, 2 * 2 * B * 64])
        sin = sb("sin", [128, B * 2 * 8])
        zbuf = sb("zbuf", [128, 64])
        z8 = sb("z8", [64, 128])
        z8b = sb("z8b", [8, 128])
        kvb = sb("kvb", [128, 2 * TPC])
        kcb = sb("kcb", [128, 2 * TPC])
        srb = sb("srb", [128, 2 * TPC], BF16)
        sA = sb("sA", [128, 128])
        sB = sb("sB", [128, 128])
        eA = sb("eA", [128, 128])
        eB = sb("eB", [128, 128])
        rDb = sb("rDb", [128, 128])
        preT = sb("preT", [128, 8 * TPC], BF16)
        wo_sb = sb("wo_sb", [128, 8 * D], BF16)
        rw_sb = sb("rw_sb", [128, 8 * 8])
        selc_sb = sb("selc_sb", [64, 8])
        onescol = sb("onescol", [128, 1])
        xtT = sb("xtT", [128, 8 * 128])
        lg_sb = sb("lg_sb", [128, 8])
        pr_sb = sb("pr_sb", [128, 8])
        t8a = sb("t8a", [128, 8])
        t8b = sb("t8b", [128, 8])
        t8c = sb("t8c", [128, 8])
        t8d = sb("t8d", [128, 8])
        t8e = sb("t8e", [128, 8])
        c0 = sb("c0", [128, 1])
        c1 = sb("c1", [128, 1])
        c2 = sb("c2", [128, 1])
        c3 = sb("c3", [128, 1])
        c4 = sb("c4", [128, 1])
        c5 = sb("c5", [128, 1])
        wd_sb = sb("wd_sb", [128, NT * 8])
        sel_sb = sb("sel_sb", [128, NT * 8])
        cnt_sb = sb("cnt_sb", [1, 128])
        selmm_sb = sb("selmm_sb", [128, 8 * 64])
        agcS0 = sqs
        agcS1 = xtT
        pt0 = psal("pt0", [128, 128])
        pt1 = psal("pt1", [128, 128])
        pm0 = psal("pm0", [128, 512])
        pm1 = psal("pm1", [128, 512])
        pr0 = psal("pr0", [128, 8])
        pc0 = psal("pc0", [1, 8])
        pcin2 = psal("pcin2", [128, 64])

        s_ld = sem("s_ld")
        s_kv = sem("s_kv")
        s_nm = sem("s_nm")
        s_rstd = sem("s_rstd")
        s_app = sem("s_app")
        s_spre = sem("s_spre")
        s_mm = sem("s_mm")
        s_ev = sem("s_ev")
        s_vf = sem("s_vf")
        s_sf = sem("s_sf")
        s_ct = sem("s_ct")
        s_cz = sem("s_cz")
        s_cze = sem("s_cze")
        s_rt = sem("s_rt")
        s_rte = sem("s_rte")
        s_sq = sem("s_sq")
        s_tr = sem("s_tr")
        s_trv = sem("s_trv")
        s_st = sem("s_st")
        s_wol = sem("s_wol")
        s_rtc = sem("s_rtc")
        s_agl = sem("s_agl")
        s_cj = sem("s_cj")
        s_cjv2 = sem("s_cjv2")
        s_cjt = sem("s_cjt")
        s_cjv = sem("s_cjv")
        s_cc = sem("s_cc")

        vf = [0]
        sf = [0]

        def vfence(v, inst):
            vf[0] += 1
            inst.then_inc(s_vf, 1)
            v.wait_ge(s_vf, vf[0])

        def sfence(sc, inst):
            sf[0] += 1
            inst.then_inc(s_sf, 1)
            sc.wait_ge(s_sf, sf[0])

        NCARR = 32

        with nc.Block() as blk:
            @blk.sync
            def _(sy):
                sy.dma_start(out=identf[:], in_=cidf.ap()).then_inc(s_ld, 16)
                sy.dma_start(out=iota8[:], in_=ciota.ap()).then_inc(s_ld, 16)
                sy.dma_start(out=td_sb[:], in_=tdT).then_inc(s_ld, 16)
                sy.dma_start(out=tf_sb[:], in_=tfT).then_inc(s_ld, 16)
                sy.dma_start(out=rw_sb[:].rearrange("p (c n) -> p c n", n=8),
                             in_=rwT.rearrange("(c p) n -> p c n", p=128)
                             ).then_inc(s_ld, 16)
                sy.dma_start(out=selc_sb[:], in_=selci).then_inc(s_ld, 16)
                sy.dma_start(out=selmm_sb[:], in_=cselmm.ap()).then_inc(s_agl, 16)
                sy.dma_start(out=agcS0[:, 0:1024], in_=agc_out[0:128, :]
                             ).then_inc(s_agl, 16)
                sy.dma_start(out=agcS1[:, 0:1024], in_=agc_out[128:256, :]
                             ).then_inc(s_agl, 16)
                for t in range(NT):
                    sy.dma_start(out=resid[:, t * D:(t + 1) * D],
                                 in_=xs[t * 128:(t + 1) * 128, :]).then_inc(s_ld, 16)
                sy.wait_ge(s_ev, 2 * NT)
                sy.dma_start(out=o_x3.rearrange("(t p) d -> p t d", p=128),
                             in_=resid[:].rearrange("p (t d) -> p t d", d=D)
                             ).then_inc(s_st, 16)
                sy.wait_ge(s_app, NT)
                sy.dma_start(out=o_ln3.rearrange("(t p) d -> p t d", p=128),
                             in_=xln3[:].rearrange("p (t d) -> p t d", d=D)
                             ).then_inc(s_st, 16)
                sy.wait_ge(s_rte, 3 * NT)
                sy.dma_start(out=o_w.rearrange("t p e -> p t e"),
                             in_=wd_sb[:].rearrange("p (t e) -> p t e", e=8)
                             ).then_inc(s_st, 16)
                sy.dma_start(out=o_sel.rearrange("t p e -> p t e"),
                             in_=sel_sb[:].rearrange("p (t e) -> p t e", e=8)
                             ).then_inc(s_st, 16)
                sy.dma_start(out=io["agn_in"].rearrange("t e -> () (t e)"),
                             in_=cnt_sb[:]).then_inc(s_st, 16)
                sy.wait_ge(s_st, 5 * 16)
            NLDC = 6

            @blk.gpsimd
            def _(g):
                g.dma_start(out=wo_sb[:].rearrange("p (c n) -> p c n", n=D),
                            in_=woT.rearrange("(c p) n -> p c n", p=128)
                            ).then_inc(s_wol, 16)
                for ct in range(8):
                    if ct >= 2:
                        g.wait_ge(s_ct, ct - 1)
                    g.dma_start(out=kvb[:, (ct % 2) * TPC:(ct % 2 + 1) * TPC],
                                in_=kvi[ct]).then_inc(s_kv, 16)
                    g.dma_start(out=kcb[:, (ct % 2) * TPC:(ct % 2 + 1) * TPC],
                                in_=kci[ct]).then_inc(s_kv, 16)
                    g.dma_start(out=srb[:, (ct % 2) * TPC:(ct % 2 + 1) * TPC],
                                in_=sri[ct]).then_inc(s_kv, 16)
                g.wait_ge(s_st, 5 * 16)
                g.collective_compute(
                    "AllGather", ALU.bypass,
                    replica_groups=[list(range(NC))],
                    ins=[io["agn_in"]], outs=[io["agn_out"]]).then_inc(s_cc)
                g.wait_ge(s_cc, 1)

            @blk.scalar
            def _(sc):
                sc.wait_ge(s_ld, 4 * 16)
                for ct in range(8):
                    if ct >= 1:
                        sc.wait_ge(s_ct, ct)
                    sfence(sc, sc.activation(etd[:], td_sb[:, ct:ct + 1], AF.Exp))
                    sfence(sc, sc.activation(dcol[:], etd[:], AF.Exp, scale=-1.0))
                    sfence(sc, sc.activation(d128col[:], etd[:], AF.Exp,
                                             scale=-128.0))
                    inst = sc.activation(tfe[:], tf_sb[:, ct:ct + 1], AF.Exp)
                    inst.then_inc(s_spre, 1)

            @blk.vector
            def _(v):
                v.memset(eps_sb[:], LN_EPS)
                v.memset(eps2_sb[:], RMS_EPS)
                v.memset(onecol[:], 1.0)
                v.memset(onescol[:], 1.0)
                for ct in range(8):
                    v.wait_ge(s_spre, ct + 1)
                    vfence(v, v.tensor_scalar_mul(dfull[:], onecol[:], dcol[:]))
                    vfence(v, v.tensor_scalar_mul(d128full[:], onecol[:, 0:64],
                                                  d128col[:]))
                    for si_ in range(2):
                        for b in range(B):
                            cidx = ct * 4 + si_ * B + b
                            slot = (ct % 2) * 4 + si_ * B + b
                            v.wait_ge(s_cjt, cidx + 1)
                            vfence(v, v.tensor_copy(
                                cin[:, slot * 64:(slot + 1) * 64],
                                pcin2[:, 0:64]))
                            v.engine_nop().then_inc(s_cjv, 1)
                            vfence(v, v.tensor_tensor_scan(
                                zbuf[:], d128full[:],
                                cin[:, slot * 64:(slot + 1) * 64],
                                0.0, ALU.mult, ALU.add))
                            v.engine_nop().then_inc(s_cze, 1)
                            v.wait_ge(s_cz, cidx * 3 + 1)
                            vfence(v, v.tensor_copy(z8[:], pt0[0:64, :]))
                            v.engine_nop().then_inc(s_cze, 1)
                            v.wait_ge(s_cz, cidx * 3 + 2)
                            vfence(v, v.tensor_copy(z8b[:], pt1[0:8, :]))
                            v.engine_nop().then_inc(s_cze, 1)
                            v.wait_ge(s_cz, cidx * 3 + 3)
                            vfence(v, v.tensor_copy(
                                sin[:, (b * 2 + si_) * 8:(b * 2 + si_) * 8 + 8],
                                pt0[:, 0:8]))
                            v.engine_nop().then_inc(s_cze, 1)
                    v.wait_ge(s_kv, 3 * (ct + 1) * 16)
                    for tau in range(NT):
                        b, kk = divmod(tau, 8)
                        p = ct % 2
                        kvt = kvb[:, p * TPC + tau * 128:p * TPC + (tau + 1) * 128]
                        kct = kcb[:, p * TPC + tau * 128:p * TPC + (tau + 1) * 128]
                        srt = srb[:, p * TPC + tau * 128:p * TPC + (tau + 1) * 128]
                        inA = sin[:, (b * 2 + 0) * 8 + kk:(b * 2 + 0) * 8 + kk + 1]
                        inB = sin[:, (b * 2 + 1) * 8 + kk:(b * 2 + 1) * 8 + kk + 1]
                        vfence(v, v.tensor_tensor_scan(sA[:], dfull[:], kvt,
                                                       inA, ALU.mult, ALU.add))
                        vfence(v, v.tensor_tensor_scan(sB[:], dfull[:], kct,
                                                       inB, ALU.mult, ALU.add))
                        vfence(v, v.tensor_copy(eA[:, 0:1], inA))
                        vfence(v, v.tensor_copy(eA[:, 1:128], sA[:, 0:127]))
                        vfence(v, v.tensor_single_scalar(eB[:, 0:1], inB, KEPS,
                                                         ALU.add))
                        vfence(v, v.tensor_single_scalar(eB[:, 1:128],
                                                         sB[:, 0:127], KEPS,
                                                         ALU.add))
                        vfence(v, v.scalar_tensor_tensor(eA[:], kvt, tfe[:],
                                                         eA[:], ALU.mult,
                                                         ALU.add))
                        vfence(v, v.scalar_tensor_tensor(eB[:], kct, tfe[:],
                                                         eB[:], ALU.mult,
                                                         ALU.add))
                        vfence(v, v.reciprocal(rDb[:], eB[:]))
                        vfence(v, v.tensor_tensor(eA[:], eA[:], rDb[:], ALU.mult))
                        vfence(v, v.tensor_tensor(
                            preT[:, ct * TPC + tau * 128:
                                 ct * TPC + (tau + 1) * 128],
                            eA[:], srt, ALU.mult))
                    v.engine_nop().then_inc(s_ct, 1)
                for tau in range(NT):
                    for half in range(2):
                        gi = tau * 2 + half
                        pm = [pm0, pm1][gi % 2]
                        v.wait_ge(s_mm, gi + 1)
                        v.wait_ge(s_ld, (NLDC + tau + 1) * 16)
                        v.scalar_tensor_tensor(
                            resid[:, tau * D + half * 512:
                                  tau * D + (half + 1) * 512],
                            pm[:], 1.0,
                            resid[:, tau * D + half * 512:
                                  tau * D + (half + 1) * 512],
                            ALU.mult, ALU.add).then_inc(s_ev, 1)

            xw_tile = lambda t: resid[:, t * D:(t + 1) * D]
            lnw_tile = lambda t: xln3[:, t * D:(t + 1) * D]

            def x_wait(v, t):
                v.wait_ge(s_ev, 2 * (t + 1))

            v_ops, s_ops, v2_ops = emit_ln_plan(
                nc, None, xw_tile, lnw_tile, stat, sqs, NT, x_wait,
                SemRef(s_app), SemRef(s_rstd), SemRef(s_nm), eps_ap=eps_sb[:])

            @blk.vector
            def _(v):
                for t in range(NT):
                    v_ops(v, t)
                    v2_ops(v, t)
                for tau in range(NT):
                    if tau >= 1:
                        v.wait_ge(s_rt, tau)
                    for ct in range(8):
                        v.wait_ge(s_tr, 8 * tau + ct + 1)
                        v.tensor_copy(xtT[:, ct * 128:(ct + 1) * 128],
                                      (pt0 if ct % 2 == 0 else pt1)[:]
                                      ).then_inc(s_trv, 1)
                    v.wait_ge(s_rt, tau + 1)
                    v.wait_ge(s_sq, tau + 1)
                    vfence(v, v.tensor_scalar_mul(lg_sb[:], pr0[:],
                                                  stat[:, 3:4]))
                    vfence(v, v.reduce_max(c0[:], lg_sb[:], axis=AX.X))
                    vfence(v, v.tensor_scalar(pr_sb[:], lg_sb[:], c0[:], None,
                                              ALU.subtract))
                    v.engine_nop().then_inc(s_rte, 1)
                    v.wait_ge(s_sq, NT + tau + 1)
                    vfence(v, v.reciprocal(c4[:], c5[:]))
                    vfence(v, v.tensor_scalar_mul(pr_sb[:], pr_sb[:], c4[:]))
                    vfence(v, v.reduce_max(c0[:], pr_sb[:], axis=AX.X))
                    vfence(v, v.tensor_scalar(t8a[:], pr_sb[:], c0[:], None,
                                              ALU.is_equal))
                    vfence(v, v.tensor_tensor(t8e[:], iota8[:], t8a[:], ALU.mult))
                    vfence(v, v.tensor_scalar(t8c[:], t8a[:], -999.0, 999.0,
                                              ALU.mult, ALU.add))
                    vfence(v, v.tensor_tensor(t8e[:], t8e[:], t8c[:], ALU.add))
                    vfence(v, v.tensor_reduce(c1[:], t8e[:], AX.X, ALU.min))
                    vfence(v, v.tensor_scalar(t8b[:], iota8[:], c1[:], None,
                                              ALU.is_equal))
                    vfence(v, v.scalar_tensor_tensor(t8c[:], t8b[:], -1000.0,
                                                     pr_sb[:], ALU.mult,
                                                     ALU.add))
                    vfence(v, v.reduce_max(c2[:], t8c[:], axis=AX.X))
                    vfence(v, v.tensor_scalar(t8d[:], t8c[:], c2[:], None,
                                              ALU.is_equal))
                    vfence(v, v.tensor_tensor(t8e[:], iota8[:], t8d[:], ALU.mult))
                    vfence(v, v.tensor_scalar(t8c[:], t8d[:], -999.0, 999.0,
                                              ALU.mult, ALU.add))
                    vfence(v, v.tensor_tensor(t8e[:], t8e[:], t8c[:], ALU.add))
                    vfence(v, v.tensor_reduce(c3[:], t8e[:], AX.X, ALU.min))
                    vfence(v, v.tensor_scalar(t8d[:], iota8[:], c3[:], None,
                                              ALU.is_equal))
                    vfence(v, v.tensor_tensor(c4[:], c0[:], c2[:], ALU.add))
                    vfence(v, v.reciprocal(c5[:], c4[:]))
                    vfence(v, v.tensor_scalar_mul(t8a[:], t8b[:], c0[:]))
                    vfence(v, v.tensor_scalar_mul(t8c[:], t8d[:], c2[:]))
                    vfence(v, v.tensor_tensor(t8a[:], t8a[:], t8c[:], ALU.add))
                    vfence(v, v.tensor_scalar_mul(
                        wd_sb[:, tau * 8:(tau + 1) * 8], t8a[:], c5[:]))
                    vfence(v, v.tensor_tensor(sel_sb[:, tau * 8:(tau + 1) * 8],
                                              t8b[:], t8d[:], ALU.add))
                    v.engine_nop().then_inc(s_rte, 1)
                    v.wait_ge(s_rtc, tau + 1)
                    vfence(v, v.tensor_copy(cnt_sb[0:1, tau * 8:(tau + 1) * 8],
                                            pc0[0:1, :]))
                    v.engine_nop().then_inc(s_rte, 1)

            @blk.scalar
            def _(sc):
                for t in range(NT):
                    s_ops(sc, t)
                for tau in range(NT):
                    sc.wait_ge(s_app, tau + 1)
                    sfence(sc, sc.activation(sqs[:],
                                             resid[:, tau * D:(tau + 1) * D],
                                             AF.Square, accum_out=stat[:, 5:6]))
                    sfence(sc, sc.activation(stat[:, 5:6], stat[:, 5:6], AF.Ln,
                                             scale=1.0 / D, bias=eps2_sb[:]))
                    inst = sc.activation(stat[:, 3:4], stat[:, 5:6], AF.Exp,
                                         scale=-0.5)
                    inst.then_inc(s_sq, 1)
                for tau in range(NT):
                    sc.wait_ge(s_rte, 3 * tau + 1)
                    inst = sc.activation(pr_sb[:], pr_sb[:], AF.Exp,
                                         accum_out=c5[:])
                    inst.then_inc(s_sq, 1)

            @blk.tensor
            def _(t):
                t.wait_ge(s_ld, 6 * 16)
                t.wait_ge(s_agl, 3 * 16)
                for ct in range(8):
                    for si_ in range(2):
                        for b in range(B):
                            cidx = ct * 4 + si_ * B + b
                            if cidx >= 1:
                                t.wait_ge(s_cjv, cidx)
                            c0_ = (0 * 4 + si_ * 2 + b) * 64
                            c1_ = (1 * 4 + si_ * 2 + b) * 64
                            t.matmul(pcin2[:, 0:64],
                                     agcS0[:, ct * 128:(ct + 1) * 128],
                                     selmm_sb[:, c0_:c0_ + 64],
                                     start=True, stop=False)
                            t.matmul(pcin2[:, 0:64],
                                     agcS1[:, ct * 128:(ct + 1) * 128],
                                     selmm_sb[:, c1_:c1_ + 64],
                                     start=False, stop=True).then_inc(s_cjt, 1)
                            t.wait_ge(s_cze, cidx * 4 + 1)
                            t.transpose(pt0[0:64, :], zbuf[:], identf[:]
                                        ).then_inc(s_cz, 1)
                            t.wait_ge(s_cze, cidx * 4 + 2)
                            t.matmul(pt1[0:8, :], selc_sb[:], z8[:],
                                     start=True, stop=True).then_inc(s_cz, 1)
                            t.wait_ge(s_cze, cidx * 4 + 3)
                            t.transpose(pt0[:, 0:8], z8b[:], identf[0:8, 0:8]
                                        ).then_inc(s_cz, 1)
                t.wait_ge(s_ct, 8)
                t.wait_ge(s_wol, 16)
                for tau in range(NT):
                    for half in range(2):
                        gi = tau * 2 + half
                        pm = [pm0, pm1][gi % 2]
                        if gi >= 2:
                            t.wait_ge(s_ev, gi - 1)
                        inst = None
                        for ct in range(8):
                            inst = t.matmul(
                                pm[:],
                                preT[:, ct * TPC + tau * 128:
                                     ct * TPC + (tau + 1) * 128],
                                wo_sb[:, ct * D + half * 512:
                                      ct * D + (half + 1) * 512],
                                start=(ct == 0), stop=(ct == 7))
                        inst.then_inc(s_mm, 1)
                def count_mm(tau):
                    t.wait_ge(s_rte, 3 * tau + 2)
                    if tau >= 1:
                        t.wait_ge(s_rte, 3 * (tau - 1) + 3)
                    t.matmul(pc0[:], onescol[:],
                             sel_sb[:, tau * 8:(tau + 1) * 8],
                             start=True, stop=True).then_inc(s_rtc, 1)

                for tau in range(NT):
                    if tau >= 1:
                        count_mm(tau - 1)
                    t.wait_ge(s_ev, 2 * (tau + 1))
                    for ct in range(8):
                        j = 8 * tau + ct
                        if j >= 2:
                            t.wait_ge(s_trv, j - 1)
                        t.transpose((pt0 if ct % 2 == 0 else pt1)[:],
                                    resid[:, tau * D + ct * 128:
                                          tau * D + (ct + 1) * 128],
                                    identf[:]).then_inc(s_tr, 1)
                    t.wait_ge(s_trv, 8 * (tau + 1))
                    if tau >= 1:
                        t.wait_ge(s_rte, 3 * (tau - 1) + 1)
                    inst = None
                    for ct in range(8):
                        inst = t.matmul(pr0[:], xtT[:, ct * 128:(ct + 1) * 128],
                                        rw_sb[:, ct * 8:(ct + 1) * 8],
                                        start=(ct == 0), stop=(ct == 7))
                    inst.then_inc(s_rt, 1)
                count_mm(NT - 1)



SLOTS = 512
CAP = 2560.0


def emit_E(nc, io, px="E"):
    x3i = io["x3S"]
    ln3i = io["ln3S"]
    wdi = io["wdS"]
    seli = io["selS"]
    cnta = io["agn_out"]
    selpre = io["selpre"]
    rgi = io["rgi"]
    rui = io["rui"]
    rdi = io["rdi"]
    sgi = io["sgi"]
    sui = io["sui"]
    sdi = io["sdi"]
    o_out = io["out"]

    cidf = nc.inline_tensor(np.eye(128, dtype=np.float32), px + "cidentf")
    cid = nc.inline_tensor(np.eye(128, dtype=BF), px + "cident")
    ctriu = nc.inline_tensor(np.triu(np.ones((128, 128), np.float32), 1),
                             px + "ctriu")
    ciota512 = nc.inline_tensor(
        np.tile(np.arange(512, dtype=np.float32), (128, 1)), px + "ciota512")
    conesr = nc.inline_tensor(np.ones((1, 128), np.float32), px + "conesr")
    _bt = np.zeros((128, 128), np.float32)
    for _j in range(128):
        for _i in range(128):
            if _j // 8 < _i // 8 and _j % 8 == _i % 8:
                _bt[_j, _i] = 1.0
    cbtriu = nc.inline_tensor(_bt, px + "cbtriu")

    sched = []
    for e in range(8):
        for tau in range(NT):
            sched.append(("dmt", e, tau))
        for dt in range(8):
            sched.append(("ga", e, dt))
        sched.append(("wg", e, 0))
        for it in range(32):
            sched.append(("gt", e, it))
        for it in range(32):
            sched.append(("up", e, it))
        for q in range(4):
            for ncn in range(2):
                for sc in range(4):
                    sched.append(("dn", e, (q, sc, ncn)))
        for tau in range(NT):
            for ncn in range(2):
                sched.append(("sct", e, (tau, ncn)))
    gidx = {}
    n = 0
    for g in sched:
        if g[0] != "dmt":
            n += 1
            gidx[g] = n
    EVENG = {"ga": "v", "wg": "v", "gt": "s", "up": "v", "dn": "v", "sct": "v"}
    rotseq = [g for g in sched if g[0] in ("ga", "gt", "up", "dn", "sct")]
    rotpos = {g: i for i, g in enumerate(rotseq)}
    evidx = {}
    evcnt = {"v": 0, "s": 0}
    for g in sched:
        k = g[0]
        if k == "dmt" or k == "wg":
            continue
        eng = EVENG[k]
        evcnt[eng] += 1
        evidx[g] = (eng, evcnt[eng])

    piecestream = []
    for e in range(8):
        for nm in ("rg", "ru", "rd"):
            for p in range(16):
                piecestream.append((e, nm, p))

    def piece_lastcons(e, nm, p):
        if nm == "rg":
            return gidx[("gt", e, 2 * p + 1)]
        if nm == "ru":
            return gidx[("up", e, 2 * p + 1)]
        q, ncn = divmod(p // 2, 2)
        return gidx[("dn", e, (q, 3, ncn))]

    lastcons = [piece_lastcons(*ps) for ps in piecestream]
    ready = {}
    ldg_cnt = [1]
    stg_cnt = [0]
    for j, ps in enumerate(piecestream):
        if j % 2 == 0:
            ldg_cnt[0] += 1
            ready[ps] = ("ldg", ldg_cnt[0] * 16)
        else:
            stg_cnt[0] += 2
            ready[ps] = ("cst", stg_cnt[0])
    NGPL = ldg_cnt[0]

    es = ExitStack()
    sb = lambda nm, s, d=F32: es.enter_context(nc.sbuf_tensor(px + nm, s, d))
    psal = lambda nm, s, d=F32: es.enter_context(nc.psum_tensor(px + nm, s, d))
    sem = lambda nm: nc.alloc_semaphore(px + nm)
    with es:
        racc = sb("racc", [128, NT * D], BF16)
        identf = sb("identf", [128, 128])
        ident = sb("ident", [128, 128], BF16)
        triu = sb("triu", [128, 128])
        iota512 = sb("iota512", [128, 512])
        onesr = sb("onesr", [1, 128])
        onescol = sb("onescol", [128, 1])
        wd_sb = sb("wd_sb", [128, NT * 8])
        sel_sb = sb("sel_sb", [128, NT * 8])
        kp_sb = sb("kp_sb", [128, NT * 8])
        wp_sb = sb("wp_sb", [128, NT * 8])
        srk_sb = sb("srk_sb", [128, NT * 8])
        cnt_sb = sb("cnt_sb", [128, 8])
        selp_sb = sb("selp_sb", [128, NT])
        offs_sb = sb("offs_sb", [16, 8])
        btriu = sb("btriu", [128, 128])
        e1tmp = sb("e1tmp", [128, 2])
        offr = sb("offr", [1, 128])
        offr2 = sb("offr2", [1, 128])
        offacc = sb("offacc", [1, 8])
        totr = sb("totr", [1, 8])
        offb = sb("offb", [128, 128])
        offb2 = sb("offb2", [128, 128])
        wsl = sb("wsl", [128, 4])
        wpb_sb = sb("wpb_sb", [128, NT * 8], BF16)

        s_ld = sem("s_ld")
        s_ldg = sem("s_ldg")
        s_ld2 = sem("s_ld2")
        s_cst = sem("s_cst")
        s_ofd = sem("s_ofd")
        s_rkt = sem("s_rkt")
        s_rk = sem("s_rk")
        s_mm = sem("s_mm")
        s_evv = sem("s_evv")
        s_evs = sem("s_evs")
        s_vf = sem("s_vf")
        s_dm = sem("s_dm")
        s_eb = sem("s_eb")
        s_wgv = sem("s_wgv")
        s_tr = sem("s_tr")
        s_trv = sem("s_trv")
        s_sh = sem("s_sh")
        s_sil = sem("s_sil")
        s_hm = sem("s_hm")
        s_trv2 = sem("s_trv2")
        s_acc = sem("s_acc")
        s_ldx = sem("s_ldx")
        s_lds = sem("s_lds")
        s_fin = sem("s_fin")
        s_st = sem("s_st")

        vf = [0]

        def vfence(v, inst):
            vf[0] += 1
            inst.then_inc(s_vf, 1)
            v.wait_ge(s_vf, vf[0])

        def rot_wait(t, g):
            r = rotpos[g]
            if r >= 4:
                prev = rotseq[r - 4]
                if prev[0] == "gt":
                    t.wait_ge(s_evs, evidx[prev][1])
                else:
                    t.wait_ge(s_evv, evidx[prev][1])

        esp = ExitStack()
        pt0 = esp.enter_context(nc.psum_tensor(px + "pt0", [128, 128], BF16))
        pt1 = esp.enter_context(nc.psum_tensor(px + "pt1", [128, 128], BF16))
        pa = esp.enter_context(nc.psum_tensor(px + "pa", [128, 512], F32))
        pb = esp.enter_context(nc.psum_tensor(px + "pb", [128, 512], F32))
        pc_ = esp.enter_context(nc.psum_tensor(px + "pc_", [128, 512], F32))
        pd_ = esp.enter_context(nc.psum_tensor(px + "pd_", [128, 512], F32))
        psmA = esp.enter_context(nc.psum_tensor(px + "psmA", [128, 512], F32))
        psmB = esp.enter_context(nc.psum_tensor(px + "psmB", [128, 512], F32))
        pr8 = psmA[:, 0:8]
        pc8 = psmB[0:1, 0:8]
        po16 = psmB[0:16, 8:16]
        pw4 = psmB[:, 16:20]
        pms4 = [pa, pb, pc_, pd_]

        with nc.Block() as blk:
            @blk.sync
            def _(sy):
                sy.dma_start(out=identf[:], in_=cidf.ap()).then_inc(s_ld, 16)
                sy.dma_start(out=ident[:], in_=cid.ap()).then_inc(s_ld, 16)
                sy.dma_start(out=triu[:], in_=ctriu.ap()).then_inc(s_ld, 16)
                sy.dma_start(out=iota512[:], in_=ciota512.ap()).then_inc(s_ld, 16)
                sy.dma_start(out=onesr[:], in_=conesr.ap()).then_inc(s_ld, 16)
                sy.dma_start(out=wd_sb[:].rearrange("p (t e) -> p t e", e=8),
                             in_=wdi.rearrange("t p e -> p t e")).then_inc(s_ld, 16)
                sy.dma_start(out=sel_sb[:].rearrange("p (t e) -> p t e", e=8),
                             in_=seli.rearrange("t p e -> p t e")).then_inc(s_ld, 16)
                sy.dma_start(out=cnt_sb[:], in_=cnta).then_inc(s_ld, 16)
                sy.dma_start(out=selp_sb[:], in_=selpre).then_inc(s_ld, 16)
                sy.dma_start(out=btriu[:], in_=cbtriu.ap()).then_inc(s_ld, 16)
            NLD0 = 10

            @blk.tensor
            def _(t):
                t.wait_ge(s_ld, NLD0 * 16)
                t.matmul(po16, selp_sb[:], cnt_sb[:], start=True,
                         stop=True).then_inc(s_rkt, 1)
                t.wait_ge(s_ofd, 16)
                t.matmul(pa[:, 0:128], onesr[:], offr[:],
                         start=True, stop=True).then_inc(s_rkt, 1)
                t.matmul(psmA[:, 0:128], triu[:], sel_sb[:, 0:128],
                         start=True, stop=True).then_inc(s_rkt, 1)
                t.wait_ge(s_rk, 3)
                t.matmul(psmB[0:1, 32:160], onescol[:], kp_sb[:, 0:128],
                         start=True, stop=True).then_inc(s_rkt, 1)
                t.wait_ge(s_rk, 4)
                t.transpose(psmA[:, 128:129], offr[0:1, :], identf[0:1, 0:1]
                            ).then_inc(s_rkt, 1)
                t.wait_ge(s_rk, 5)
                t.matmul(psmA[:, 192:193], btriu[:], e1tmp[:, 0:1],
                         start=True, stop=True).then_inc(s_rkt, 1)
                t.wait_ge(s_rk, 6)
                t.transpose(psmB[0:1, 192:320], e1tmp[:, 1:2], identf[:]
                            ).then_inc(s_rkt, 1)
                t.wait_ge(s_rk, 7)
                t.matmul(pa[:, 0:128], onesr[:], offr2[:],
                         start=True, stop=True).then_inc(s_rkt, 1)
                t.wait_ge(s_rk, 8)
                t.matmul(psmA[:, 0:128], triu[:], kp_sb[:, 0:128],
                         start=True, stop=True).then_inc(s_rkt, 1)

            @blk.gpsimd
            def _(g):
                g.wait_ge(s_rk, 1)
                g.dma_start(out=offr[0:1, :], in_=offs_sb[:]).then_inc(s_ofd, 16)

            @blk.vector
            def _(v):
                v.memset(onescol[:], 1.0)
                v.memset(offacc[:], 0.0)
                v.memset(racc[:], 0.0)
                v.wait_ge(s_rkt, 1)
                vfence(v, v.tensor_copy(offs_sb[:], po16))
                v.engine_nop().then_inc(s_rk, 1)
                v.wait_ge(s_rkt, 2)
                vfence(v, v.tensor_copy(offb[:], pa[:, 0:128]))
                v.engine_nop().then_inc(s_rk, 1)
                v.wait_ge(s_rkt, 3)
                vfence(v, v.tensor_tensor(kp_sb[:, 0:128], psmA[:, 0:128],
                                          offb[:], ALU.add))
                vfence(v, v.tensor_single_scalar(
                    kp_sb[:, 0:128], kp_sb[:, 0:128], CAP, ALU.is_lt))
                vfence(v, v.tensor_tensor(kp_sb[:, 0:128], kp_sb[:, 0:128],
                                          sel_sb[:, 0:128], ALU.mult))
                vfence(v, v.tensor_tensor(wp_sb[:, 0:128], kp_sb[:, 0:128],
                                          wd_sb[:, 0:128], ALU.mult))
                vfence(v, v.tensor_copy(wpb_sb[:], wp_sb[:]))
                v.engine_nop().then_inc(s_rk, 1)
                v.wait_ge(s_rkt, 4)
                vfence(v, v.tensor_copy(offr[0:1, :], psmB[0:1, 32:160]))
                v.engine_nop().then_inc(s_rk, 1)
                v.wait_ge(s_rkt, 5)
                vfence(v, v.tensor_copy(e1tmp[:, 0:1], psmA[:, 128:129]))
                v.engine_nop().then_inc(s_rk, 1)
                v.wait_ge(s_rkt, 6)
                vfence(v, v.tensor_copy(e1tmp[:, 1:2], psmA[:, 192:193]))
                v.engine_nop().then_inc(s_rk, 1)
                v.wait_ge(s_rkt, 7)
                vfence(v, v.tensor_copy(offr2[0:1, :], psmB[0:1, 192:320]))
                v.engine_nop().then_inc(s_rk, 1)
                v.wait_ge(s_rkt, 8)
                vfence(v, v.tensor_copy(offb2[:], pa[:, 0:128]))
                v.engine_nop().then_inc(s_rk, 1)
                v.wait_ge(s_rkt, 9)
                vfence(v, v.tensor_tensor(srk_sb[:, 0:128], psmA[:, 0:128],
                                          offb2[:], ALU.add))
                v.engine_nop().then_inc(s_rk, 1)

            with ExitStack() as esx:
                sbx = lambda nm, s, d=F32: esx.enter_context(
                    nc.sbuf_tensor(px + nm, s, d))
                xln3 = sbx("xln3", [128, NT * D], BF16)
                wbuf = sbx("wbuf", [128, 8192], BF16)
                stg = sbx("stg", [128, 2048], F32)
                dmT = sbx("dmT", [128, 4 * TPC], BF16)
                xgT = sbx("xgT", [128, 8 * 512], BF16)
                hT = sbx("hT", [128, 32 * 512], BF16)
                eacc = sbx("eacc", [128, 4 * D])
                ebf = sbx("ebf", [128, 4 * D], BF16)
                dmat = sbx("dmat", [128, NT * 512], BF16)

                WTS = {"rg": rgi, "ru": rui, "rd": rdi}

                def piece_src(e, nm, p):
                    wt = WTS[nm]
                    if nm == "rd":
                        q, ncn, rh = p // 4, (p // 2) % 2, p % 2
                        return wt[e, q * 1024 + rh * 512:
                                  q * 1024 + (rh + 1) * 512,
                                  ncn * 512:(ncn + 1) * 512]
                    return wt[e, :, p * 256:(p + 1) * 256]

                @blk.gpsimd
                def _(g):
                    g.dma_start(out=xln3[:].rearrange("p (t d) -> p t d", d=D),
                                in_=ln3i.rearrange("(t p) d -> p t d", p=128)
                                ).then_inc(s_ldg, 16)
                    for j, (e, nm, p) in enumerate(piecestream):
                        if j % 2 != 0:
                            continue
                        if j >= 4:
                            g.wait_ge(s_mm, lastcons[j - 4])
                        reg = (j % 4) * 2048
                        if nm == "rd":
                            g.dma_start(
                                out=wbuf[:, reg:reg + 2048].rearrange(
                                    "p (c n) -> p c n", n=512),
                                in_=piece_src(e, nm, p).rearrange(
                                    "(c p) n -> p c n", p=128)
                            ).then_inc(s_ldg, 16)
                        else:
                            g.dma_start(
                                out=wbuf[:, reg:reg + 2048].rearrange(
                                    "p (c n) -> p c n", n=256),
                                in_=piece_src(e, nm, p).rearrange(
                                    "(c p) n -> p c n", p=128)
                            ).then_inc(s_ldg, 16)

                @blk.sync
                def _(sy):
                    slot = 0
                    for j, (e, nm, p) in enumerate(piecestream):
                        if j % 2 != 1:
                            continue
                        if j >= 4:
                            sy.wait_ge(s_mm, lastcons[j - 4])
                        src = piece_src(e, nm, p)
                        nn = 512 if nm == "rd" else 256
                        rs = 256 if nm == "rd" else 512
                        for s in range(2):
                            if slot >= 2:
                                sy.wait_ge(s_cst, slot - 1)
                            so = (slot % 2) * 1024
                            sy.dma_start(
                                out=stg[:, so:so + 1024].rearrange(
                                    "p (c n) -> p c n", n=nn),
                                in_=src[s * rs:(s + 1) * rs, :]
                                .rearrange("(c p) n -> p c n", p=128)
                            ).then_inc(s_ld2, 16)
                            slot += 1

                @blk.tensor
                def _(t):
                    t.wait_ge(s_ldg, 16)
                    tcount = 0
                    for g in sched:
                        kind, e, arg = g
                        if kind == "dmt":
                            tau = arg
                            t.wait_ge(s_dm, e + 1)
                            for sc in range(4):
                                tcount += 1
                                if tcount >= 3:
                                    t.wait_ge(s_trv, tcount - 2)
                                t.transpose(
                                    (pt0 if tcount % 2 == 0 else pt1)[:],
                                    dmat[:, tau * 512 + sc * 128:
                                         tau * 512 + (sc + 1) * 128],
                                    ident[:]).then_inc(s_tr, 1)
                            continue
                        if kind == "wg":
                            if e >= 1:
                                t.wait_ge(s_wgv, e)
                            t.wait_ge(s_dm, e + 1)
                            inst = None
                            for sc in range(4):
                                for tau in range(NT):
                                    inst = t.matmul(
                                        pw4[:, sc:sc + 1],
                                        dmat[:, tau * 512 + sc * 128:
                                             tau * 512 + (sc + 1) * 128],
                                        wpb_sb[:, tau * 8 + e:tau * 8 + e + 1],
                                        start=(tau == 0), stop=(tau == NT - 1))
                            inst.then_inc(s_mm, 1)
                            continue
                        pm = pms4[rotpos[g] % 4]
                        rot_wait(t, g)
                        inst = None
                        if kind == "ga":
                            dt = arg
                            t.wait_ge(s_dm, e + 1)
                            for tau in range(NT):
                                inst = t.matmul(
                                    pm[:],
                                    xln3[:, tau * D + dt * 128:
                                         tau * D + (dt + 1) * 128],
                                    dmat[:, tau * 512:(tau + 1) * 512],
                                    start=(tau == 0), stop=(tau == NT - 1))
                        elif kind in ("gt", "up"):
                            it = arg
                            p, i2 = divmod(it, 2)
                            nm = "rg" if kind == "gt" else "ru"
                            if i2 == 0:
                                sk, sv = ready[(e, nm, p)]
                                t.wait_ge(s_ldg if sk == "ldg" else s_cst, sv)
                            reg = (p % 4) * 2048
                            for ct in range(8):
                                inst = t.matmul(
                                    pm[:],
                                    wbuf[:, reg + ct * 256 + i2 * 128:
                                         reg + ct * 256 + (i2 + 1) * 128],
                                    xgT[:, ct * 512:(ct + 1) * 512],
                                    start=(ct == 0), stop=(ct == 7))
                        elif kind == "dn":
                            q, sc, ncn = arg
                            p1 = q * 4 + ncn * 2
                            p2 = p1 + 1
                            r1 = (p1 % 4) * 2048
                            r2 = (p2 % 4) * 2048
                            if sc == 0:
                                for pp in (p1, p2):
                                    sk, sv = ready[(e, "rd", pp)]
                                    t.wait_ge(s_ldg if sk == "ldg" else s_cst,
                                              sv)
                            for itl in range(8):
                                it = q * 8 + itl
                                woff = (r1 + itl * 512 if itl < 4
                                        else r2 + (itl - 4) * 512)
                                inst = t.matmul(
                                    pm[:],
                                    hT[:, it * 512 + sc * 128:
                                       it * 512 + (sc + 1) * 128],
                                    wbuf[:, woff:woff + 512],
                                    start=(itl == 0), stop=(itl == 7))
                        elif kind == "sct":
                            tau, ncn = arg
                            t.wait_ge(s_eb, e + 1)
                            for sc in range(4):
                                inst = t.matmul(
                                    pm[:],
                                    dmT[:, sc * TPC + tau * 128:
                                        sc * TPC + (tau + 1) * 128],
                                    ebf[:, sc * D + ncn * 512:
                                        sc * D + (ncn + 1) * 512],
                                    start=(sc == 0), stop=(sc == 3))
                        inst.then_inc(s_mm, 1)

                @blk.vector
                def _(v):
                    trv_cnt = 0
                    for e in range(8):
                        if e >= 1:
                            v.wait_ge(s_mm, gidx[("sct", e - 1, (NT - 1, 1))])
                        for tau in range(NT):
                            vfence(v, v.tensor_scalar(
                                dmat[:, tau * 512:(tau + 1) * 512],
                                iota512[:],
                                srk_sb[:, tau * 8 + e:tau * 8 + e + 1],
                                kp_sb[:, tau * 8 + e:tau * 8 + e + 1],
                                ALU.is_equal, ALU.mult))
                        v.engine_nop().then_inc(s_dm, 1)
                        for tau in range(NT):
                            for sc in range(4):
                                trv_cnt += 1
                                v.wait_ge(s_tr, trv_cnt)
                                v.tensor_copy(
                                    dmT[:, sc * TPC + tau * 128:
                                        sc * TPC + (tau + 1) * 128],
                                    (pt0 if trv_cnt % 2 == 0 else pt1)[:]
                                ).then_inc(s_trv, 1)
                        for g in sched:
                            kind, ge, arg = g
                            if ge != e or kind == "dmt":
                                continue
                            if kind == "wg":
                                v.wait_ge(s_mm, gidx[g])
                                vfence(v, v.tensor_copy(wsl[:], pw4))
                                v.engine_nop().then_inc(s_wgv, 1)
                                continue
                            if kind == "gt":
                                continue
                            pm = pms4[rotpos[g] % 4]
                            v.wait_ge(s_mm, gidx[g])
                            if kind == "ga":
                                dt = arg
                                v.tensor_copy(xgT[:, dt * 512:(dt + 1) * 512],
                                              pm[:]).then_inc(s_evv, 1)
                            elif kind == "up":
                                it = arg
                                v.wait_ge(s_evs, 32 * e + it + 1)
                                v.tensor_tensor(
                                    hT[:, it * 512:(it + 1) * 512],
                                    hT[:, it * 512:(it + 1) * 512],
                                    pm[:], ALU.mult).then_inc(s_evv, 1)
                            elif kind == "dn":
                                q, sc, ncn = arg
                                dst = eacc[:, sc * D + ncn * 512:
                                           sc * D + (ncn + 1) * 512]
                                if q == 0:
                                    v.tensor_copy(dst, pm[:]).then_inc(s_evv, 1)
                                else:
                                    v.tensor_tensor(dst, dst, pm[:], ALU.add
                                                    ).then_inc(s_evv, 1)
                                if arg == (3, 3, 1):
                                    v.wait_ge(s_wgv, e + 1)
                                    for sc2 in range(4):
                                        vfence(v, v.tensor_scalar_mul(
                                            ebf[:, sc2 * D:(sc2 + 1) * D],
                                            eacc[:, sc2 * D:(sc2 + 1) * D],
                                            wsl[:, sc2:sc2 + 1]))
                                    v.engine_nop().then_inc(s_eb, 1)
                            elif kind == "sct":
                                tau, ncn = arg
                                dst = racc[:, tau * D + ncn * 512:
                                           tau * D + (ncn + 1) * 512]
                                v.tensor_tensor(dst, dst, pm[:], ALU.add
                                                ).then_inc(s_evv, 1)

                @blk.scalar
                def _(sc):
                    def silu(e, it):
                        g = ("gt", e, it)
                        pm = pms4[rotpos[g] % 4]
                        sc.wait_ge(s_mm, gidx[g])
                        sc.activation(hT[:, it * 512:(it + 1) * 512],
                                      pm[:], AF.Silu).then_inc(s_evs, 1)

                    slot = 0
                    for j, (e, nm, p) in enumerate(piecestream):
                        if j % 2 == 1:
                            reg = (j % 4) * 2048
                            for s in range(2):
                                so = (slot % 2) * 1024
                                sc.wait_ge(s_ld2, 16 * (slot + 1))
                                inst = sc.activation(
                                    wbuf[:, reg + s * 1024:
                                         reg + (s + 1) * 1024],
                                    stg[:, so:so + 1024], AF.Copy)
                                inst.then_inc(s_cst, 1)
                                slot += 1
                        if nm == "rg" and p >= 2:
                            silu(e, 2 * (p - 2))
                            silu(e, 2 * (p - 2) + 1)
                        elif nm == "ru" and p == 0:
                            for it in range(28, 32):
                                silu(e, it)

            esp.close()
            NEXP = gidx[("sct", 7, (NT - 1, 1))]
            TRSH = 8 * NT * 4
            with ExitStack() as esh:
                sbh = lambda nm, s, d=F32: esh.enter_context(
                    nc.sbuf_tensor(px + nm, s, d))
                xlnT = sbh("xlnT", [128, 8 * TPC], BF16)
                acc = sbh("acc", [128, NT * D])
                sgb = sbh("sgb", [128, 2 * 4096], BF16)
                sub = sbh("sub", [128, 2 * 4096], BF16)
                sdb = sbh("sdb", [128, 2 * 4096], BF16)
                gtile = sbh("gtile", [128, 512], BF16)
                htile = sbh("htile", [128, 512], BF16)
                hTt = sbh("hTt", [128, 512], BF16)
                pg = esh.enter_context(nc.psum_tensor(px + "pg", [128, 512], F32))
                pu = esh.enter_context(nc.psum_tensor(px + "pu", [128, 512], F32))
                pdn = esh.enter_context(nc.psum_tensor(px + "pdn", [128, 1024], F32))
                ptA = esh.enter_context(nc.psum_tensor(px + "ptA", [128, 128], BF16))
                ptB = esh.enter_context(nc.psum_tensor(px + "ptB", [128, 128], BF16))

                @blk.sync
                def _(sy):
                    sy.wait_ge(s_mm, NEXP)
                    for ct in range(8):
                        sy.dma_start(out=xlnT[:, ct * TPC:(ct + 1) * TPC],
                                     in_=ln3i[:, ct * 128:(ct + 1) * 128],
                                     transpose=True).then_inc(s_lds, 16)
                    sy.dma_start(out=acc[:].rearrange("p (t d) -> p t d", d=D),
                                 in_=x3i.rearrange("(t p) d -> p t d", p=128)
                                 ).then_inc(s_ldx, 16)

                @blk.gpsimd
                def _(g):
                    g.wait_ge(s_mm, NEXP)
                    for ic in range(16):
                        if ic >= 2:
                            g.wait_ge(s_sh, 3 * NT * (ic - 1))
                        g.dma_start(
                            out=sgb[:, (ic % 2) * 4096:(ic % 2 + 1) * 4096]
                            .rearrange("p (c n) -> p c n", n=512),
                            in_=sgi[:, ic * 512:(ic + 1) * 512]
                            .rearrange("(c p) n -> p c n", p=128)
                        ).then_inc(s_ldg, 16)
                        g.dma_start(
                            out=sub[:, (ic % 2) * 4096:(ic % 2 + 1) * 4096]
                            .rearrange("p (c n) -> p c n", n=512),
                            in_=sui[:, ic * 512:(ic + 1) * 512]
                            .rearrange("(c p) n -> p c n", p=128)
                        ).then_inc(s_ldg, 16)
                        g.dma_start(
                            out=sdb[:, (ic % 2) * 4096:(ic % 2 + 1) * 4096]
                            .rearrange("p (c n) -> p c n", n=D),
                            in_=sdi[ic * 512:(ic + 1) * 512, :]
                            .rearrange("(c p) n -> p c n", p=128)
                        ).then_inc(s_ldg, 16)

                NW = NGPL

                def shld(ic):
                    return (NW + 3 * (ic + 1)) * 16

                @blk.tensor
                def _(t):
                    t.wait_ge(s_lds, 8 * 16)
                    for ic in range(16):
                        t.wait_ge(s_ldg, shld(ic))
                        for tau in range(NT):
                            G = ic * NT + tau
                            base = (ic % 2) * 4096
                            if G >= 1:
                                t.wait_ge(s_sil, G)
                            inst = None
                            for ct in range(8):
                                inst = t.matmul(
                                    pg[:],
                                    xlnT[:, ct * TPC + tau * 128:
                                         ct * TPC + (tau + 1) * 128],
                                    sgb[:, base + ct * 512:base + (ct + 1) * 512],
                                    start=(ct == 0), stop=(ct == 7))
                            inst.then_inc(s_sh, 1)
                            if G >= 1:
                                t.wait_ge(s_hm, G)
                            for ct in range(8):
                                inst = t.matmul(
                                    pu[:],
                                    xlnT[:, ct * TPC + tau * 128:
                                         ct * TPC + (tau + 1) * 128],
                                    sub[:, base + ct * 512:base + (ct + 1) * 512],
                                    start=(ct == 0), stop=(ct == 7))
                            inst.then_inc(s_sh, 1)
                            t.wait_ge(s_hm, G + 1)
                            for isub in range(4):
                                j = 4 * G + isub
                                if j >= 2:
                                    t.wait_ge(s_trv2, j - 1)
                                t.transpose((ptA if j % 2 == 0 else ptB)[:],
                                            htile[:, isub * 128:(isub + 1) * 128],
                                            ident[:]).then_inc(s_tr, 1)
                            t.wait_ge(s_trv2, 4 * (G + 1))
                            if G >= 1:
                                t.wait_ge(s_acc, 2 * G)
                            inst = None
                            for ncn in range(2):
                                for isub in range(4):
                                    inst = t.matmul(
                                        pdn[:, ncn * 512:(ncn + 1) * 512],
                                        hTt[:, isub * 128:(isub + 1) * 128],
                                        sdb[:, base + isub * 1024 + ncn * 512:
                                            base + isub * 1024 + (ncn + 1) * 512],
                                        start=(isub == 0), stop=(isub == 3))
                            inst.then_inc(s_sh, 1)

                @blk.scalar
                def _(sc):
                    for ic in range(16):
                        for tau in range(NT):
                            G = ic * NT + tau
                            sc.wait_ge(s_sh, 3 * G + 1)
                            if G >= 1:
                                sc.wait_ge(s_hm, G)
                            sc.activation(gtile[:], pg[:], AF.Silu
                                          ).then_inc(s_sil, 1)

                @blk.vector
                def _(v):
                    tr_base = TRSH
                    for ic in range(16):
                        for tau in range(NT):
                            G = ic * NT + tau
                            v.wait_ge(s_sil, G + 1)
                            v.wait_ge(s_sh, 3 * G + 2)
                            if G >= 1:
                                v.wait_ge(s_tr, tr_base + 4 * G)
                            vfence(v, v.tensor_tensor(htile[:], gtile[:],
                                                      pu[:], ALU.mult))
                            v.engine_nop().then_inc(s_hm, 1)
                            for isub in range(4):
                                j = 4 * G + isub
                                v.wait_ge(s_tr, tr_base + j + 1)
                                if G >= 1 and isub == 0:
                                    v.wait_ge(s_sh, 3 * G)
                                v.tensor_copy(
                                    hTt[:, isub * 128:(isub + 1) * 128],
                                    (ptA if j % 2 == 0 else ptB)[:]
                                ).then_inc(s_trv2, 1)
                            v.wait_ge(s_sh, 3 * G + 3)
                            v.wait_ge(s_ldx, 16)
                            for ncn in range(2):
                                v.tensor_tensor(
                                    acc[:, tau * D + ncn * 512:
                                        tau * D + (ncn + 1) * 512],
                                    acc[:, tau * D + ncn * 512:
                                        tau * D + (ncn + 1) * 512],
                                    pdn[:, ncn * 512:(ncn + 1) * 512],
                                    ALU.add).then_inc(s_acc, 1)
                    for tau in range(NT):
                        v.tensor_tensor(acc[:, tau * D:(tau + 1) * D],
                                        acc[:, tau * D:(tau + 1) * D],
                                        racc[:, tau * D:(tau + 1) * D],
                                        ALU.add).then_inc(s_fin, 1)

                @blk.sync
                def _(sy):
                    sy.wait_ge(s_fin, NT)
                    sy.dma_start(out=o_out.rearrange("(t p) d -> p t d", p=128),
                                 in_=acc[:].rearrange("p (t d) -> p t d", d=D)
                                 ).then_inc(s_st, 16)
                    sy.wait_ge(s_st, 16)


class SemRef:
    def __init__(self, h):
        self.h = h



def declare_io(nc):
    io = {}
    ext = [
        ("xs", [TPC, D], F32), ("wqp", [HP, D, 128], F32),
        ("wkp", [HP, D, 128], F32), ("wvf", [D, D], F32),
        ("bqT", [DH, H], F32), ("bkT", [DH, H], F32), ("bvf", [1, D], F32),
        ("wop", [HP, 128, D], F32), ("bo", [128, D], F32),
        ("wqT", [D, D], F32), ("rtT", [D, 255], F32), ("lv", [256, D], F32),
        ("woutT", [D, D], F32), ("wob", [128, D], F32),
        ("mk", [8, 128, 128], BF16),
        ("bsel", [128, NT], F32),
        ("wvT", [D, D], F32), ("wkT", [D, D], F32), ("wrT", [D, D], F32),
        ("tmkT", [128, 8], F32), ("tmvT", [128, 8], F32),
        ("tmrT", [128, 8], F32), ("tdT", [128, 8], F32),
        ("tfT", [128, 8], F32), ("woT", [D, D], F32), ("rwT", [D, 8], F32),
        ("selci", [64, 8], F32), ("selpre", [128, NT], F32),
        ("rgi", [8, D, 4096], F32), ("rui", [8, D, 4096], F32),
        ("rdi", [8, 4096, D], F32), ("sgi", [D, 8192], F32),
        ("sui", [D, 8192], F32), ("sdi", [8192, D], F32),
    ]
    for n, s, d in ext:
        io[n] = nc.dram_tensor(n, s, d, kind="ExternalInput").ap()
    io["out"] = nc.dram_tensor("out", [TPC, D], F32, kind="ExternalOutput").ap()
    scr = [
        ("qS", [HP, 128, TPC], BF16),
        ("agk_in", [HP, 128, TPC], BF16),
        ("agk_out", [NC * HP, 128, TPC], BF16),
        ("agv_in", [TPC, D], BF16), ("agv_out", [NC * TPC, D], BF16),
        ("x2S", [TPC, D], F32),
        ("agb_in", [NT, D], F32), ("agb_out", [NC * NT, D], F32),
        ("kvS", [8, 128, TPC], F32), ("kcS", [8, 128, TPC], F32),
        ("srS", [8, 128, TPC], F32),
        ("agc_in", [32, 1024], F32), ("agc_out", [NC * 32, 1024], F32),
        ("x3S", [TPC, D], F32), ("ln3S", [TPC, D], BF16),
        ("wdS", [NT, 128, 8], F32), ("selS", [NT, 128, 8], F32),
        ("agn_in", [NT, 8], F32), ("agn_out", [NC * NT, 8], F32),
    ]
    for n, s, d in scr:
        io[n] = nc.dram_tensor(n, s, d).ap()
    return io


def build_F():
    nc = bass.Bass("TRN2", target_bir_lowering=False, debug=False)
    io = declare_io(nc)
    for emit in (emit_A, emit_B, emit_C, emit_D, emit_E):
        with nc.cleanup_on_exit():
            emit(nc, io)
            nc.all_engine_barrier()
    return nc



def prep_shared(inputs):
    s = {}
    WQ = inputs["L0_attn_W_Q"]
    WK = inputs["L0_attn_W_K"]
    WV = inputs["L0_attn_W_V"]
    s["wqp"] = np.ascontiguousarray(
        WQ.reshape(HP, 2, D, DH).transpose(0, 2, 1, 3).reshape(HP, D, 128))
    s["wkp"] = np.ascontiguousarray(
        WK.reshape(HP, 2, D, DH).transpose(0, 2, 1, 3).reshape(HP, D, 128))
    s["wvf"] = np.ascontiguousarray(WV.transpose(1, 0, 2).reshape(D, D))
    s["bqT"] = np.ascontiguousarray(inputs["L0_attn_b_Q"].T)
    s["bkT"] = np.ascontiguousarray(inputs["L0_attn_b_K"].T)
    s["bvf"] = np.ascontiguousarray(inputs["L0_attn_b_V"].reshape(1, D))
    WO = np.asarray(inputs["L0_attn_W_O"])
    s["wop"] = np.ascontiguousarray(WO.reshape(HP, 128, D))
    s["bo"] = np.ascontiguousarray(np.broadcast_to(
        np.asarray(inputs["L0_attn_b_O"]).reshape(1, D), (128, D)))
    s["wqT"] = np.ascontiguousarray(np.asarray(inputs["L0_ffn_Wq"]).T)
    s["rtT"] = np.ascontiguousarray(np.asarray(inputs["L0_ffn_routers"]).T)
    s["lv"] = np.ascontiguousarray(np.asarray(inputs["L0_ffn_leaf_values"]))
    s["woutT"] = np.ascontiguousarray(np.asarray(inputs["L0_ffn_Wout_w"]).T)
    s["wob"] = np.ascontiguousarray(np.broadcast_to(
        np.asarray(inputs["L0_ffn_Wout_b"]).reshape(1, D), (128, D)))
    s["wvT"] = np.ascontiguousarray(np.asarray(inputs["L1_attn_Wv"]).T)
    s["wkT"] = np.ascontiguousarray(np.asarray(inputs["L1_attn_Wk"]).T)
    s["wrT"] = np.ascontiguousarray(np.asarray(inputs["L1_attn_Wr"]).T)
    s["tmkT"] = np.ascontiguousarray(
        np.asarray(inputs["L1_attn_time_mix_k"]).reshape(8, 128).T)
    s["tmvT"] = np.ascontiguousarray(
        np.asarray(inputs["L1_attn_time_mix_v"]).reshape(8, 128).T)
    s["tmrT"] = np.ascontiguousarray(
        np.asarray(inputs["L1_attn_time_mix_r"]).reshape(8, 128).T)
    s["tdT"] = np.ascontiguousarray(
        np.asarray(inputs["L1_attn_time_decay"]).reshape(8, 128).T)
    s["tfT"] = np.ascontiguousarray(
        np.asarray(inputs["L1_attn_time_first"]).reshape(8, 128).T)
    s["woT"] = np.ascontiguousarray(np.asarray(inputs["L1_attn_Wo"]).T)
    s["rwT"] = np.ascontiguousarray(np.asarray(inputs["L1_ffn_router_w"]).T)
    s["rgi"] = np.ascontiguousarray(np.asarray(inputs["L1_ffn_rg"]))
    s["rui"] = np.ascontiguousarray(np.asarray(inputs["L1_ffn_ru"]))
    s["rdi"] = np.ascontiguousarray(np.asarray(inputs["L1_ffn_rd"]))
    s["sgi"] = np.ascontiguousarray(np.asarray(inputs["L1_ffn_sg"]))
    s["sui"] = np.ascontiguousarray(np.asarray(inputs["L1_ffn_su"]))
    s["sdi"] = np.ascontiguousarray(np.asarray(inputs["L1_ffn_sd"]))
    return s


def make_mask(core):
    tri = np.triu(np.ones((128, 128))).astype(BF)
    m = np.zeros((8, 128, 128), dtype=BF)
    for r in range(8):
        if r < core:
            m[r] = 1.0
        elif r == core:
            m[r] = tri
    return m


def make_bsel(core):
    bs = np.zeros((128, NT), np.float32)
    for b in range(B):
        for k in range(8):
            tau = b * 8 + k
            if core > 0:
                bs[(core - 1) * 16 + tau, tau] = 1.0
            elif k > 0:
                bs[7 * 16 + tau - 1, tau] = 1.0
    return bs


def make_selc(core):
    s = np.zeros((64, 8), np.float32)
    for k in range(8):
        j = 8 * k + core - 1
        if j >= 0:
            s[j, k] = 1.0
    return s


def make_selpre(core):
    sp = np.zeros((128, NT), np.float32)
    for r in range(128):
        c, tp = divmod(r, 16)
        br, kr = divmod(tp, 8)
        gr = br * 64 + 8 * kr + c
        for b in range(B):
            for k in range(8):
                if gr < b * 64 + 8 * k + core:
                    sp[r, b * 8 + k] = 1.0
    return sp


def kernel(**inputs):
    x = np.asarray(inputs["L0_attn_normalized_resid_pre"], dtype=np.float32)
    xsh = shard_rows(x)
    shared = prep_shared(inputs)
    maps = [dict(xs=xsh[i], mk=make_mask(i), bsel=make_bsel(i),
                 selci=make_selc(i), selpre=make_selpre(i), **shared)
            for i in range(NC)]
    nc = build_F()
    res = run_bass_kernel_spmd(nc, maps, core_ids=list(range(NC)))
    out = unshard_rows([np.asarray(r["out"]) for r in res.results])
    return np.ascontiguousarray(out.reshape(B, SQ, D).astype(np.float32))


# version 06
# speedup vs baseline: 1.1143x; 1.0056x over previous
import numpy as np
import ml_dtypes
from contextlib import ExitStack

import concourse.bass as bass
import concourse.mybir as mybir
from concourse.bass_utils import run_bass_kernel_spmd

F32 = mybir.dt.float32
BF16 = mybir.dt.bfloat16
AF = mybir.ActivationFunctionType
ALU = mybir.AluOpType
AX = mybir.AxisListType
BF = ml_dtypes.bfloat16

NC = 8
B = 2
SQ = 8192
D = 1024
NBLK = 64
TPC = 2048
NT = 16
H = 16
HP = 8
DH = 64
LN_EPS = 1e-5



def shard_rows(x):
    xb = np.ascontiguousarray(x).reshape(B, NBLK, 128, x.shape[-1])
    return [np.ascontiguousarray(xb[:, i::8].reshape(TPC, x.shape[-1]))
            for i in range(NC)]


def unshard_rows(parts):
    Dd = parts[0].shape[-1]
    out = np.empty((B, NBLK, 128, Dd), parts[0].dtype)
    for i in range(NC):
        out[:, i::8] = np.asarray(parts[i]).reshape(B, 8, 128, Dd)
    return out.reshape(B, NBLK * 128, Dd)



def emit_ln_plan(nc, bodies, xw_tile, lnw_tile, stat, sq, nt, x_wait, app_sem,
                 rstd_sem, nm_sem, eps_ap=None, out_f32_tile=None):
    def v_ops(v, t):
        x_wait(v, t)
        p = t % 2
        inst = v.reduce_sum(stat[:, 4 + p:5 + p], xw_tile(t), axis=AX.X)
        inst.then_inc(nm_sem.h, 1)
        v.wait_ge(nm_sem.h, 2 * t + 1)
        inst = v.tensor_scalar_mul(stat[:, 0 + p:1 + p], stat[:, 4 + p:5 + p],
                                   -1.0 / D)
        inst.then_inc(nm_sem.h, 1)

    def s_ops(sc, t):
        p = t % 2
        sc.wait_ge(nm_sem.h, 2 * t + 2)
        if t >= 2:
            sc.wait_ge(app_sem.h, t - 1)
        inst = sc.activation(sq[:], xw_tile(t), AF.Square,
                             bias=stat[:, 0 + p:1 + p],
                             accum_out=stat[:, 6 + p:7 + p])
        inst.then_inc(rstd_sem.h, 1)
        sc.wait_ge(rstd_sem.h, 3 * t + 1)
        inst = sc.activation(stat[:, 6 + p:7 + p], stat[:, 6 + p:7 + p], AF.Ln,
                             scale=1.0 / D, bias=eps_ap)
        inst.then_inc(rstd_sem.h, 1)
        sc.wait_ge(rstd_sem.h, 3 * t + 2)
        inst = sc.activation(stat[:, 2 + p:3 + p], stat[:, 6 + p:7 + p], AF.Exp,
                             scale=-0.5)
        inst.then_inc(rstd_sem.h, 1)

    def v2_ops(v, t):
        p = t % 2
        v.wait_ge(rstd_sem.h, 3 * t + 3)
        inst = v.tensor_scalar(lnw_tile(t), xw_tile(t), stat[:, 0 + p:1 + p],
                               stat[:, 2 + p:3 + p], ALU.add, ALU.mult)
        if out_f32_tile is not None:
            inst = v.tensor_scalar(out_f32_tile(t), xw_tile(t),
                                   stat[:, 0 + p:1 + p], stat[:, 2 + p:3 + p],
                                   ALU.add, ALU.mult)
        inst.then_inc(app_sem.h, 1)

    return v_ops, s_ops, v2_ops



def emit_A(nc, io, px="A"):
    xs = io["xs"]
    wqp = io["wqp"]
    wkp = io["wkp"]
    wvf = io["wvf"]
    bqT = io["bqT"]
    bkT = io["bkT"]
    bvf = io["bvf"]
    o_qT = io["qS"]
    o_kT = io["agk_in"]
    o_v = io["agv_in"]

    es = ExitStack()
    sb = lambda n, s, d=F32: es.enter_context(nc.sbuf_tensor(px + n, s, d))
    psal = lambda n, s, d=F32: es.enter_context(nc.psum_tensor(px + n, s, d))
    sem = lambda n: nc.alloc_semaphore(px + n)
    with es:
        xw = sb("xw", [128, 2 * D])
        lnw = sb("lnw", [128, 2 * D], BF16)
        stat = sb("stat", [128, 8])
        sq = sb("sq", [128, D])
        xlnT = sb("xlnT", [128, 8 * TPC], BF16)
        ident = sb("ident", [128, 128], BF16)
        wq_sb = sb("wq_sb", [128, HP * 1024], BF16)
        wk_sb = sb("wk_sb", [128, HP * 1024], BF16)
        wv_sb = sb("wv_sb", [128, 8 * D], BF16)
        bq_sb = sb("bq_sb", [DH, H])
        bk_sb = sb("bk_sb", [DH, H])
        bqp = sb("bqp", [128, HP])
        bkp = sb("bkp", [128, HP])
        bv_sb = sb("bv_sb", [1, D], BF16)
        ones_row = sb("ones_row", [1, 128], BF16)
        eps_sb = sb("eps_sb", [128, 1])
        qT_sb = sb("qT_sb", [128, HP * TPC], BF16)
        kT_sb = sb("kT_sb", [128, HP * TPC], BF16)
        v_sb = sb("v_sb", [128, NT * D], BF16)
        pt0 = psal("pt0", [128, 128], BF16)
        pt1 = psal("pt1", [128, 128], BF16)
        pqk = psal("pqk", [128, 1024])
        pqk2 = psal("pqk2", [128, 1024])

        s_ld = sem("s_ld")
        s_x = sem("s_x")
        s_nm = sem("s_nm")
        s_rstd = sem("s_rstd")
        s_app = sem("s_app")
        s_tr = sem("s_tr")
        s_trv = sem("s_trv")
        s_mm = sem("s_mm")
        s_ev = sem("s_ev")
        s_st = sem("s_st")
        s_cc = sem("s_cc")

        cid = nc.inline_tensor(np.eye(128, dtype=BF), px + "cident")
        cones = nc.inline_tensor(np.ones((1, 128), dtype=BF), px + "cones")

        N_WLOADS = 2 + 2 * HP + 1 + 3
        TRT = 8 * NT
        groups = ([("k", hp, tc) for hp in range(HP) for tc in range(4)]
                  + [("v", tau, hc) for tau in range(NT) for hc in range(2)]
                  + [("q", hp, tc) for hp in range(HP) for tc in range(4)])
        NG = len(groups)
        pslot = [(pqk, 0), (pqk, 512), (pqk2, 0), (pqk2, 512)]

        with nc.Block() as blk:
            @blk.gpsimd
            def _(g):
                g.dma_start(out=ident[:], in_=cid.ap()).then_inc(s_ld, 16)
                g.dma_start(out=ones_row[:], in_=cones.ap()).then_inc(s_ld, 16)
                for hp in range(HP):
                    g.dma_start(out=wq_sb[:, hp * 1024:(hp + 1) * 1024]
                                .rearrange("p (ct e) -> p ct e", e=128),
                                in_=wqp[hp].rearrange("(ct p) e -> p ct e", p=128)
                                ).then_inc(s_ld, 16)
                    g.dma_start(out=wk_sb[:, hp * 1024:(hp + 1) * 1024]
                                .rearrange("p (ct e) -> p ct e", e=128),
                                in_=wkp[hp].rearrange("(ct p) e -> p ct e", p=128)
                                ).then_inc(s_ld, 16)
                g.dma_start(out=wv_sb[:].rearrange("p (ct n) -> p ct n", n=D),
                            in_=wvf.rearrange("(ct p) n -> p ct n", p=128)
                            ).then_inc(s_ld, 16)
                g.dma_start(out=bq_sb[:], in_=bqT).then_inc(s_ld, 16)
                g.dma_start(out=bk_sb[:], in_=bkT).then_inc(s_ld, 16)
                g.dma_start(out=bv_sb[:], in_=bvf).then_inc(s_ld, 16)
                for t in range(NT):
                    if t >= 2:
                        g.wait_ge(s_rstd, 3 * (t - 1))
                        g.wait_ge(s_app, t - 1)
                    g.dma_start(out=xw[:, (t % 2) * D:(t % 2 + 1) * D],
                                in_=xs[t * 128:(t + 1) * 128, :]).then_inc(s_x, 16)

            xw_tile = lambda t: xw[:, (t % 2) * D:(t % 2 + 1) * D]
            lnw_tile = lambda t: lnw[:, (t % 2) * D:(t % 2 + 1) * D]

            def x_wait(v, t):
                v.wait_ge(s_x, (t + 1) * 16)

            v_ops, s_ops, v2_ops = emit_ln_plan(
                nc, None, xw_tile, lnw_tile, stat, sq, NT, x_wait,
                SemRef(s_app), SemRef(s_rstd), SemRef(s_nm), eps_ap=eps_sb[:])

            @blk.vector
            def _(v):
                v.memset(eps_sb[:], LN_EPS)
                v.wait_ge(s_ld, N_WLOADS * 16)
                for hp in range(HP):
                    v.tensor_scalar_mul(bqp[0:64, hp:hp + 1],
                                        bq_sb[:, 2 * hp:2 * hp + 1], 1.0 / 256.0)
                    v.tensor_scalar_mul(bqp[64:128, hp:hp + 1],
                                        bq_sb[:, 2 * hp + 1:2 * hp + 2], 1.0 / 256.0)
                    v.tensor_scalar_mul(bkp[0:64, hp:hp + 1],
                                        bk_sb[:, 2 * hp:2 * hp + 1], 1.0 / 32.0)
                    v.tensor_scalar_mul(bkp[64:128, hp:hp + 1],
                                        bk_sb[:, 2 * hp + 1:2 * hp + 2], 1.0 / 32.0)
                for t in range(NT):
                    if t >= 2:
                        v.wait_ge(s_tr, 8 * (t - 1))
                    v_ops(v, t)
                    v2_ops(v, t)
                    if t >= 1:
                        for ct in range(8):
                            j = 8 * (t - 1) + ct
                            v.wait_ge(s_tr, j + 1)
                            v.tensor_copy(
                                xlnT[:, ct * TPC + (t - 1) * 128:
                                     ct * TPC + t * 128],
                                (pt0 if j % 2 == 0 else pt1)[:]
                            ).then_inc(s_trv, 1)
                for ct in range(8):
                    j = 8 * (NT - 1) + ct
                    v.wait_ge(s_tr, j + 1)
                    v.tensor_copy(
                        xlnT[:, ct * TPC + (NT - 1) * 128:ct * TPC + NT * 128],
                        (pt0 if j % 2 == 0 else pt1)[:]).then_inc(s_trv, 1)

            @blk.scalar
            def _(sc):
                for t in range(NT):
                    s_ops(sc, t)
                for gi, grp in enumerate(groups):
                    pbuf, off = pslot[gi % 4]
                    sc.wait_ge(s_mm, gi + 1)
                    kind = grp[0]
                    if kind == "q":
                        _, hp, tc = grp
                        sc.activation(qT_sb[:, hp * TPC + tc * 512:
                                            hp * TPC + (tc + 1) * 512],
                                      pbuf[:, off:off + 512], AF.Identity,
                                      bias=bqp[:, hp:hp + 1], scale=1.0 / 256.0
                                      ).then_inc(s_ev, 1)
                    elif kind == "k":
                        _, hp, tc = grp
                        sc.activation(kT_sb[:, hp * TPC + tc * 512:
                                            hp * TPC + (tc + 1) * 512],
                                      pbuf[:, off:off + 512], AF.Identity,
                                      bias=bkp[:, hp:hp + 1], scale=1.0 / 32.0
                                      ).then_inc(s_ev, 1)
                    else:
                        _, tau, hc = grp
                        sc.activation(v_sb[:, tau * D + hc * 512:
                                           tau * D + (hc + 1) * 512],
                                      pbuf[:, off:off + 512], AF.Copy,
                                      scale=1.0 / 32.0).then_inc(s_ev, 1)

            @blk.tensor
            def _(t):
                for tt in range(NT):
                    t.wait_ge(s_app, tt + 1)
                    for ct in range(8):
                        j = 8 * tt + ct
                        if j >= 2:
                            t.wait_ge(s_trv, j - 1)
                        t.transpose((pt0 if j % 2 == 0 else pt1)[:],
                                    lnw[:, (tt % 2) * D + ct * 128:
                                        (tt % 2) * D + (ct + 1) * 128],
                                    ident[:]).then_inc(s_tr, 1)
                t.wait_ge(s_trv, TRT)
                t.wait_ge(s_ld, N_WLOADS * 16)
                for gi, grp in enumerate(groups):
                    pbuf, off = pslot[gi % 4]
                    if gi >= 4:
                        t.wait_ge(s_ev, gi - 3)
                    kind = grp[0]
                    if kind in ("q", "k"):
                        _, hp, tc = grp
                        wsb = wq_sb if kind == "q" else wk_sb
                        for ct in range(8):
                            inst = t.matmul(
                                pbuf[:, off:off + 512],
                                wsb[:, hp * 1024 + ct * 128:
                                    hp * 1024 + (ct + 1) * 128],
                                xlnT[:, ct * TPC + tc * 512:
                                     ct * TPC + (tc + 1) * 512],
                                start=(ct == 0), stop=(ct == 7))
                        inst.then_inc(s_mm, 1)
                    else:
                        _, tau, hc = grp
                        t.matmul(pbuf[:, off:off + 512], ones_row[:],
                                 bv_sb[0:1, hc * 512:(hc + 1) * 512],
                                 start=True, stop=False)
                        for ct in range(8):
                            inst = t.matmul(
                                pbuf[:, off:off + 512],
                                xlnT[:, ct * TPC + tau * 128:
                                     ct * TPC + (tau + 1) * 128],
                                wv_sb[:, ct * D + hc * 512:
                                      ct * D + (hc + 1) * 512],
                                start=False, stop=(ct == 7))
                        inst.then_inc(s_mm, 1)

            @blk.sync
            def _(sy):
                for hp in range(HP):
                    sy.wait_ge(s_ev, 4 * hp + 4)
                    sy.dma_start(out=o_kT[hp], in_=kT_sb[:, hp * TPC:(hp + 1) * TPC]
                                 ).then_inc(s_st, 16)
                for tau in range(NT):
                    sy.wait_ge(s_ev, 32 + 2 * tau + 2)
                    sy.dma_start(out=o_v[tau * 128:(tau + 1) * 128, :],
                                 in_=v_sb[:, tau * D:(tau + 1) * D]
                                 ).then_inc(s_st, 16)
                for hp in range(HP):
                    sy.wait_ge(s_ev, 64 + 4 * hp + 4)
                    sy.dma_start(out=o_qT[hp], in_=qT_sb[:, hp * TPC:(hp + 1) * TPC]
                                 ).then_inc(s_st, 16)
                sy.wait_ge(s_st, 32 * 16)

            @blk.gpsimd
            def _(g):
                g.wait_ge(s_st, 8 * 16)
                g.collective_compute(
                    "AllGather", ALU.bypass,
                    replica_groups=[list(range(NC))],
                    ins=[io["agk_in"]], outs=[io["agk_out"]]).then_inc(s_cc)
                g.wait_ge(s_st, 24 * 16)
                g.collective_compute(
                    "AllGather", ALU.bypass,
                    replica_groups=[list(range(NC))],
                    ins=[io["agv_in"]], outs=[io["agv_out"]]).then_inc(io["ccv"])
                g.wait_ge(s_cc, 1)




def attn_plan():
    items = []
    st = sa = spv = svd = sz = 0
    for b in range(B):
        for hp in range(HP):
            for h2 in range(2):
                for qc in range(2):
                    kts = 32 * qc + 32
                    G = (kts + 1) // 2
                    qlo = []
                    for kt in range(kts):
                        a = kt // 8
                        qlo.append(128 * max(0, a - 4 * qc))
                    diag_of_group = {}
                    for gi in range(G):
                        dk = []
                        for kt in (2 * gi, 2 * gi + 1):
                            if kt >= kts:
                                continue
                            a = kt // 8
                            if a >= 4 * qc:
                                dk.append((kt, a - 4 * qc, kt % 8))
                        if dk:
                            diag_of_group[gi] = dk
                    items.append(dict(b=b, hp=hp, h2=h2, qc=qc,
                                      kts=kts, G=G, qlo=qlo,
                                      diag=diag_of_group,
                                      st0=st, sa0=sa, spv0=spv, svd0=svd,
                                      sz0=sz))
                    st += G
                    sa += G
                    spv += G
                    svd += len(diag_of_group)
                    sz += 1
    return items, st, sa, spv, svd, sz


SERIAL_ATTN = False
WO_BIAS_MM = True


def emit_B(nc, io, px="B", dbg=False):
    xs = io["xs"]
    qTi = io["qS"]
    agk_out = io["agk_out"]
    agv_out = io["agv_out"]
    wop = io["wop"]
    bo = io["bo"]
    wqT = io["wqT"]
    rtT = io["rtT"]
    lv = io["lv"]
    woutT = io["woutT"]
    wob = io["wob"]
    mk = io["mk"]
    o_x2 = io["x2S"]

    cid = nc.inline_tensor(np.eye(128, dtype=BF), px + "cident")
    cones = nc.inline_tensor(np.ones((1, 128), dtype=BF), px + "cones")
    c8 = nc.inline_tensor(np.full((1, 64), 0.125, dtype=np.float32), px + "c8")
    pmat_np = np.zeros((512, 256), dtype=BF)
    for m in range(256):
        for lev in range(8):
            node = (2 ** lev - 1) + (m >> (8 - lev))
            branch = (m >> (7 - lev)) & 1
            pmat_np[node if branch else 256 + node, m] = 1.0
    cpm = nc.inline_tensor(pmat_np, px + "cpmat")

    items, ST, SA, SPV, SVD, SZ = attn_plan()

    es = ExitStack()
    sb = lambda n, s, d=F32: es.enter_context(nc.sbuf_tensor(px + n, s, d))
    sem = lambda n: nc.alloc_semaphore(px + n)
    with es:
        resid = sb("resid", [128, NT * D])
        ident = sb("ident", [128, 128], BF16)
        mk_sb = sb("mk_sb", [128, 8 * 128], BF16)
        ones_row = sb("ones_row", [1, 128], BF16)
        oneseighth = sb("oneseighth", [1, 64])
        bo8 = sb("bo8", [128, D])
        wob_sb = sb("wob_sb", [128, D])
        wtmp = sb("wtmp", [128, D])
        stat = sb("stat", [128, 8])
        sqs = sb("sqs", [128, D])
        eps_sb = sb("eps_sb", [128, 1])
        one_sb = sb("one_sb", [128, 1])
        rl_sb = sb("rl_sb", [1, 512])
        bc_sb = sb("bc_sb", [64, 512])

        s_ld = sem("s_ld")
        s_ldg = sem("s_ldg")
        s_x = sem("s_x")
        s_kl = sem("s_kl")
        s_vl = sem("s_vl")
        s_t = sem("s_t")
        s_a = sem("s_a")
        s_vd = sem("s_vd")
        s_pv = sem("s_pv")
        s_zr = sem("s_zr")
        s_zb = sem("s_zb")
        s_zc = sem("s_zc")
        s_ze = sem("s_ze")
        s_pre = sem("s_pre")
        s_wo = sem("s_wo")
        s_woe = sem("s_woe")
        s_nm = sem("s_nm")
        s_rstd = sem("s_rstd")
        s_app = sem("s_app")
        s_tr = sem("s_tr")
        s_trv = sem("s_trv")
        s_mm = sem("s_mm")
        s_ev = sem("s_ev")
        s_sc = sem("s_sc")
        s_st = sem("s_st")
        s_vf = sem("s_vf")
        s_cc = sem("s_cc")
        VF0 = 0

        with nc.Block() as blk:
            @blk.sync
            def _(sy):
                sy.dma_start(out=ident[:], in_=cid.ap()).then_inc(s_ld, 16)
                sy.dma_start(out=mk_sb[:].rearrange("p (r n) -> p r n", n=128),
                             in_=mk.rearrange("r p n -> p r n")).then_inc(s_ld, 16)
                sy.dma_start(out=ones_row[:], in_=cones.ap()).then_inc(s_ld, 16)
                sy.dma_start(out=oneseighth[:], in_=c8.ap()).then_inc(s_ld, 16)
                sy.dma_start(out=wtmp[:], in_=bo).then_inc(s_ld, 16)
                for t in range(NT):
                    sy.dma_start(out=resid[:, t * D:(t + 1) * D],
                                 in_=xs[t * 128:(t + 1) * 128, :]).then_inc(s_x, 16)
            NLD0 = 5

            esz = ExitStack()
            z2T = esz.enter_context(nc.sbuf_tensor(px + "z2T", [128, NT * D], BF16))
            with ExitStack() as es1:
                sb1 = lambda n, s, d=F32: es1.enter_context(nc.sbuf_tensor(px + n, s, d))
                q_sb = sb1("q_sb", [128, HP * TPC], BF16)
                kp_sb = sb1("kp_sb", [128, 2 * SQ], BF16)
                vh_sb = sb1("vh_sb", [128, 2 * 64 * 65], BF16)
                esb0 = sb1("esb0", [128, 1024], BF16)
                esb1 = sb1("esb1", [128, 1024], BF16)
                ps_s0 = es1.enter_context(nc.psum_tensor(px + "ps_s0", [128, 1024], F32))
                ps_s1 = es1.enter_context(nc.psum_tensor(px + "ps_s1", [128, 1024], F32))
                ps_z = es1.enter_context(nc.psum_tensor(px + "ps_z", [65, 512], F32))
                ps_b = es1.enter_context(nc.psum_tensor(px + "ps_b", [64, 512], F32))
                psl = [ps_s0, ps_s1]
                esl = [esb0, esb1]

                @blk.vector
                def _(v):
                    v.memset(eps_sb[:], LN_EPS)
                    v.memset(one_sb[:], 1.0)
                    v.wait_ge(s_ld, NLD0 * 16)
                    v.tensor_scalar_mul(bo8[:], wtmp[:], 0.125).then_inc(s_pre, 1)

                @blk.gpsimd
                def _(g):
                    for hp in range(HP):
                        g.dma_start(out=q_sb[:, hp * TPC:(hp + 1) * TPC],
                                    in_=qTi[hp]).then_inc(s_ldg, 16)
                    g.memset(vh_sb[:].rearrange("p (s kt e) -> p s kt e", s=2, e=65)
                             [:, :, :, 64:65], 1.0).then_inc(s_pre, 1)

                @blk.sync
                def _(sy):
                    loads = ([("k", m, 4 * m) for m in range(B * HP)]
                             + [("v", n, 2 * n) for n in range(B * H)])
                    loads.sort(key=lambda x: (x[2], x[0]))
                    kcnt = vcnt = 0
                    ccv_waited = False
                    for typ, idx, _need in loads:
                        if typ == "v" and not ccv_waited:
                            sy.wait_ge(io["ccv"], 1)
                            ccv_waited = True
                        if typ == "k":
                            b, hp = divmod(idx, HP)
                            if idx >= 2:
                                e_it = items[(idx - 2) * 4 + 3]
                                sy.wait_ge(s_t, e_it["st0"] + e_it["G"])
                            dstk = kp_sb[:, (idx % 2) * SQ:(idx % 2 + 1) * SQ] \
                                .rearrange("p (k c t) -> p k c t", c=NC, t=128)
                            for c in range(NC):
                                sy.dma_start(
                                    out=dstk[:, :, c, :],
                                    in_=agk_out[c * HP + hp]
                                    [:, b * 1024:(b + 1) * 1024]
                                    .rearrange("p (k t) -> p k t", t=128)
                                ).then_inc(s_kl, 16)
                            kcnt += 1
                        else:
                            b, h = divmod(idx, H)
                            if idx >= 2:
                                e_it = items[(idx - 2) * 2 + 1]
                                sy.wait_ge(s_pv, e_it["spv0"] + e_it["G"])
                            dstv = vh_sb[:, (idx % 2) * 4160:(idx % 2 + 1) * 4160] \
                                .rearrange("p (k c e) -> p k c e", c=NC, e=65)
                            for c in range(NC):
                                sy.dma_start(
                                    out=dstv[:, :, c, 0:64],
                                    in_=agv_out[c * TPC + b * 1024:
                                                c * TPC + (b + 1) * 1024,
                                                h * DH:(h + 1) * DH]
                                    .rearrange("(k p) e -> p k e", p=128)
                                ).then_inc(s_vl, 16)
                            vcnt += 1

                @blk.scalar
                def _(sc):
                    for it in items:
                        for g in range(it["G"]):
                            ga = it["sa0"] + g
                            sc.wait_ge(s_t, it["st0"] + g + 1)
                            if ga >= 2:
                                sc.wait_ge(s_pv, ga - 1)
                            lo = min((kt % 2) * 512 + it["qlo"][kt]
                                     for kt in (2 * g, 2 * g + 1) if kt < it["kts"])
                            hi = 512 if (2 * g + 1 >= it["kts"]) else 1024
                            sc.activation(esl[g % 2][:, lo:hi],
                                          psl[g % 2][:, lo:hi], AF.Exp
                                          ).then_inc(s_a, 1)

                @blk.tensor
                def _(t):
                    t.wait_ge(s_ld, NLD0 * 16)
                    t.wait_ge(s_ldg, HP * 16)
                    t.wait_ge(s_pre, 2)
                    for it in items:
                        b, hp, h2, qc = it["b"], it["hp"], it["h2"], it["qc"]
                        m = b * HP + hp
                        n = m * 2 + h2
                        G, kts, qlo = it["G"], it["kts"], it["qlo"]
                        t.wait_ge(s_kl, (m + 1) * NC * 16)
                        t.wait_ge(s_vl, (n + 1) * NC * 16)
                        kbase = (m % 2) * SQ
                        vbase = (n % 2) * 4160
                        qbase = hp * TPC + b * 1024 + qc * 512
                        if it["sz0"] >= 1:
                            t.wait_ge(s_ze, it["sz0"])
                        if it["sa0"] >= 1:
                            t.wait_ge(s_a, it["sa0"])

                        def pv_group(gp):
                            final = (gp == G - 1)
                            t.wait_ge(s_a, it["sa0"] + gp + 1)
                            nd = sum(1 for gg in it["diag"] if gg <= gp)
                            if nd:
                                t.wait_ge(s_vd, it["svd0"] + nd)
                            inst = None
                            for kt in (2 * gp, 2 * gp + 1):
                                if kt >= kts:
                                    continue
                                inst = t.matmul(
                                    ps_z[:, qlo[kt]:512],
                                    vh_sb[:, vbase + kt * 65:vbase + (kt + 1) * 65],
                                    esl[gp % 2][:, (kt % 2) * 512 + qlo[kt]:
                                                (kt % 2) * 512 + 512],
                                    start=(kt == 0), stop=final)
                            inst.then_inc(s_pv, 1)

                        for g in range(G):
                            if SERIAL_ATTN:
                                t.wait_ge(s_a, it["sa0"] + g)
                            elif g >= 2:
                                t.wait_ge(s_a, it["sa0"] + g - 1)
                            inst = None
                            for kt in (2 * g, 2 * g + 1):
                                if kt >= kts:
                                    continue
                                inst = t.matmul(
                                    psl[g % 2][:, (kt % 2) * 512 + qlo[kt]:
                                               (kt % 2) * 512 + 512],
                                    kp_sb[h2 * 64:(h2 + 1) * 64,
                                          kbase + kt * 128:kbase + (kt + 1) * 128],
                                    q_sb[h2 * 64:(h2 + 1) * 64,
                                         qbase + qlo[kt]:qbase + 512],
                                    start=True, stop=True)
                            inst.then_inc(s_t, 1)
                            if SERIAL_ATTN:
                                pv_group(g)
                            elif g >= 1:
                                pv_group(g - 1)
                        if not SERIAL_ATTN:
                            pv_group(G - 1)
                        t.wait_ge(s_zr, it["sz0"] + 1)
                        if it["sz0"] >= 1:
                            t.wait_ge(s_zc, it["sz0"])
                        t.matmul(ps_b[:], oneseighth[:], rl_sb[:],
                                 start=True, stop=True).then_inc(s_zb, 1)

                @blk.vector
                def _(v):
                    for it in items:
                        b, hp, h2, qc = it["b"], it["hp"], it["h2"], it["qc"]
                        for gi, dks in sorted(it["diag"].items()):
                            v.wait_ge(s_a, it["sa0"] + gi + 1)
                            inst = None
                            for kt, koff, r in dks:
                                pos = (kt % 2) * 512 + koff * 128
                                inst = v.tensor_tensor(
                                    esl[gi % 2][:, pos:pos + 128],
                                    esl[gi % 2][:, pos:pos + 128],
                                    mk_sb[:, r * 128:(r + 1) * 128],
                                    ALU.mult)
                            inst.then_inc(s_vd, 1)
                        v.wait_ge(s_pv, it["spv0"] + it["G"])
                        v.reciprocal(rl_sb[:], ps_z[64:65, :]).then_inc(s_zr, 1)
                        v.wait_ge(s_zb, it["sz0"] + 1)
                        v.tensor_copy(bc_sb[:], ps_b[:]).then_inc(s_zc, 1)
                        v.wait_ge(s_zc, it["sz0"] + 1)
                        zc = (b * 8 + hp) * 1024 + qc * 512
                        v.tensor_tensor(z2T[h2 * 64:(h2 + 1) * 64, zc:zc + 512],
                                        ps_z[0:64, :], bc_sb[:],
                                        ALU.mult).then_inc(s_ze, 1)

            with ExitStack() as es2:
                wo_sb = es2.enter_context(nc.sbuf_tensor(px + "wo_sb", [128, HP * D], BF16))
                ps_o0 = es2.enter_context(nc.psum_tensor(px + "ps_o0", [128, 1024], F32))
                ps_o1 = es2.enter_context(nc.psum_tensor(px + "ps_o1", [128, 1024], F32))

                @blk.gpsimd
                def _(g):
                    g.wait_ge(s_ze, SZ)
                    for hp in range(HP):
                        g.dma_start(out=wo_sb[:, hp * D:(hp + 1) * D],
                                    in_=wop[hp]).then_inc(s_ldg, 16)
                NLDG1 = HP + HP

                @blk.tensor
                def _(t):
                    t.wait_ge(s_ldg, NLDG1 * 16)
                    t.wait_ge(s_pre, 2)
                    for tau in range(NT):
                        b, k = divmod(tau, 8)
                        pso = [ps_o0, ps_o1][tau % 2]
                        if tau >= 2:
                            t.wait_ge(s_woe, tau - 1)
                        last = items[(b * HP + HP - 1) * 4 + 3]
                        t.wait_ge(s_ze, last["sz0"] + 1)
                        inst = None
                        for half in range(2):
                            for hp in range(HP):
                                inst = t.matmul(
                                    pso[:, half * 512:(half + 1) * 512],
                                    z2T[:, (b * 8 + hp) * 1024 + k * 128:
                                        (b * 8 + hp) * 1024 + (k + 1) * 128],
                                    wo_sb[:, hp * D + half * 512:
                                          hp * D + (half + 1) * 512],
                                    start=(hp == 0), stop=(hp == HP - 1))
                        inst.then_inc(s_wo, 1)

                @blk.vector
                def _(v):
                    v.wait_ge(s_pre, 1)
                    for tau in range(NT):
                        pso = [ps_o0, ps_o1][tau % 2]
                        v.wait_ge(s_wo, tau + 1)
                        v.wait_ge(s_x, (tau + 1) * 16)
                        if tau >= 1:
                            v.wait_ge(s_woe, tau)
                        v.tensor_tensor(wtmp[:], pso[:], bo8[:],
                                        ALU.add).then_inc(s_vf, 1)
                        v.wait_ge(s_vf, VF0 + tau + 1)
                        v.tensor_tensor(
                            resid[:, tau * D:(tau + 1) * D],
                            resid[:, tau * D:(tau + 1) * D], wtmp[:],
                            ALU.add).then_inc(s_woe, 1)

            if dbg:
                @blk.gpsimd
                def _(g):
                    g.wait_ge(s_ze, SZ)
                    g.dma_start(out=o_z2, in_=z2T[:]).then_inc(s_st, 16)
                    g.wait_ge(s_woe, NT)
                    for tau in range(NT):
                        g.dma_start(out=o_x1[tau * 128:(tau + 1) * 128, :],
                                    in_=resid[:, tau * D:(tau + 1) * D]
                                    ).then_inc(s_st, 16)
                    g.wait_ge(s_st, 17 * 16)
            esz.close()

            with ExitStack() as es3:
                sb3 = lambda n, s, d=F32: es3.enter_context(nc.sbuf_tensor(px + n, s, d))
                lnw = sb3("lnw", [128, 2 * D], BF16)
                xlnT = sb3("xlnT", [128, 8 * TPC], BF16)
                pt0 = es3.enter_context(nc.psum_tensor(px + "pt0", [128, 128], BF16))
                pt1 = es3.enter_context(nc.psum_tensor(px + "pt1", [128, 128], BF16))
                pm0 = es3.enter_context(nc.psum_tensor(px + "pm0", [128, 512], F32))
                pm1 = es3.enter_context(nc.psum_tensor(px + "pm1", [128, 512], F32))
                pm2 = es3.enter_context(nc.psum_tensor(px + "pm2", [128, 512], F32))
                pm3 = es3.enter_context(nc.psum_tensor(px + "pm3", [128, 512], F32))
                pms = [pm0, pm1, pm2, pm3]

                xw_tile = lambda t: resid[:, t * D:(t + 1) * D]
                lnw_tile = lambda t: lnw[:, (t % 2) * D:(t % 2 + 1) * D]

                def x_wait(v, t):
                    v.wait_ge(s_wo, NT)
                    if dbg:
                        v.wait_ge(s_st, 16)
                    v.wait_ge(s_woe, t + 1)

                v_ops, s_ops, v2_ops = emit_ln_plan(
                    nc, None, xw_tile, lnw_tile, stat, sqs, NT, x_wait,
                    SemRef(s_app), SemRef(s_rstd), SemRef(s_nm),
                    eps_ap=eps_sb[:])

                @blk.vector
                def _(v):
                    for t in range(NT):
                        if t >= 2:
                            v.wait_ge(s_tr, 8 * (t - 1))
                        v_ops(v, t)
                        v2_ops(v, t)
                        if t >= 1:
                            for ct in range(8):
                                j = 8 * (t - 1) + ct
                                v.wait_ge(s_tr, j + 1)
                                v.tensor_copy(
                                    xlnT[:, ct * TPC + (t - 1) * 128:
                                         ct * TPC + t * 128],
                                    (pt0 if j % 2 == 0 else pt1)[:]
                                ).then_inc(s_trv, 1)
                    for ct in range(8):
                        j = 8 * (NT - 1) + ct
                        v.wait_ge(s_tr, j + 1)
                        v.tensor_copy(xlnT[:, ct * TPC + (NT - 1) * 128:
                                           ct * TPC + NT * 128],
                                      (pt0 if j % 2 == 0 else pt1)[:]
                                      ).then_inc(s_trv, 1)

                @blk.scalar
                def _(sc):
                    for t in range(NT):
                        s_ops(sc, t)

                @blk.tensor
                def _(t):
                    for tt in range(NT):
                        t.wait_ge(s_app, tt + 1)
                        for ct in range(8):
                            j = 8 * tt + ct
                            if j >= 2:
                                t.wait_ge(s_trv, j - 1)
                            t.transpose((pt0 if j % 2 == 0 else pt1)[:],
                                        lnw[:, (tt % 2) * D + ct * 128:
                                            (tt % 2) * D + (ct + 1) * 128],
                                        ident[:]).then_inc(s_tr, 1)

                with ExitStack() as es4:
                    sb4 = lambda n, s, d=F32: es4.enter_context(nc.sbuf_tensor(px + n, s, d))
                    lgT = sb4("lgT", [128, 4 * TPC], BF16)
                    lpT = sb4("lpT", [128, 2 * TPC], BF16)
                    tmp_e = sb4("tmp_e", [128, 512])
                    pm_sb = sb4("pm_sb", [128, 4 * 256], BF16)
                    lv_sb = sb4("lv_sb", [128, 2 * D], BF16)

                    fgroups = ([("q0", ec, tc) for ec in range(8) for tc in range(4)]
                               + [("gl", rc, tc) for rc in range(2) for tc in range(4)]
                               + [("lp", mc, tc) for mc in range(2) for tc in range(4)]
                               + [("ret", dc, tc) for dc in range(8) for tc in range(4)]
                               + [("out", tau, half) for tau in range(NT)
                                  for half in range(2)])

                    def scalar_evac(sc, gi, grp):
                        pm = pms[gi % 4]
                        kind = grp[0]
                        sc.wait_ge(s_mm, gi + 1)
                        if kind == "q0":
                            _, ec, tc = grp
                            sc.activation(qT0[:, ec * TPC + tc * 512:
                                              ec * TPC + (tc + 1) * 512],
                                          pm[:], AF.Copy).then_inc(s_ev, 1)
                        elif kind == "gl":
                            _, rc, tc = grp
                            scc[0] += 1
                            sc.activation(tmp_e[:], pm[:], AF.Exp,
                                          scale=-1.0).then_inc(s_sc, 1)
                            sc.wait_ge(s_sc, scc[0])
                            scc[0] += 1
                            sc.activation(
                                lgT[:, rc * TPC + tc * 512:
                                    rc * TPC + (tc + 1) * 512],
                                tmp_e[:], AF.Ln, bias=one_sb[:]).then_inc(s_sc, 1)
                            sc.wait_ge(s_sc, scc[0])
                            scc[0] += 1
                            sc.activation(tmp_e[:], pm[:], AF.Exp,
                                          scale=1.0).then_inc(s_sc, 1)
                            sc.wait_ge(s_sc, scc[0])
                            sc.activation(
                                lgT[:, (2 + rc) * TPC + tc * 512:
                                    (2 + rc) * TPC + (tc + 1) * 512],
                                tmp_e[:], AF.Ln, bias=one_sb[:]
                            ).then_inc(s_ev, 1)
                        elif kind == "lp":
                            _, mc, tc = grp
                            sc.activation(lpT[:, mc * TPC + tc * 512:
                                              mc * TPC + (tc + 1) * 512],
                                          pm[:], AF.Exp, scale=-1.0
                                          ).then_inc(s_ev, 1)
                        elif kind == "ret":
                            _, dc, tc = grp
                            sc.activation(retT[:, dc * TPC + tc * 512:
                                               dc * TPC + (tc + 1) * 512],
                                          pm[:], AF.Copy).then_inc(s_ev, 1)

                    scc = [0]

                    with ExitStack() as es5:
                        sb5 = lambda n, s, d=F32: es5.enter_context(
                            nc.sbuf_tensor(px + n, s, d))
                        wq0 = sb5("wq0", [128, 8 * D], BF16)
                        rt_sb = sb5("rt_sb", [128, 8 * 256], BF16)
                        qT0 = sb5("qT0", [128, 8 * TPC], BF16)

                        @blk.gpsimd
                        def _(g):
                            g.wait_ge(s_wo, NT)
                            g.dma_start(out=wq0[:].rearrange("p (c n) -> p c n", n=D),
                                        in_=wqT.rearrange("(c p) n -> p c n", p=128)
                                        ).then_inc(s_ldg, 16)
                            g.memset(rt_sb[:], 0.0).then_inc(s_pre, 1)
                            g.wait_ge(s_pre, 3)
                            g.dma_start(out=rt_sb[:]
                                        .rearrange("p (c n) -> p c n", n=256)
                                        [:, :, 0:255],
                                        in_=rtT.rearrange("(c p) n -> p c n", p=128)
                                        ).then_inc(s_ldg, 16)
                            g.dma_start(out=pm_sb[:]
                                        .rearrange("p (c n) -> p c n", n=256),
                                        in_=cpm.ap().rearrange("(c p) n -> p c n",
                                                               p=128)
                                        ).then_inc(s_ldg, 16)
                            g.dma_start(out=lv_sb[:]
                                        .rearrange("p (c n) -> p c n", n=D),
                                        in_=lv.rearrange("(c p) n -> p c n", p=128)
                                        ).then_inc(s_ldg, 16)
                            g.dma_start(out=wob_sb[:], in_=wob).then_inc(s_ldg, 16)
                        NLDG2 = NLDG1 + 5

                        @blk.tensor
                        def _(t):
                            t.wait_ge(s_ldg, NLDG2 * 16)
                            t.wait_ge(s_trv, 8 * NT)
                            for gi, grp in enumerate(fgroups[:40]):
                                pm = pms[gi % 4]
                                if gi >= 4:
                                    t.wait_ge(s_ev, gi - 3)
                                kind = grp[0]
                                inst = None
                                if kind == "q0":
                                    _, ec, tc = grp
                                    for ct in range(8):
                                        inst = t.matmul(
                                            pm[:],
                                            wq0[:, ct * D + ec * 128:
                                                ct * D + (ec + 1) * 128],
                                            xlnT[:, ct * TPC + tc * 512:
                                                 ct * TPC + (tc + 1) * 512],
                                            start=(ct == 0), stop=(ct == 7))
                                else:
                                    _, rc, tc = grp
                                    t.wait_ge(s_ev, 32)
                                    for ct in range(8):
                                        inst = t.matmul(
                                            pm[:],
                                            rt_sb[:, ct * 256 + rc * 128:
                                                  ct * 256 + (rc + 1) * 128],
                                            qT0[:, ct * TPC + tc * 512:
                                                ct * TPC + (tc + 1) * 512],
                                            start=(ct == 0), stop=(ct == 7))
                                inst.then_inc(s_mm, 1)

                        @blk.scalar
                        def _(sc):
                            for gi, grp in enumerate(fgroups[:40]):
                                scalar_evac(sc, gi, grp)

                    with ExitStack() as es6:
                        sb6 = lambda n, s, d=F32: es6.enter_context(
                            nc.sbuf_tensor(px + n, s, d))
                        retT = sb6("retT", [128, 8 * TPC], BF16)
                        wout_sb = sb6("wout_sb", [128, 8 * D], BF16)

                        @blk.gpsimd
                        def _(g):
                            g.wait_ge(s_mm, 40)
                            g.dma_start(out=wout_sb[:]
                                        .rearrange("p (c n) -> p c n", n=D),
                                        in_=woutT.rearrange("(c p) n -> p c n",
                                                            p=128)
                                        ).then_inc(s_ldg, 16)
                        NLDG3 = NLDG2 + 1

                        @blk.tensor
                        def _(t):
                            t.wait_ge(s_ldg, NLDG3 * 16)
                            for gj, grp in enumerate(fgroups[40:]):
                                gi = 40 + gj
                                pm = pms[gi % 4]
                                t.wait_ge(s_ev, gi - 3)
                                kind = grp[0]
                                inst = None
                                if kind == "lp":
                                    _, mc, tc = grp
                                    t.wait_ge(s_ev, 40)
                                    for ct in range(4):
                                        inst = t.matmul(
                                            pm[:],
                                            pm_sb[:, ct * 256 + mc * 128:
                                                  ct * 256 + (mc + 1) * 128],
                                            lgT[:, ct * TPC + tc * 512:
                                                ct * TPC + (tc + 1) * 512],
                                            start=(ct == 0), stop=(ct == 3))
                                elif kind == "ret":
                                    _, dc, tc = grp
                                    t.wait_ge(s_ev, 48)
                                    for ct in range(2):
                                        inst = t.matmul(
                                            pm[:],
                                            lv_sb[:, ct * D + dc * 128:
                                                  ct * D + (dc + 1) * 128],
                                            lpT[:, ct * TPC + tc * 512:
                                                ct * TPC + (tc + 1) * 512],
                                            start=(ct == 0), stop=(ct == 1))
                                else:
                                    _, tau, half = grp
                                    t.wait_ge(s_ev, 80)
                                    for ct in range(8):
                                        inst = t.matmul(
                                            pm[:],
                                            retT[:, ct * TPC + tau * 128:
                                                 ct * TPC + (tau + 1) * 128],
                                            wout_sb[:, ct * D + half * 512:
                                                    ct * D + (half + 1) * 512],
                                            start=(ct == 0), stop=(ct == 7))
                                inst.then_inc(s_mm, 1)

                        @blk.scalar
                        def _(sc):
                            for gj, grp in enumerate(fgroups[40:]):
                                if grp[0] == "out":
                                    break
                                scalar_evac(sc, 40 + gj, grp)

                        @blk.vector
                        def _(v):
                            for oi, (tau, half) in enumerate(
                                    [(tau, half) for tau in range(NT)
                                     for half in range(2)]):
                                gi = 80 + oi
                                pm = pms[gi % 4]
                                v.wait_ge(s_mm, gi + 1)
                                if dbg:
                                    v.wait_ge(s_st, 17 * 16)
                                if oi >= 1:
                                    v.wait_ge(s_ev, 80 + oi)
                                v.tensor_tensor(
                                    wtmp[:, 0:512], pm[:],
                                    wob_sb[:, half * 512:(half + 1) * 512],
                                    ALU.add).then_inc(s_vf, 1)
                                v.wait_ge(s_vf, NT + oi + 1)
                                v.tensor_tensor(
                                    resid[:, tau * D + half * 512:
                                          tau * D + (half + 1) * 512],
                                    resid[:, tau * D + half * 512:
                                          tau * D + (half + 1) * 512],
                                    wtmp[:, 0:512],
                                    ALU.add).then_inc(s_ev, 1)

            @blk.sync
            def _(sy):
                for tau in range(NT):
                    sy.wait_ge(s_ev, 80 + 2 * tau + 2)
                    sy.dma_start(out=o_x2[tau * 128:(tau + 1) * 128, :],
                                 in_=resid[:, tau * D:(tau + 1) * D]
                                 ).then_inc(s_st, 16)
                sy.dma_start(out=io["agb_in"].rearrange("t d -> () t d"),
                             in_=resid[127:128, :]
                             .rearrange("q (t d) -> q t d", d=D)
                             ).then_inc(s_st, 16)
                sy.wait_ge(s_st, (NT + 1 + (17 if dbg else 0)) * 16)

            @blk.gpsimd
            def _(g):
                g.wait_ge(s_st, (NT + 1) * 16)
                g.collective_compute(
                    "AllGather", ALU.bypass,
                    replica_groups=[list(range(NC))],
                    ins=[io["agb_in"]], outs=[io["agb_out"]]).then_inc(s_cc)
                g.wait_ge(s_cc, 1)



def emit_C(nc, io, px="C"):
    xs = io["x2S"]
    agb_out = io["agb_out"]
    bsel = io["bsel"]
    wvT = io["wvT"]
    wkT = io["wkT"]
    wrT = io["wrT"]
    tmkT = io["tmkT"]
    tmvT = io["tmvT"]
    tmrT = io["tmrT"]
    tdT = io["tdT"]
    o_kv = io["kvS"]
    o_kc = io["kcS"]
    o_sr = io["srS"]

    cid = nc.inline_tensor(np.eye(128, dtype=BF), px + "cident")
    cidf = nc.inline_tensor(np.eye(128, dtype=np.float32), px + "cidentf")

    es = ExitStack()
    sb = lambda n, s, d=F32: es.enter_context(nc.sbuf_tensor(px + n, s, d))
    psal = lambda n, s, d=F32: es.enter_context(nc.psum_tensor(px + n, s, d))
    sem = lambda n: nc.alloc_semaphore(px + n)
    with es:
        ident = sb("ident", [128, 128], BF16)
        stat = sb("stat", [128, 8])
        sqs = sb("sqs", [128, D])
        eps_sb = sb("eps_sb", [128, 1])
        xw = sb("xw", [128, 2 * D])
        lnw = sb("lnw", [128, 2 * D], BF16)
        xlnT = sb("xlnT", [128, 8 * TPC], BF16)
        xxT = sb("xxT", [128, 8 * TPC], BF16)
        bndw = sb("bndw", [16, D])
        bndln = sb("bndln", [16, D], BF16)
        bndT = sb("bndT", [128, 8 * NT], BF16)
        tmL = {}
        tmL["v"] = sb("tmv_sb", [128, 8])
        tmL["k"] = sb("tmk_sb", [128, 8])
        tmL["r"] = sb("tmr_sb", [128, 8])
        tm1L = {}
        tm1L["v"] = sb("tm1v_sb", [128, 8])
        tm1L["k"] = sb("tm1k_sb", [128, 8])
        tm1L["r"] = sb("tm1r_sb", [128, 8])
        td_sb = sb("td_sb", [128, 8])
        dcol = sb("dcol", [128, 1])
        etd = sb("etd", [128, 1])
        dfull = sb("dfull", [128, 128])
        onecol = sb("onecol", [128, 128])
        mixT = sb("mixT", [128, 8 * TPC], BF16)
        p1buf = sb("p1buf", [128, TPC], BF16)
        vT_sb = sb("vT_sb", [128, 8 * TPC], BF16)
        w_sb = sb("w_sb", [128, 8 * D], BF16)
        kstage = sb("kstage", [128, TPC])
        kcst = sb("kcst", [128, TPC])
        kvst = sb("kvst", [128, TPC])
        srst = sb("srst", [128, 2 * 512])
        sexp = sb("sexp", [128, 2 * 512])
        scano = sb("scano", [128, 128])
        carr_sb = sb("carr_sb", [128, 8 * 32])
        identf = sb("identf", [128, 128])
        bsel_sb = sb("bsel_sb", [128, NT])
        agbS = kstage
        carr2 = kvst
        pt0 = psal("pt0", [128, 128], BF16)
        pt1 = psal("pt1", [128, 128], BF16)
        pm0 = psal("pm0", [128, 512])
        pm1 = psal("pm1", [128, 512])
        pm2 = psal("pm2", [128, 512])
        pm3 = psal("pm3", [128, 512])
        pcar = psal("pcar", [128, 128])
        pms = [pm0, pm1, pm2, pm3]

        s_ld = sem("s_ld")
        s_ldg = sem("s_ldg")
        s_x = sem("s_x")
        s_nm = sem("s_nm")
        s_rstd = sem("s_rstd")
        s_app = sem("s_app")
        s_tr = sem("s_tr")
        s_trv = sem("s_trv")
        s_spre = sem("s_spre")
        s_mx = sem("s_mx")
        s_xx = sem("s_xx")
        s_mm = sem("s_mm")
        s_ev = sem("s_ev")
        s_vf = sem("s_vf")
        s_sf = sem("s_sf")
        s_kc = sem("s_kc")
        s_kv = sem("s_kv")
        s_dc = sem("s_dc")
        s_scn = sem("s_scn")
        s_sg = sem("s_sg")
        s_st = sem("s_st")
        s_wol = sem("s_wol")
        s_rtc = sem("s_rtc")
        s_bm = sem("s_bm")
        s_bnde = sem("s_bnde")
        s_cat = sem("s_cat")
        s_care = sem("s_care")
        s_cc = sem("s_cc")

        vf = [0]
        sf = [0]

        def vfence(v, inst):
            vf[0] += 1
            inst.then_inc(s_vf, 1)
            v.wait_ge(s_vf, vf[0])

        def sfence(sc, inst):
            sf[0] += 1
            inst.then_inc(s_sf, 1)
            sc.wait_ge(s_sf, sf[0])

        projs = ["v", "k", "r"]
        TRT_C = 8 * NT

        with nc.Block() as blk:
            @blk.sync
            def _(sy):
                sy.dma_start(out=ident[:], in_=cid.ap()).then_inc(s_ld, 16)
                sy.dma_start(out=identf[:], in_=cidf.ap()).then_inc(s_ld, 16)
                sy.dma_start(out=agbS[:, 0:D], in_=agb_out).then_inc(s_ld, 16)
                sy.dma_start(out=bsel_sb[:], in_=bsel).then_inc(s_ld, 16)
                sy.dma_start(out=tmL["k"][:], in_=tmkT).then_inc(s_ld, 16)
                sy.dma_start(out=tmL["v"][:], in_=tmvT).then_inc(s_ld, 16)
                sy.dma_start(out=tmL["r"][:], in_=tmrT).then_inc(s_ld, 16)
                sy.dma_start(out=td_sb[:], in_=tdT).then_inc(s_ld, 16)
                for t in range(NT):
                    if t >= 2:
                        sy.wait_ge(s_rstd, 3 * (t - 1))
                        sy.wait_ge(s_app, t - 1)
                    sy.dma_start(out=xw[:, (t % 2) * D:(t % 2 + 1) * D],
                                 in_=xs[t * 128:(t + 1) * 128, :]).then_inc(s_x, 16)
                for dt in range(8):
                    sy.wait_ge(s_kc, dt + 1)
                    sy.dma_start(out=o_kc[dt], in_=kcst[:]).then_inc(s_st, 16)
                    sy.wait_ge(s_kv, dt + 1)
                    sy.dma_start(out=o_kv[dt], in_=kvst[:]).then_inc(s_st, 16)
                for dt in range(8):
                    for tc in range(4):
                        sy.wait_ge(s_sg, dt * 4 + tc + 1)
                        cc = (dt * 4 + tc) % 2
                        sy.dma_start(out=o_sr[dt, :, tc * 512:(tc + 1) * 512],
                                     in_=srst[:, cc * 512:(cc + 1) * 512]
                                     ).then_inc(s_st, 16)
                sy.wait_ge(s_care, 8)
                sy.dma_start(out=io["agc_in"], in_=carr2[0:32, 0:1024]
                             ).then_inc(s_st, 16)
                sy.wait_ge(s_st, (16 + 32 + 1) * 16)
            NLD = 8

            @blk.gpsimd
            def _(g):
                for pi, w in enumerate((wvT, wkT, wrT)):
                    if pi >= 1:
                        g.wait_ge(s_mm, pi * 32)
                    g.dma_start(out=w_sb[:].rearrange("p (c n) -> p c n", n=D),
                                in_=w.rearrange("(c p) n -> p c n", p=128)
                                ).then_inc(s_ldg, 16)
                g.wait_ge(s_st, (16 + 32 + 1) * 16)
                g.collective_compute(
                    "AllGather", ALU.bypass,
                    replica_groups=[list(range(NC))],
                    ins=[io["agc_in"]], outs=[io["agc_out"]]).then_inc(s_cc)
                g.wait_ge(s_cc, 1)

            xw_tile = lambda t: xw[:, (t % 2) * D:(t % 2 + 1) * D]
            lnw_tile = lambda t: lnw[:, (t % 2) * D:(t % 2 + 1) * D]

            def x_wait(v, t):
                v.wait_ge(s_x, (t + 1) * 16)

            v_ops, s_ops, v2_ops = emit_ln_plan(
                nc, None, xw_tile, lnw_tile, stat, sqs, NT, x_wait,
                SemRef(s_app), SemRef(s_rstd), SemRef(s_nm), eps_ap=eps_sb[:])

            @blk.vector
            def _(v):
                v.memset(eps_sb[:], LN_EPS)
                v.memset(onecol[:], 1.0)
                for t in range(NT):
                    if t >= 2:
                        v.wait_ge(s_tr, 8 * (t - 1))
                    v_ops(v, t)
                    v2_ops(v, t)
                    if t >= 1:
                        for ct in range(8):
                            j = 8 * (t - 1) + ct
                            v.wait_ge(s_tr, j + 1)
                            v.tensor_copy(
                                xlnT[:, ct * TPC + (t - 1) * 128:ct * TPC + t * 128],
                                (pt0 if j % 2 == 0 else pt1)[:]).then_inc(s_trv, 1)
                for ct in range(8):
                    j = 8 * (NT - 1) + ct
                    v.wait_ge(s_tr, j + 1)
                    v.tensor_copy(xlnT[:, ct * TPC + (NT - 1) * 128:
                                       ct * TPC + NT * 128],
                                  (pt0 if j % 2 == 0 else pt1)[:]).then_inc(s_trv, 1)
                v.wait_ge(s_bm, 2)
                for ch in range(2):
                    vfence(v, v.tensor_copy(
                        bndw[0:16, ch * 512:(ch + 1) * 512],
                        pms[ch][0:16, :]))
                v.engine_nop().then_inc(s_bnde, 1)
                v.wait_ge(s_ld, NLD * 16)
                vfence(v, v.reduce_sum(stat[0:16, 4:5], bndw[:], axis=AX.X))
                v.tensor_scalar_mul(stat[0:16, 0:1], stat[0:16, 4:5],
                                    -1.0 / D).then_inc(s_mx, 1)
                v.wait_ge(s_spre, 1)
                v.tensor_scalar(bndln[:], bndw[:], stat[0:16, 0:1],
                                stat[0:16, 2:3], ALU.add,
                                ALU.mult).then_inc(s_mx, 1)
                for ct in range(8):
                    j2 = TRT_C + ct
                    v.wait_ge(s_tr, j2 + 1)
                    v.tensor_copy(bndT[:, ct * NT:(ct + 1) * NT],
                                  (pt0 if j2 % 2 == 0 else pt1)[:, 0:16]
                                  ).then_inc(s_trv, 1)
                for p in projs:
                    inst = v.tensor_scalar(tm1L[p][:], tmL[p][:], -1.0, 1.0,
                                           ALU.mult, ALU.add)
                inst.then_inc(s_mx, 1)
                v.wait_ge(s_trv, TRT_C + 8)
                inst = None
                for ct in range(8):
                    for tau in range(NT):
                        v.tensor_copy(
                            xxT[:, ct * TPC + tau * 128 + 1:ct * TPC + (tau + 1) * 128],
                            xlnT[:, ct * TPC + tau * 128:ct * TPC + (tau + 1) * 128 - 1])
                        inst = v.tensor_copy(
                            xxT[:, ct * TPC + tau * 128:ct * TPC + tau * 128 + 1],
                            bndT[:, ct * NT + tau:ct * NT + tau + 1])
                inst.then_inc(s_xx, 1)
                for pi, p in enumerate(projs):
                    for ct in range(8):
                        if pi > 0 and ct == 0:
                            v.wait_ge(s_mm, pi * 32)
                        vfence(v, v.tensor_scalar(
                            p1buf[:], xxT[:, ct * TPC:(ct + 1) * TPC],
                            tm1L[p][:, ct:ct + 1], None, ALU.mult))
                        vfence(v, v.scalar_tensor_tensor(
                            mixT[:, ct * TPC:(ct + 1) * TPC],
                            xlnT[:, ct * TPC:(ct + 1) * TPC],
                            tmL[p][:, ct:ct + 1], p1buf[:],
                            ALU.mult, ALU.add))
                        v.engine_nop().then_inc(s_mx, 1)
                    for dt in range(8):
                        for tc in range(4):
                            gi = pi * 32 + dt * 4 + tc
                            pm = pms[gi % 4]
                            if p != "r":
                                v.wait_ge(s_mm, gi + 1)
                            if p == "v":
                                v.tensor_copy(
                                    vT_sb[:, dt * TPC + tc * 512:
                                          dt * TPC + (tc + 1) * 512],
                                    pm[:]).then_inc(s_ev, 1)
                            elif p == "k":
                                if dt >= 1 and tc == 0:
                                    v.wait_ge(s_st, (2 * dt) * 16)
                                v.tensor_single_scalar(
                                    kstage[:, tc * 512:(tc + 1) * 512],
                                    pm[:], 60.0, ALU.min).then_inc(s_ev, 1)
                            else:
                                pass
                        if p == "k":
                            v.wait_ge(s_kc, dt + 1)
                            v.tensor_tensor(
                                kvst[:], kcst[:],
                                vT_sb[:, dt * TPC:(dt + 1) * TPC],
                                ALU.mult).then_inc(s_kv, 1)
                            v.wait_ge(s_dc, dt + 1)
                            vfence(v, v.tensor_scalar_mul(dfull[:], onecol[:],
                                                          dcol[:]))
                            for tau in range(NT):
                                for si, srcbuf in enumerate((kvst, kcst)):
                                    vfence(v, v.tensor_tensor_scan(
                                        scano[:], dfull[:],
                                        srcbuf[:, tau * 128:(tau + 1) * 128],
                                        0.0, ALU.mult, ALU.add))
                                    vfence(v, v.tensor_copy(
                                        carr_sb[:, dt * 32 + tau * 2 + si:
                                                dt * 32 + tau * 2 + si + 1],
                                        scano[:, 127:128]))
                            v.engine_nop().then_inc(s_scn, 1)
                    if p == "r":
                        for dt in range(8):
                            for tc in range(4):
                                ci = dt * 4 + tc
                                cc = ci % 2
                                v.wait_ge(s_ev, 64 + ci + 1)
                                vfence(v, v.tensor_single_scalar(
                                    sexp[:, cc * 512:(cc + 1) * 512],
                                    sexp[:, cc * 512:(cc + 1) * 512],
                                    1.0, ALU.add))
                                if ci >= 2:
                                    v.wait_ge(s_st, (16 + ci - 1) * 16)
                                vfence(v, v.reciprocal(
                                    srst[:, cc * 512:(cc + 1) * 512],
                                    sexp[:, cc * 512:(cc + 1) * 512]))
                                v.engine_nop().then_inc(s_sg, 1)

            @blk.scalar
            def _(sc):
                for t in range(NT):
                    s_ops(sc, t)
                sc.wait_ge(s_mx, 1)
                sfence(sc, sc.activation(sqs[0:16, :], bndw[:], AF.Square,
                                         bias=stat[0:16, 0:1],
                                         accum_out=stat[0:16, 6:7]))
                sfence(sc, sc.activation(stat[0:16, 6:7], stat[0:16, 6:7], AF.Ln,
                                         scale=1.0 / D, bias=eps_sb[0:16, :]))
                sc.activation(stat[0:16, 2:3], stat[0:16, 6:7], AF.Exp,
                              scale=-0.5).then_inc(s_spre, 1)
                sc.wait_ge(s_ld, NLD * 16)
                for dt in range(8):
                    if dt >= 1:
                        sc.wait_ge(s_scn, dt)
                    sfence(sc, sc.activation(etd[:], td_sb[:, dt:dt + 1], AF.Exp))
                    sc.activation(dcol[:], etd[:], AF.Exp,
                                  scale=-1.0).then_inc(s_dc, 1)
                    sc.wait_ge(s_ev, 32 + dt * 4 + 4)
                    if dt >= 1:
                        sc.wait_ge(s_st, (2 * dt - 1) * 16)
                    sc.activation(kcst[:], kstage[:], AF.Exp).then_inc(s_kc, 1)
                for dt in range(8):
                    for tc in range(4):
                        ci = dt * 4 + tc
                        cc = ci % 2
                        gi = 64 + ci
                        pm = pms[gi % 4]
                        sc.wait_ge(s_mm, gi + 1)
                        if ci >= 2:
                            sc.wait_ge(s_sg, ci - 1)
                        sc.activation(sexp[:, cc * 512:(cc + 1) * 512], pm[:],
                                      AF.Exp, scale=-1.0).then_inc(s_ev, 1)

            @blk.tensor
            def _(t):
                t.wait_ge(s_ld, NLD * 16)
                for ch in range(2):
                    t.matmul(pms[ch][0:16, :], bsel_sb[:],
                             agbS[:, ch * 512:(ch + 1) * 512],
                             start=True, stop=True).then_inc(s_bm, 1)
                for tt in range(NT):
                    t.wait_ge(s_app, tt + 1)
                    for ct in range(8):
                        j = 8 * tt + ct
                        if j >= 2:
                            t.wait_ge(s_trv, j - 1)
                        t.transpose((pt0 if j % 2 == 0 else pt1)[:],
                                    lnw[:, (tt % 2) * D + ct * 128:
                                        (tt % 2) * D + (ct + 1) * 128],
                                    ident[:]).then_inc(s_tr, 1)
                t.wait_ge(s_mx, 2)
                for ct in range(8):
                    j2 = TRT_C + ct
                    t.wait_ge(s_trv, j2 - 1)
                    t.transpose((pt0 if j2 % 2 == 0 else pt1)[0:128, 0:16],
                                bndln[:, ct * 128:(ct + 1) * 128],
                                ident[0:16, 0:16]).then_inc(s_tr, 1)
                t.wait_ge(s_xx, 1)
                t.wait_ge(s_bnde, 1)
                for pi in range(3):
                    t.wait_ge(s_ldg, (pi + 1) * 16)
                    t.wait_ge(s_mx, 3 + (pi + 1) * 8)
                    for dt in range(8):
                        for tc in range(4):
                            gi = pi * 32 + dt * 4 + tc
                            pm = pms[gi % 4]
                            if gi >= 4:
                                t.wait_ge(s_ev, gi - 3)
                            inst = None
                            for ct in range(8):
                                inst = t.matmul(
                                    pm[:],
                                    w_sb[:, ct * D + dt * 128:ct * D + (dt + 1) * 128],
                                    mixT[:, ct * TPC + tc * 512:
                                         ct * TPC + (tc + 1) * 512],
                                    start=(ct == 0), stop=(ct == 7))
                            inst.then_inc(s_mm, 1)
                t.wait_ge(s_scn, 8)
                for dt in range(8):
                    if dt >= 1:
                        t.wait_ge(s_care, dt)
                    t.transpose(pcar[0:32, :], carr_sb[:, dt * 32:(dt + 1) * 32],
                                identf[:]).then_inc(s_cat, 1)

            @blk.vector
            def _(v):
                v.wait_ge(s_st, 16 * 16)
                for dt in range(8):
                    v.wait_ge(s_cat, dt + 1)
                    v.tensor_copy(carr2[0:32, dt * 128:(dt + 1) * 128],
                                  pcar[0:32, :]).then_inc(s_care, 1)



def emit_D(nc, io, px="D"):
    xs = io["x2S"]
    kvi = io["kvS"]
    kci = io["kcS"]
    sri = io["srS"]
    agc_out = io["agc_out"]
    tdT = io["tdT"]
    tfT = io["tfT"]
    woT = io["woT"]
    rwT = io["rwT"]
    selci = io["selci"]
    o_x3 = io["x3S"]
    o_ln3 = io["ln3S"]
    o_w = io["wdS"]
    o_sel = io["selS"]

    cidf = nc.inline_tensor(np.eye(128, dtype=np.float32), px + "cidentf")
    ciota = nc.inline_tensor(
        np.tile(np.arange(8, dtype=np.float32), (128, 1)), px + "ciota8")
    selmm_np = np.zeros((128, 8 * 64), np.float32)
    for t_ in range(2):
        for si_ in range(2):
            for b_ in range(B):
                for j_ in range(64):
                    r = (j_ % 8) * 32 + (b_ * 8 + j_ // 8) * 2 + si_
                    if r // 128 == t_:
                        selmm_np[r % 128,
                                 (t_ * 4 + si_ * 2 + b_) * 64 + j_] = 1.0
    cselmm = nc.inline_tensor(selmm_np, px + "cselmm")
    RMS_EPS = float(np.finfo(np.float32).eps)
    KEPS = 1e-8

    es = ExitStack()
    sb = lambda n, s, d=F32: es.enter_context(nc.sbuf_tensor(px + n, s, d))
    psal = lambda n, s, d=F32: es.enter_context(nc.psum_tensor(px + n, s, d))
    sem = lambda n: nc.alloc_semaphore(px + n)
    with es:
        resid = sb("resid", [128, NT * D])
        identf = sb("identf", [128, 128])
        iota8 = sb("iota8", [128, 8])
        stat = sb("stat", [128, 8])
        sqs = sb("sqs", [128, D])
        eps_sb = sb("eps_sb", [128, 1])
        eps2_sb = sb("eps2_sb", [128, 1])
        xln3 = sb("xln3", [128, NT * D], BF16)
        td_sb = sb("td_sb", [128, 8])
        tf_sb = sb("tf_sb", [128, 8])
        etd = sb("etd", [128, 1])
        dcol = sb("dcol", [128, 1])
        d128col = sb("d128col", [128, 1])
        tfe = sb("tfe", [128, 1])
        dfull = sb("dfull", [128, 128])
        d128full = sb("d128full", [128, 64])
        onecol = sb("onecol", [128, 128])
        cin = sb("cin", [128, 2 * 2 * B * 64])
        sin = sb("sin", [128, B * 2 * 8])
        zbuf = sb("zbuf", [128, 64])
        z8 = sb("z8", [64, 128])
        z8b = sb("z8b", [8, 128])
        kvb = sb("kvb", [128, 2 * TPC])
        kcb = sb("kcb", [128, 2 * TPC])
        srb = sb("srb", [128, 2 * TPC], BF16)
        sA = sb("sA", [128, 128])
        sB = sb("sB", [128, 128])
        eA = sb("eA", [128, 128])
        eB = sb("eB", [128, 128])
        rDb = sb("rDb", [128, 128])
        preT = sb("preT", [128, 8 * TPC], BF16)
        wo_sb = sb("wo_sb", [128, 8 * D], BF16)
        rw_sb = sb("rw_sb", [128, 8 * 8])
        selc_sb = sb("selc_sb", [64, 8])
        onescol = sb("onescol", [128, 1])
        xtT = sb("xtT", [128, 8 * 128])
        lg_sb = sb("lg_sb", [128, 8])
        pr_sb = sb("pr_sb", [128, 8])
        t8a = sb("t8a", [128, 8])
        t8b = sb("t8b", [128, 8])
        t8c = sb("t8c", [128, 8])
        t8d = sb("t8d", [128, 8])
        t8e = sb("t8e", [128, 8])
        c0 = sb("c0", [128, 1])
        c1 = sb("c1", [128, 1])
        c2 = sb("c2", [128, 1])
        c3 = sb("c3", [128, 1])
        c4 = sb("c4", [128, 1])
        c5 = sb("c5", [128, 1])
        wd_sb = sb("wd_sb", [128, NT * 8])
        sel_sb = sb("sel_sb", [128, NT * 8])
        cnt_sb = sb("cnt_sb", [1, 128])
        selmm_sb = sb("selmm_sb", [128, 8 * 64])
        agcS0 = sqs
        agcS1 = xtT
        pt0 = psal("pt0", [128, 128])
        pt1 = psal("pt1", [128, 128])
        pm0 = psal("pm0", [128, 512])
        pm1 = psal("pm1", [128, 512])
        pr0 = psal("pr0", [128, 8])
        pc0 = psal("pc0", [1, 8])
        pcin2 = psal("pcin2", [128, 64])

        s_ld = sem("s_ld")
        s_kv = sem("s_kv")
        s_nm = sem("s_nm")
        s_rstd = sem("s_rstd")
        s_app = sem("s_app")
        s_spre = sem("s_spre")
        s_mm = sem("s_mm")
        s_ev = sem("s_ev")
        s_vf = sem("s_vf")
        s_sf = sem("s_sf")
        s_ct = sem("s_ct")
        s_cz = sem("s_cz")
        s_cze = sem("s_cze")
        s_rt = sem("s_rt")
        s_rte = sem("s_rte")
        s_sq = sem("s_sq")
        s_tr = sem("s_tr")
        s_trv = sem("s_trv")
        s_st = sem("s_st")
        s_wol = sem("s_wol")
        s_rtc = sem("s_rtc")
        s_agl = sem("s_agl")
        s_cj = sem("s_cj")
        s_cjv2 = sem("s_cjv2")
        s_cjt = sem("s_cjt")
        s_cjv = sem("s_cjv")
        s_cc = sem("s_cc")

        vf = [0]
        sf = [0]

        def vfence(v, inst):
            vf[0] += 1
            inst.then_inc(s_vf, 1)
            v.wait_ge(s_vf, vf[0])

        def sfence(sc, inst):
            sf[0] += 1
            inst.then_inc(s_sf, 1)
            sc.wait_ge(s_sf, sf[0])

        NCARR = 32

        with nc.Block() as blk:
            @blk.sync
            def _(sy):
                sy.dma_start(out=identf[:], in_=cidf.ap()).then_inc(s_ld, 16)
                sy.dma_start(out=iota8[:], in_=ciota.ap()).then_inc(s_ld, 16)
                sy.dma_start(out=td_sb[:], in_=tdT).then_inc(s_ld, 16)
                sy.dma_start(out=tf_sb[:], in_=tfT).then_inc(s_ld, 16)
                sy.dma_start(out=rw_sb[:].rearrange("p (c n) -> p c n", n=8),
                             in_=rwT.rearrange("(c p) n -> p c n", p=128)
                             ).then_inc(s_ld, 16)
                sy.dma_start(out=selc_sb[:], in_=selci).then_inc(s_ld, 16)
                sy.dma_start(out=selmm_sb[:], in_=cselmm.ap()).then_inc(s_agl, 16)
                sy.dma_start(out=agcS0[:, 0:1024], in_=agc_out[0:128, :]
                             ).then_inc(s_agl, 16)
                sy.dma_start(out=agcS1[:, 0:1024], in_=agc_out[128:256, :]
                             ).then_inc(s_agl, 16)
                for t in range(NT):
                    sy.dma_start(out=resid[:, t * D:(t + 1) * D],
                                 in_=xs[t * 128:(t + 1) * 128, :]).then_inc(s_ld, 16)
                sy.wait_ge(s_ev, 2 * NT)
                sy.dma_start(out=o_x3.rearrange("(t p) d -> p t d", p=128),
                             in_=resid[:].rearrange("p (t d) -> p t d", d=D)
                             ).then_inc(s_st, 16)
                sy.wait_ge(s_app, NT)
                sy.dma_start(out=o_ln3.rearrange("(t p) d -> p t d", p=128),
                             in_=xln3[:].rearrange("p (t d) -> p t d", d=D)
                             ).then_inc(s_st, 16)
                sy.wait_ge(s_rte, 3 * NT)
                sy.dma_start(out=o_w.rearrange("t p e -> p t e"),
                             in_=wd_sb[:].rearrange("p (t e) -> p t e", e=8)
                             ).then_inc(s_st, 16)
                sy.dma_start(out=o_sel.rearrange("t p e -> p t e"),
                             in_=sel_sb[:].rearrange("p (t e) -> p t e", e=8)
                             ).then_inc(s_st, 16)
                sy.dma_start(out=io["agn_in"].rearrange("t e -> () (t e)"),
                             in_=cnt_sb[:]).then_inc(s_st, 16)
                sy.wait_ge(s_st, 5 * 16)
            NLDC = 6

            @blk.gpsimd
            def _(g):
                g.dma_start(out=wo_sb[:].rearrange("p (c n) -> p c n", n=D),
                            in_=woT.rearrange("(c p) n -> p c n", p=128)
                            ).then_inc(s_wol, 16)
                for ct in range(8):
                    if ct >= 2:
                        g.wait_ge(s_ct, ct - 1)
                    g.dma_start(out=kvb[:, (ct % 2) * TPC:(ct % 2 + 1) * TPC],
                                in_=kvi[ct]).then_inc(s_kv, 16)
                    g.dma_start(out=kcb[:, (ct % 2) * TPC:(ct % 2 + 1) * TPC],
                                in_=kci[ct]).then_inc(s_kv, 16)
                    g.dma_start(out=srb[:, (ct % 2) * TPC:(ct % 2 + 1) * TPC],
                                in_=sri[ct]).then_inc(s_kv, 16)
                g.wait_ge(s_st, 5 * 16)
                g.collective_compute(
                    "AllGather", ALU.bypass,
                    replica_groups=[list(range(NC))],
                    ins=[io["agn_in"]], outs=[io["agn_out"]]).then_inc(s_cc)
                g.wait_ge(s_cc, 1)

            @blk.scalar
            def _(sc):
                sc.wait_ge(s_ld, 4 * 16)
                for ct in range(8):
                    if ct >= 1:
                        sc.wait_ge(s_ct, ct)
                    sfence(sc, sc.activation(etd[:], td_sb[:, ct:ct + 1], AF.Exp))
                    sfence(sc, sc.activation(dcol[:], etd[:], AF.Exp, scale=-1.0))
                    sfence(sc, sc.activation(d128col[:], etd[:], AF.Exp,
                                             scale=-128.0))
                    inst = sc.activation(tfe[:], tf_sb[:, ct:ct + 1], AF.Exp)
                    inst.then_inc(s_spre, 1)

            @blk.vector
            def _(v):
                v.memset(eps_sb[:], LN_EPS)
                v.memset(eps2_sb[:], RMS_EPS)
                v.memset(onecol[:], 1.0)
                v.memset(onescol[:], 1.0)
                for ct in range(8):
                    v.wait_ge(s_spre, ct + 1)
                    vfence(v, v.tensor_scalar_mul(dfull[:], onecol[:], dcol[:]))
                    vfence(v, v.tensor_scalar_mul(d128full[:], onecol[:, 0:64],
                                                  d128col[:]))
                    for si_ in range(2):
                        for b in range(B):
                            cidx = ct * 4 + si_ * B + b
                            slot = (ct % 2) * 4 + si_ * B + b
                            v.wait_ge(s_cjt, cidx + 1)
                            vfence(v, v.tensor_copy(
                                cin[:, slot * 64:(slot + 1) * 64],
                                pcin2[:, 0:64]))
                            v.engine_nop().then_inc(s_cjv, 1)
                            vfence(v, v.tensor_tensor_scan(
                                zbuf[:], d128full[:],
                                cin[:, slot * 64:(slot + 1) * 64],
                                0.0, ALU.mult, ALU.add))
                            v.engine_nop().then_inc(s_cze, 1)
                            v.wait_ge(s_cz, cidx * 3 + 1)
                            vfence(v, v.tensor_copy(z8[:], pt0[0:64, :]))
                            v.engine_nop().then_inc(s_cze, 1)
                            v.wait_ge(s_cz, cidx * 3 + 2)
                            vfence(v, v.tensor_copy(z8b[:], pt1[0:8, :]))
                            v.engine_nop().then_inc(s_cze, 1)
                            v.wait_ge(s_cz, cidx * 3 + 3)
                            vfence(v, v.tensor_copy(
                                sin[:, (b * 2 + si_) * 8:(b * 2 + si_) * 8 + 8],
                                pt0[:, 0:8]))
                            v.engine_nop().then_inc(s_cze, 1)
                    v.wait_ge(s_kv, 3 * (ct + 1) * 16)
                    for tau in range(NT):
                        b, kk = divmod(tau, 8)
                        p = ct % 2
                        kvt = kvb[:, p * TPC + tau * 128:p * TPC + (tau + 1) * 128]
                        kct = kcb[:, p * TPC + tau * 128:p * TPC + (tau + 1) * 128]
                        srt = srb[:, p * TPC + tau * 128:p * TPC + (tau + 1) * 128]
                        inA = sin[:, (b * 2 + 0) * 8 + kk:(b * 2 + 0) * 8 + kk + 1]
                        inB = sin[:, (b * 2 + 1) * 8 + kk:(b * 2 + 1) * 8 + kk + 1]
                        vfence(v, v.tensor_tensor_scan(sA[:], dfull[:], kvt,
                                                       inA, ALU.mult, ALU.add))
                        vfence(v, v.tensor_tensor_scan(sB[:], dfull[:], kct,
                                                       inB, ALU.mult, ALU.add))
                        vfence(v, v.tensor_copy(eA[:, 0:1], inA))
                        vfence(v, v.tensor_copy(eA[:, 1:128], sA[:, 0:127]))
                        vfence(v, v.tensor_single_scalar(eB[:, 0:1], inB, KEPS,
                                                         ALU.add))
                        vfence(v, v.tensor_single_scalar(eB[:, 1:128],
                                                         sB[:, 0:127], KEPS,
                                                         ALU.add))
                        vfence(v, v.scalar_tensor_tensor(eA[:], kvt, tfe[:],
                                                         eA[:], ALU.mult,
                                                         ALU.add))
                        vfence(v, v.scalar_tensor_tensor(eB[:], kct, tfe[:],
                                                         eB[:], ALU.mult,
                                                         ALU.add))
                        vfence(v, v.reciprocal(rDb[:], eB[:]))
                        vfence(v, v.tensor_tensor(eA[:], eA[:], rDb[:], ALU.mult))
                        vfence(v, v.tensor_tensor(
                            preT[:, ct * TPC + tau * 128:
                                 ct * TPC + (tau + 1) * 128],
                            eA[:], srt, ALU.mult))
                    v.engine_nop().then_inc(s_ct, 1)
                for tau in range(NT):
                    for half in range(2):
                        gi = tau * 2 + half
                        pm = [pm0, pm1][gi % 2]
                        v.wait_ge(s_mm, gi + 1)
                        v.wait_ge(s_ld, (NLDC + tau + 1) * 16)
                        v.scalar_tensor_tensor(
                            resid[:, tau * D + half * 512:
                                  tau * D + (half + 1) * 512],
                            pm[:], 1.0,
                            resid[:, tau * D + half * 512:
                                  tau * D + (half + 1) * 512],
                            ALU.mult, ALU.add).then_inc(s_ev, 1)

            xw_tile = lambda t: resid[:, t * D:(t + 1) * D]
            lnw_tile = lambda t: xln3[:, t * D:(t + 1) * D]

            def x_wait(v, t):
                v.wait_ge(s_ev, 2 * (t + 1))

            v_ops, s_ops, v2_ops = emit_ln_plan(
                nc, None, xw_tile, lnw_tile, stat, sqs, NT, x_wait,
                SemRef(s_app), SemRef(s_rstd), SemRef(s_nm), eps_ap=eps_sb[:])

            @blk.vector
            def _(v):
                for t in range(NT):
                    v_ops(v, t)
                    v2_ops(v, t)
                for tau in range(NT):
                    if tau >= 1:
                        v.wait_ge(s_rt, tau)
                    for ct in range(8):
                        v.wait_ge(s_tr, 8 * tau + ct + 1)
                        v.tensor_copy(xtT[:, ct * 128:(ct + 1) * 128],
                                      (pt0 if ct % 2 == 0 else pt1)[:]
                                      ).then_inc(s_trv, 1)
                    v.wait_ge(s_rt, tau + 1)
                    v.wait_ge(s_sq, tau + 1)
                    vfence(v, v.tensor_scalar_mul(lg_sb[:], pr0[:],
                                                  stat[:, 3:4]))
                    vfence(v, v.reduce_max(c0[:], lg_sb[:], axis=AX.X))
                    vfence(v, v.tensor_scalar(pr_sb[:], lg_sb[:], c0[:], None,
                                              ALU.subtract))
                    v.engine_nop().then_inc(s_rte, 1)
                    v.wait_ge(s_sq, NT + tau + 1)
                    vfence(v, v.reciprocal(c4[:], c5[:]))
                    vfence(v, v.tensor_scalar_mul(pr_sb[:], pr_sb[:], c4[:]))
                    vfence(v, v.reduce_max(c0[:], pr_sb[:], axis=AX.X))
                    vfence(v, v.tensor_scalar(t8a[:], pr_sb[:], c0[:], None,
                                              ALU.is_equal))
                    vfence(v, v.tensor_tensor(t8e[:], iota8[:], t8a[:], ALU.mult))
                    vfence(v, v.tensor_scalar(t8c[:], t8a[:], -999.0, 999.0,
                                              ALU.mult, ALU.add))
                    vfence(v, v.tensor_tensor(t8e[:], t8e[:], t8c[:], ALU.add))
                    vfence(v, v.tensor_reduce(c1[:], t8e[:], AX.X, ALU.min))
                    vfence(v, v.tensor_scalar(t8b[:], iota8[:], c1[:], None,
                                              ALU.is_equal))
                    vfence(v, v.scalar_tensor_tensor(t8c[:], t8b[:], -1000.0,
                                                     pr_sb[:], ALU.mult,
                                                     ALU.add))
                    vfence(v, v.reduce_max(c2[:], t8c[:], axis=AX.X))
                    vfence(v, v.tensor_scalar(t8d[:], t8c[:], c2[:], None,
                                              ALU.is_equal))
                    vfence(v, v.tensor_tensor(t8e[:], iota8[:], t8d[:], ALU.mult))
                    vfence(v, v.tensor_scalar(t8c[:], t8d[:], -999.0, 999.0,
                                              ALU.mult, ALU.add))
                    vfence(v, v.tensor_tensor(t8e[:], t8e[:], t8c[:], ALU.add))
                    vfence(v, v.tensor_reduce(c3[:], t8e[:], AX.X, ALU.min))
                    vfence(v, v.tensor_scalar(t8d[:], iota8[:], c3[:], None,
                                              ALU.is_equal))
                    vfence(v, v.tensor_tensor(c4[:], c0[:], c2[:], ALU.add))
                    vfence(v, v.reciprocal(c5[:], c4[:]))
                    vfence(v, v.tensor_scalar_mul(t8a[:], t8b[:], c0[:]))
                    vfence(v, v.tensor_scalar_mul(t8c[:], t8d[:], c2[:]))
                    vfence(v, v.tensor_tensor(t8a[:], t8a[:], t8c[:], ALU.add))
                    vfence(v, v.tensor_scalar_mul(
                        wd_sb[:, tau * 8:(tau + 1) * 8], t8a[:], c5[:]))
                    vfence(v, v.tensor_tensor(sel_sb[:, tau * 8:(tau + 1) * 8],
                                              t8b[:], t8d[:], ALU.add))
                    v.engine_nop().then_inc(s_rte, 1)
                    v.wait_ge(s_rtc, tau + 1)
                    vfence(v, v.tensor_copy(cnt_sb[0:1, tau * 8:(tau + 1) * 8],
                                            pc0[0:1, :]))
                    v.engine_nop().then_inc(s_rte, 1)

            @blk.scalar
            def _(sc):
                for t in range(NT):
                    s_ops(sc, t)
                for tau in range(NT):
                    sc.wait_ge(s_app, tau + 1)
                    sfence(sc, sc.activation(sqs[:],
                                             resid[:, tau * D:(tau + 1) * D],
                                             AF.Square, accum_out=stat[:, 5:6]))
                    sfence(sc, sc.activation(stat[:, 5:6], stat[:, 5:6], AF.Ln,
                                             scale=1.0 / D, bias=eps2_sb[:]))
                    inst = sc.activation(stat[:, 3:4], stat[:, 5:6], AF.Exp,
                                         scale=-0.5)
                    inst.then_inc(s_sq, 1)
                for tau in range(NT):
                    sc.wait_ge(s_rte, 3 * tau + 1)
                    inst = sc.activation(pr_sb[:], pr_sb[:], AF.Exp,
                                         accum_out=c5[:])
                    inst.then_inc(s_sq, 1)

            @blk.tensor
            def _(t):
                t.wait_ge(s_ld, 6 * 16)
                t.wait_ge(s_agl, 3 * 16)
                for ct in range(8):
                    for si_ in range(2):
                        for b in range(B):
                            cidx = ct * 4 + si_ * B + b
                            if cidx >= 1:
                                t.wait_ge(s_cjv, cidx)
                            c0_ = (0 * 4 + si_ * 2 + b) * 64
                            c1_ = (1 * 4 + si_ * 2 + b) * 64
                            t.matmul(pcin2[:, 0:64],
                                     agcS0[:, ct * 128:(ct + 1) * 128],
                                     selmm_sb[:, c0_:c0_ + 64],
                                     start=True, stop=False)
                            t.matmul(pcin2[:, 0:64],
                                     agcS1[:, ct * 128:(ct + 1) * 128],
                                     selmm_sb[:, c1_:c1_ + 64],
                                     start=False, stop=True).then_inc(s_cjt, 1)
                            t.wait_ge(s_cze, cidx * 4 + 1)
                            t.transpose(pt0[0:64, :], zbuf[:], identf[:]
                                        ).then_inc(s_cz, 1)
                            t.wait_ge(s_cze, cidx * 4 + 2)
                            t.matmul(pt1[0:8, :], selc_sb[:], z8[:],
                                     start=True, stop=True).then_inc(s_cz, 1)
                            t.wait_ge(s_cze, cidx * 4 + 3)
                            t.transpose(pt0[:, 0:8], z8b[:], identf[0:8, 0:8]
                                        ).then_inc(s_cz, 1)
                t.wait_ge(s_ct, 8)
                t.wait_ge(s_wol, 16)
                for tau in range(NT):
                    for half in range(2):
                        gi = tau * 2 + half
                        pm = [pm0, pm1][gi % 2]
                        if gi >= 2:
                            t.wait_ge(s_ev, gi - 1)
                        inst = None
                        for ct in range(8):
                            inst = t.matmul(
                                pm[:],
                                preT[:, ct * TPC + tau * 128:
                                     ct * TPC + (tau + 1) * 128],
                                wo_sb[:, ct * D + half * 512:
                                      ct * D + (half + 1) * 512],
                                start=(ct == 0), stop=(ct == 7))
                        inst.then_inc(s_mm, 1)
                def count_mm(tau):
                    t.wait_ge(s_rte, 3 * tau + 2)
                    if tau >= 1:
                        t.wait_ge(s_rte, 3 * (tau - 1) + 3)
                    t.matmul(pc0[:], onescol[:],
                             sel_sb[:, tau * 8:(tau + 1) * 8],
                             start=True, stop=True).then_inc(s_rtc, 1)

                for tau in range(NT):
                    if tau >= 1:
                        count_mm(tau - 1)
                    t.wait_ge(s_ev, 2 * (tau + 1))
                    for ct in range(8):
                        j = 8 * tau + ct
                        if j >= 2:
                            t.wait_ge(s_trv, j - 1)
                        t.transpose((pt0 if ct % 2 == 0 else pt1)[:],
                                    resid[:, tau * D + ct * 128:
                                          tau * D + (ct + 1) * 128],
                                    identf[:]).then_inc(s_tr, 1)
                    t.wait_ge(s_trv, 8 * (tau + 1))
                    if tau >= 1:
                        t.wait_ge(s_rte, 3 * (tau - 1) + 1)
                    inst = None
                    for ct in range(8):
                        inst = t.matmul(pr0[:], xtT[:, ct * 128:(ct + 1) * 128],
                                        rw_sb[:, ct * 8:(ct + 1) * 8],
                                        start=(ct == 0), stop=(ct == 7))
                    inst.then_inc(s_rt, 1)
                count_mm(NT - 1)



SLOTS = 512
CAP = 2560.0


def emit_E(nc, io, px="E"):
    x3i = io["x3S"]
    ln3i = io["ln3S"]
    wdi = io["wdS"]
    seli = io["selS"]
    cnta = io["agn_out"]
    selpre = io["selpre"]
    rgi = io["rgi"]
    rui = io["rui"]
    rdi = io["rdi"]
    sgi = io["sgi"]
    sui = io["sui"]
    sdi = io["sdi"]
    o_out = io["out"]

    cidf = nc.inline_tensor(np.eye(128, dtype=np.float32), px + "cidentf")
    cid = nc.inline_tensor(np.eye(128, dtype=BF), px + "cident")
    ctriu = nc.inline_tensor(np.triu(np.ones((128, 128), np.float32), 1),
                             px + "ctriu")
    ciota512 = nc.inline_tensor(
        np.tile(np.arange(512, dtype=np.float32), (128, 1)), px + "ciota512")
    conesr = nc.inline_tensor(np.ones((1, 128), np.float32), px + "conesr")
    _bt = np.zeros((128, 128), np.float32)
    for _j in range(128):
        for _i in range(128):
            if _j // 8 < _i // 8 and _j % 8 == _i % 8:
                _bt[_j, _i] = 1.0
    cbtriu = nc.inline_tensor(_bt, px + "cbtriu")

    sched = []
    for e in range(8):
        for tau in range(NT):
            sched.append(("dmt", e, tau))
        for dt in range(8):
            sched.append(("ga", e, dt))
        sched.append(("wg", e, 0))
        for it in range(32):
            sched.append(("gt", e, it))
        for it in range(32):
            sched.append(("up", e, it))
        for q in range(4):
            for ncn in range(2):
                for sc in range(4):
                    sched.append(("dn", e, (q, sc, ncn)))
        for tau in range(NT):
            for ncn in range(2):
                sched.append(("sct", e, (tau, ncn)))
    gidx = {}
    n = 0
    for g in sched:
        if g[0] != "dmt":
            n += 1
            gidx[g] = n
    EVENG = {"ga": "v", "wg": "v", "gt": "s", "up": "v", "dn": "v", "sct": "v"}
    rotseq = [g for g in sched if g[0] in ("ga", "gt", "up", "dn", "sct")]
    rotpos = {g: i for i, g in enumerate(rotseq)}
    evidx = {}
    evcnt = {"v": 0, "s": 0}
    for g in sched:
        k = g[0]
        if k == "dmt" or k == "wg":
            continue
        eng = EVENG[k]
        evcnt[eng] += 1
        evidx[g] = (eng, evcnt[eng])

    piecestream = []
    for e in range(8):
        for nm in ("rg", "ru", "rd"):
            for p in range(16):
                piecestream.append((e, nm, p))

    def piece_lastcons(e, nm, p):
        if nm == "rg":
            return gidx[("gt", e, 2 * p + 1)]
        if nm == "ru":
            return gidx[("up", e, 2 * p + 1)]
        q, ncn = divmod(p // 2, 2)
        return gidx[("dn", e, (q, 3, ncn))]

    lastcons = [piece_lastcons(*ps) for ps in piecestream]
    ready = {}
    ldg_cnt = [1]
    stg_cnt = [0]
    for j, ps in enumerate(piecestream):
        if j % 2 == 0:
            ldg_cnt[0] += 1
            ready[ps] = ("ldg", ldg_cnt[0] * 16)
        else:
            stg_cnt[0] += 2
            ready[ps] = ("cst", stg_cnt[0])
    NGPL = ldg_cnt[0]

    es = ExitStack()
    sb = lambda nm, s, d=F32: es.enter_context(nc.sbuf_tensor(px + nm, s, d))
    psal = lambda nm, s, d=F32: es.enter_context(nc.psum_tensor(px + nm, s, d))
    sem = lambda nm: nc.alloc_semaphore(px + nm)
    with es:
        racc = sb("racc", [128, NT * D], BF16)
        identf = sb("identf", [128, 128])
        ident = sb("ident", [128, 128], BF16)
        triu = sb("triu", [128, 128])
        iota512 = sb("iota512", [128, 512])
        onesr = sb("onesr", [1, 128])
        onescol = sb("onescol", [128, 1])
        wd_sb = sb("wd_sb", [128, NT * 8])
        sel_sb = sb("sel_sb", [128, NT * 8])
        kp_sb = sb("kp_sb", [128, NT * 8])
        wp_sb = sb("wp_sb", [128, NT * 8])
        srk_sb = sb("srk_sb", [128, NT * 8])
        cnt_sb = sb("cnt_sb", [128, 8])
        selp_sb = sb("selp_sb", [128, NT])
        offs_sb = sb("offs_sb", [16, 8])
        btriu = sb("btriu", [128, 128])
        e1tmp = sb("e1tmp", [128, 2])
        offr = sb("offr", [1, 128])
        offr2 = sb("offr2", [1, 128])
        offacc = sb("offacc", [1, 8])
        totr = sb("totr", [1, 8])
        offb = sb("offb", [128, 128])
        offb2 = sb("offb2", [128, 128])
        wsl = sb("wsl", [128, 4])
        wpb_sb = sb("wpb_sb", [128, NT * 8], BF16)

        s_ld = sem("s_ld")
        s_ldg = sem("s_ldg")
        s_ld2 = sem("s_ld2")
        s_cst = sem("s_cst")
        s_ofd = sem("s_ofd")
        s_rkt = sem("s_rkt")
        s_rk = sem("s_rk")
        s_mm = sem("s_mm")
        s_evv = sem("s_evv")
        s_evs = sem("s_evs")
        s_vf = sem("s_vf")
        s_dm = sem("s_dm")
        s_eb = sem("s_eb")
        s_wgv = sem("s_wgv")
        s_tr = sem("s_tr")
        s_trv = sem("s_trv")
        s_sh = sem("s_sh")
        s_sil = sem("s_sil")
        s_hm = sem("s_hm")
        s_trv2 = sem("s_trv2")
        s_acc = sem("s_acc")
        s_ldx = sem("s_ldx")
        s_lds = sem("s_lds")
        s_fin = sem("s_fin")
        s_st = sem("s_st")

        vf = [0]

        def vfence(v, inst):
            vf[0] += 1
            inst.then_inc(s_vf, 1)
            v.wait_ge(s_vf, vf[0])

        def rot_wait(t, g):
            r = rotpos[g]
            if r >= 4:
                prev = rotseq[r - 4]
                if prev[0] == "gt":
                    t.wait_ge(s_evs, evidx[prev][1])
                else:
                    t.wait_ge(s_evv, evidx[prev][1])

        esp = ExitStack()
        pt0 = esp.enter_context(nc.psum_tensor(px + "pt0", [128, 128], BF16))
        pt1 = esp.enter_context(nc.psum_tensor(px + "pt1", [128, 128], BF16))
        pa = esp.enter_context(nc.psum_tensor(px + "pa", [128, 512], F32))
        pb = esp.enter_context(nc.psum_tensor(px + "pb", [128, 512], F32))
        pc_ = esp.enter_context(nc.psum_tensor(px + "pc_", [128, 512], F32))
        pd_ = esp.enter_context(nc.psum_tensor(px + "pd_", [128, 512], F32))
        psmA = esp.enter_context(nc.psum_tensor(px + "psmA", [128, 512], F32))
        psmB = esp.enter_context(nc.psum_tensor(px + "psmB", [128, 512], F32))
        pr8 = psmA[:, 0:8]
        pc8 = psmB[0:1, 0:8]
        po16 = psmB[0:16, 8:16]
        pw4 = psmB[:, 16:20]
        pms4 = [pa, pb, pc_, pd_]

        with nc.Block() as blk:
            @blk.sync
            def _(sy):
                sy.dma_start(out=identf[:], in_=cidf.ap()).then_inc(s_ld, 16)
                sy.dma_start(out=ident[:], in_=cid.ap()).then_inc(s_ld, 16)
                sy.dma_start(out=triu[:], in_=ctriu.ap()).then_inc(s_ld, 16)
                sy.dma_start(out=iota512[:], in_=ciota512.ap()).then_inc(s_ld, 16)
                sy.dma_start(out=onesr[:], in_=conesr.ap()).then_inc(s_ld, 16)
                sy.dma_start(out=wd_sb[:].rearrange("p (t e) -> p t e", e=8),
                             in_=wdi.rearrange("t p e -> p t e")).then_inc(s_ld, 16)
                sy.dma_start(out=sel_sb[:].rearrange("p (t e) -> p t e", e=8),
                             in_=seli.rearrange("t p e -> p t e")).then_inc(s_ld, 16)
                sy.dma_start(out=cnt_sb[:], in_=cnta).then_inc(s_ld, 16)
                sy.dma_start(out=selp_sb[:], in_=selpre).then_inc(s_ld, 16)
                sy.dma_start(out=btriu[:], in_=cbtriu.ap()).then_inc(s_ld, 16)
            NLD0 = 10

            @blk.tensor
            def _(t):
                t.wait_ge(s_ld, NLD0 * 16)
                t.matmul(po16, selp_sb[:], cnt_sb[:], start=True,
                         stop=True).then_inc(s_rkt, 1)
                t.wait_ge(s_ofd, 16)
                t.matmul(pa[:, 0:128], onesr[:], offr[:],
                         start=True, stop=True).then_inc(s_rkt, 1)
                t.matmul(psmA[:, 0:128], triu[:], sel_sb[:, 0:128],
                         start=True, stop=True).then_inc(s_rkt, 1)
                t.wait_ge(s_rk, 3)
                t.matmul(psmB[0:1, 32:160], onescol[:], kp_sb[:, 0:128],
                         start=True, stop=True).then_inc(s_rkt, 1)
                t.wait_ge(s_rk, 4)
                t.transpose(psmA[:, 128:129], offr[0:1, :], identf[0:1, 0:1]
                            ).then_inc(s_rkt, 1)
                t.wait_ge(s_rk, 5)
                t.matmul(psmA[:, 192:193], btriu[:], e1tmp[:, 0:1],
                         start=True, stop=True).then_inc(s_rkt, 1)
                t.wait_ge(s_rk, 6)
                t.transpose(psmB[0:1, 192:320], e1tmp[:, 1:2], identf[:]
                            ).then_inc(s_rkt, 1)
                t.wait_ge(s_rk, 7)
                t.matmul(pa[:, 0:128], onesr[:], offr2[:],
                         start=True, stop=True).then_inc(s_rkt, 1)
                t.wait_ge(s_rk, 8)
                t.matmul(psmA[:, 0:128], triu[:], kp_sb[:, 0:128],
                         start=True, stop=True).then_inc(s_rkt, 1)

            @blk.gpsimd
            def _(g):
                g.wait_ge(s_rk, 1)
                g.dma_start(out=offr[0:1, :], in_=offs_sb[:]).then_inc(s_ofd, 16)

            @blk.vector
            def _(v):
                v.memset(onescol[:], 1.0)
                v.memset(offacc[:], 0.0)
                v.memset(racc[:], 0.0)
                v.wait_ge(s_rkt, 1)
                vfence(v, v.tensor_copy(offs_sb[:], po16))
                v.engine_nop().then_inc(s_rk, 1)
                v.wait_ge(s_rkt, 2)
                vfence(v, v.tensor_copy(offb[:], pa[:, 0:128]))
                v.engine_nop().then_inc(s_rk, 1)
                v.wait_ge(s_rkt, 3)
                vfence(v, v.tensor_tensor(kp_sb[:, 0:128], psmA[:, 0:128],
                                          offb[:], ALU.add))
                vfence(v, v.tensor_single_scalar(
                    kp_sb[:, 0:128], kp_sb[:, 0:128], CAP, ALU.is_lt))
                vfence(v, v.tensor_tensor(kp_sb[:, 0:128], kp_sb[:, 0:128],
                                          sel_sb[:, 0:128], ALU.mult))
                vfence(v, v.tensor_tensor(wp_sb[:, 0:128], kp_sb[:, 0:128],
                                          wd_sb[:, 0:128], ALU.mult))
                vfence(v, v.tensor_copy(wpb_sb[:], wp_sb[:]))
                v.engine_nop().then_inc(s_rk, 1)
                v.wait_ge(s_rkt, 4)
                vfence(v, v.tensor_copy(offr[0:1, :], psmB[0:1, 32:160]))
                v.engine_nop().then_inc(s_rk, 1)
                v.wait_ge(s_rkt, 5)
                vfence(v, v.tensor_copy(e1tmp[:, 0:1], psmA[:, 128:129]))
                v.engine_nop().then_inc(s_rk, 1)
                v.wait_ge(s_rkt, 6)
                vfence(v, v.tensor_copy(e1tmp[:, 1:2], psmA[:, 192:193]))
                v.engine_nop().then_inc(s_rk, 1)
                v.wait_ge(s_rkt, 7)
                vfence(v, v.tensor_copy(offr2[0:1, :], psmB[0:1, 192:320]))
                v.engine_nop().then_inc(s_rk, 1)
                v.wait_ge(s_rkt, 8)
                vfence(v, v.tensor_copy(offb2[:], pa[:, 0:128]))
                v.engine_nop().then_inc(s_rk, 1)
                v.wait_ge(s_rkt, 9)
                vfence(v, v.tensor_tensor(srk_sb[:, 0:128], psmA[:, 0:128],
                                          offb2[:], ALU.add))
                v.engine_nop().then_inc(s_rk, 1)

            with ExitStack() as esx:
                sbx = lambda nm, s, d=F32: esx.enter_context(
                    nc.sbuf_tensor(px + nm, s, d))
                xln3 = sbx("xln3", [128, NT * D], BF16)
                wbuf = sbx("wbuf", [128, 8192], BF16)
                stg = sbx("stg", [128, 2048], F32)
                dmT = sbx("dmT", [128, 4 * TPC], BF16)
                xgT = sbx("xgT", [128, 8 * 512], BF16)
                hT = sbx("hT", [128, 32 * 512], BF16)
                eacc = sbx("eacc", [128, 4 * D])
                ebf = sbx("ebf", [128, 4 * D], BF16)
                dmat = sbx("dmat", [128, NT * 512], BF16)

                WTS = {"rg": rgi, "ru": rui, "rd": rdi}

                def piece_src(e, nm, p):
                    wt = WTS[nm]
                    if nm == "rd":
                        q, ncn, rh = p // 4, (p // 2) % 2, p % 2
                        return wt[e, q * 1024 + rh * 512:
                                  q * 1024 + (rh + 1) * 512,
                                  ncn * 512:(ncn + 1) * 512]
                    return wt[e, :, p * 256:(p + 1) * 256]

                @blk.gpsimd
                def _(g):
                    g.dma_start(out=xln3[:].rearrange("p (t d) -> p t d", d=D),
                                in_=ln3i.rearrange("(t p) d -> p t d", p=128)
                                ).then_inc(s_ldg, 16)
                    for j, (e, nm, p) in enumerate(piecestream):
                        if j % 2 != 0:
                            continue
                        if j >= 4:
                            g.wait_ge(s_mm, lastcons[j - 4])
                        reg = (j % 4) * 2048
                        if nm == "rd":
                            g.dma_start(
                                out=wbuf[:, reg:reg + 2048].rearrange(
                                    "p (c n) -> p c n", n=512),
                                in_=piece_src(e, nm, p).rearrange(
                                    "(c p) n -> p c n", p=128)
                            ).then_inc(s_ldg, 16)
                        else:
                            g.dma_start(
                                out=wbuf[:, reg:reg + 2048].rearrange(
                                    "p (c n) -> p c n", n=256),
                                in_=piece_src(e, nm, p).rearrange(
                                    "(c p) n -> p c n", p=128)
                            ).then_inc(s_ldg, 16)

                @blk.sync
                def _(sy):
                    slot = 0
                    for j, (e, nm, p) in enumerate(piecestream):
                        if j % 2 != 1:
                            continue
                        if j >= 4:
                            sy.wait_ge(s_mm, lastcons[j - 4])
                        src = piece_src(e, nm, p)
                        nn = 512 if nm == "rd" else 256
                        rs = 256 if nm == "rd" else 512
                        for s in range(2):
                            if slot >= 2:
                                sy.wait_ge(s_cst, slot - 1)
                            so = (slot % 2) * 1024
                            sy.dma_start(
                                out=stg[:, so:so + 1024].rearrange(
                                    "p (c n) -> p c n", n=nn),
                                in_=src[s * rs:(s + 1) * rs, :]
                                .rearrange("(c p) n -> p c n", p=128)
                            ).then_inc(s_ld2, 16)
                            slot += 1

                @blk.tensor
                def _(t):
                    t.wait_ge(s_ldg, 16)
                    tcount = 0
                    for g in sched:
                        kind, e, arg = g
                        if kind == "dmt":
                            tau = arg
                            t.wait_ge(s_dm, e + 1)
                            for sc in range(4):
                                tcount += 1
                                if tcount >= 3:
                                    t.wait_ge(s_trv, tcount - 2)
                                t.transpose(
                                    (pt0 if tcount % 2 == 0 else pt1)[:],
                                    dmat[:, tau * 512 + sc * 128:
                                         tau * 512 + (sc + 1) * 128],
                                    ident[:]).then_inc(s_tr, 1)
                            continue
                        if kind == "wg":
                            if e >= 1:
                                t.wait_ge(s_wgv, e)
                            t.wait_ge(s_dm, e + 1)
                            inst = None
                            for sc in range(4):
                                for tau in range(NT):
                                    inst = t.matmul(
                                        pw4[:, sc:sc + 1],
                                        dmat[:, tau * 512 + sc * 128:
                                             tau * 512 + (sc + 1) * 128],
                                        wpb_sb[:, tau * 8 + e:tau * 8 + e + 1],
                                        start=(tau == 0), stop=(tau == NT - 1))
                            inst.then_inc(s_mm, 1)
                            continue
                        pm = pms4[rotpos[g] % 4]
                        rot_wait(t, g)
                        inst = None
                        if kind == "ga":
                            dt = arg
                            t.wait_ge(s_dm, e + 1)
                            for tau in range(NT):
                                inst = t.matmul(
                                    pm[:],
                                    xln3[:, tau * D + dt * 128:
                                         tau * D + (dt + 1) * 128],
                                    dmat[:, tau * 512:(tau + 1) * 512],
                                    start=(tau == 0), stop=(tau == NT - 1))
                        elif kind in ("gt", "up"):
                            it = arg
                            p, i2 = divmod(it, 2)
                            nm = "rg" if kind == "gt" else "ru"
                            if i2 == 0:
                                sk, sv = ready[(e, nm, p)]
                                t.wait_ge(s_ldg if sk == "ldg" else s_cst, sv)
                            reg = (p % 4) * 2048
                            for ct in range(8):
                                inst = t.matmul(
                                    pm[:],
                                    wbuf[:, reg + ct * 256 + i2 * 128:
                                         reg + ct * 256 + (i2 + 1) * 128],
                                    xgT[:, ct * 512:(ct + 1) * 512],
                                    start=(ct == 0), stop=(ct == 7))
                        elif kind == "dn":
                            q, sc, ncn = arg
                            p1 = q * 4 + ncn * 2
                            p2 = p1 + 1
                            r1 = (p1 % 4) * 2048
                            r2 = (p2 % 4) * 2048
                            if sc == 0:
                                for pp in (p1, p2):
                                    sk, sv = ready[(e, "rd", pp)]
                                    t.wait_ge(s_ldg if sk == "ldg" else s_cst,
                                              sv)
                            for itl in range(8):
                                it = q * 8 + itl
                                woff = (r1 + itl * 512 if itl < 4
                                        else r2 + (itl - 4) * 512)
                                inst = t.matmul(
                                    pm[:],
                                    hT[:, it * 512 + sc * 128:
                                       it * 512 + (sc + 1) * 128],
                                    wbuf[:, woff:woff + 512],
                                    start=(itl == 0), stop=(itl == 7))
                        elif kind == "sct":
                            tau, ncn = arg
                            t.wait_ge(s_eb, e + 1)
                            for sc in range(4):
                                inst = t.matmul(
                                    pm[:],
                                    dmT[:, sc * TPC + tau * 128:
                                        sc * TPC + (tau + 1) * 128],
                                    ebf[:, sc * D + ncn * 512:
                                        sc * D + (ncn + 1) * 512],
                                    start=(sc == 0), stop=(sc == 3))
                        inst.then_inc(s_mm, 1)

                @blk.vector
                def _(v):
                    trv_cnt = 0
                    for e in range(8):
                        if e >= 1:
                            v.wait_ge(s_mm, gidx[("sct", e - 1, (NT - 1, 1))])
                        for tau in range(NT):
                            vfence(v, v.tensor_scalar(
                                dmat[:, tau * 512:(tau + 1) * 512],
                                iota512[:],
                                srk_sb[:, tau * 8 + e:tau * 8 + e + 1],
                                kp_sb[:, tau * 8 + e:tau * 8 + e + 1],
                                ALU.is_equal, ALU.mult))
                        v.engine_nop().then_inc(s_dm, 1)
                        for tau in range(NT):
                            for sc in range(4):
                                trv_cnt += 1
                                v.wait_ge(s_tr, trv_cnt)
                                v.tensor_copy(
                                    dmT[:, sc * TPC + tau * 128:
                                        sc * TPC + (tau + 1) * 128],
                                    (pt0 if trv_cnt % 2 == 0 else pt1)[:]
                                ).then_inc(s_trv, 1)
                        for g in sched:
                            kind, ge, arg = g
                            if ge != e or kind == "dmt":
                                continue
                            if kind == "wg":
                                v.wait_ge(s_mm, gidx[g])
                                vfence(v, v.tensor_copy(wsl[:], pw4))
                                v.engine_nop().then_inc(s_wgv, 1)
                                continue
                            if kind == "gt":
                                continue
                            pm = pms4[rotpos[g] % 4]
                            v.wait_ge(s_mm, gidx[g])
                            if kind == "ga":
                                dt = arg
                                v.tensor_copy(xgT[:, dt * 512:(dt + 1) * 512],
                                              pm[:]).then_inc(s_evv, 1)
                            elif kind == "up":
                                it = arg
                                v.wait_ge(s_evs, 32 * e + it + 1)
                                v.tensor_tensor(
                                    hT[:, it * 512:(it + 1) * 512],
                                    hT[:, it * 512:(it + 1) * 512],
                                    pm[:], ALU.mult).then_inc(s_evv, 1)
                            elif kind == "dn":
                                q, sc, ncn = arg
                                dst = eacc[:, sc * D + ncn * 512:
                                           sc * D + (ncn + 1) * 512]
                                if q == 0:
                                    v.tensor_copy(dst, pm[:]).then_inc(s_evv, 1)
                                else:
                                    v.tensor_tensor(dst, dst, pm[:], ALU.add
                                                    ).then_inc(s_evv, 1)
                                if arg == (3, 3, 1):
                                    v.wait_ge(s_wgv, e + 1)
                                    for sc2 in range(4):
                                        vfence(v, v.tensor_scalar_mul(
                                            ebf[:, sc2 * D:(sc2 + 1) * D],
                                            eacc[:, sc2 * D:(sc2 + 1) * D],
                                            wsl[:, sc2:sc2 + 1]))
                                    v.engine_nop().then_inc(s_eb, 1)
                            elif kind == "sct":
                                tau, ncn = arg
                                dst = racc[:, tau * D + ncn * 512:
                                           tau * D + (ncn + 1) * 512]
                                v.tensor_tensor(dst, dst, pm[:], ALU.add
                                                ).then_inc(s_evv, 1)

                @blk.scalar
                def _(sc):
                    def silu(e, it):
                        g = ("gt", e, it)
                        pm = pms4[rotpos[g] % 4]
                        sc.wait_ge(s_mm, gidx[g])
                        sc.activation(hT[:, it * 512:(it + 1) * 512],
                                      pm[:], AF.Silu).then_inc(s_evs, 1)

                    slot = 0
                    for j, (e, nm, p) in enumerate(piecestream):
                        if j % 2 == 1:
                            reg = (j % 4) * 2048
                            for s in range(2):
                                so = (slot % 2) * 1024
                                sc.wait_ge(s_ld2, 16 * (slot + 1))
                                inst = sc.activation(
                                    wbuf[:, reg + s * 1024:
                                         reg + (s + 1) * 1024],
                                    stg[:, so:so + 1024], AF.Copy)
                                inst.then_inc(s_cst, 1)
                                slot += 1
                        if nm == "rg" and p >= 2:
                            silu(e, 2 * (p - 2))
                            silu(e, 2 * (p - 2) + 1)
                        elif nm == "ru" and p == 0:
                            for it in range(28, 32):
                                silu(e, it)

            esp.close()
            NEXP = gidx[("sct", 7, (NT - 1, 1))]
            TRSH = 8 * NT * 4
            with ExitStack() as esh:
                sbh = lambda nm, s, d=F32: esh.enter_context(
                    nc.sbuf_tensor(px + nm, s, d))
                xlnT = sbh("xlnT", [128, 8 * TPC], BF16)
                acc = sbh("acc", [128, NT * D])
                sgb = sbh("sgb", [128, 2 * 4096], BF16)
                sub = sbh("sub", [128, 2 * 4096], BF16)
                sdb = sbh("sdb", [128, 2 * 4096], BF16)
                gtile = sbh("gtile", [128, 512], BF16)
                htile = sbh("htile", [128, 512], BF16)
                hTt = sbh("hTt", [128, 512], BF16)
                pg = esh.enter_context(nc.psum_tensor(px + "pg", [128, 512], F32))
                pu = esh.enter_context(nc.psum_tensor(px + "pu", [128, 512], F32))
                pdn = esh.enter_context(nc.psum_tensor(px + "pdn", [128, 1024], F32))
                ptA = esh.enter_context(nc.psum_tensor(px + "ptA", [128, 128], BF16))
                ptB = esh.enter_context(nc.psum_tensor(px + "ptB", [128, 128], BF16))

                @blk.sync
                def _(sy):
                    sy.wait_ge(s_mm, NEXP)
                    for ct in range(8):
                        sy.dma_start(out=xlnT[:, ct * TPC:(ct + 1) * TPC],
                                     in_=ln3i[:, ct * 128:(ct + 1) * 128],
                                     transpose=True).then_inc(s_lds, 16)
                    sy.dma_start(out=acc[:].rearrange("p (t d) -> p t d", d=D),
                                 in_=x3i.rearrange("(t p) d -> p t d", p=128)
                                 ).then_inc(s_ldx, 16)

                @blk.gpsimd
                def _(g):
                    g.wait_ge(s_mm, NEXP)
                    for ic in range(16):
                        if ic >= 2:
                            g.wait_ge(s_sh, 3 * NT * (ic - 1))
                        g.dma_start(
                            out=sgb[:, (ic % 2) * 4096:(ic % 2 + 1) * 4096]
                            .rearrange("p (c n) -> p c n", n=512),
                            in_=sgi[:, ic * 512:(ic + 1) * 512]
                            .rearrange("(c p) n -> p c n", p=128)
                        ).then_inc(s_ldg, 16)
                        g.dma_start(
                            out=sub[:, (ic % 2) * 4096:(ic % 2 + 1) * 4096]
                            .rearrange("p (c n) -> p c n", n=512),
                            in_=sui[:, ic * 512:(ic + 1) * 512]
                            .rearrange("(c p) n -> p c n", p=128)
                        ).then_inc(s_ldg, 16)
                        g.dma_start(
                            out=sdb[:, (ic % 2) * 4096:(ic % 2 + 1) * 4096]
                            .rearrange("p (c n) -> p c n", n=D),
                            in_=sdi[ic * 512:(ic + 1) * 512, :]
                            .rearrange("(c p) n -> p c n", p=128)
                        ).then_inc(s_ldg, 16)

                NW = NGPL

                def shld(ic):
                    return (NW + 3 * (ic + 1)) * 16

                @blk.tensor
                def _(t):
                    t.wait_ge(s_lds, 8 * 16)
                    for ic in range(16):
                        t.wait_ge(s_ldg, shld(ic))
                        for tau in range(NT):
                            G = ic * NT + tau
                            base = (ic % 2) * 4096
                            if G >= 1:
                                t.wait_ge(s_sil, G)
                            inst = None
                            for ct in range(8):
                                inst = t.matmul(
                                    pg[:],
                                    xlnT[:, ct * TPC + tau * 128:
                                         ct * TPC + (tau + 1) * 128],
                                    sgb[:, base + ct * 512:base + (ct + 1) * 512],
                                    start=(ct == 0), stop=(ct == 7))
                            inst.then_inc(s_sh, 1)
                            if G >= 1:
                                t.wait_ge(s_hm, G)
                            for ct in range(8):
                                inst = t.matmul(
                                    pu[:],
                                    xlnT[:, ct * TPC + tau * 128:
                                         ct * TPC + (tau + 1) * 128],
                                    sub[:, base + ct * 512:base + (ct + 1) * 512],
                                    start=(ct == 0), stop=(ct == 7))
                            inst.then_inc(s_sh, 1)
                            t.wait_ge(s_hm, G + 1)
                            for isub in range(4):
                                j = 4 * G + isub
                                if j >= 2:
                                    t.wait_ge(s_trv2, j - 1)
                                t.transpose((ptA if j % 2 == 0 else ptB)[:],
                                            htile[:, isub * 128:(isub + 1) * 128],
                                            ident[:]).then_inc(s_tr, 1)
                            t.wait_ge(s_trv2, 4 * (G + 1))
                            if G >= 1:
                                t.wait_ge(s_acc, 2 * G)
                            inst = None
                            for ncn in range(2):
                                for isub in range(4):
                                    inst = t.matmul(
                                        pdn[:, ncn * 512:(ncn + 1) * 512],
                                        hTt[:, isub * 128:(isub + 1) * 128],
                                        sdb[:, base + isub * 1024 + ncn * 512:
                                            base + isub * 1024 + (ncn + 1) * 512],
                                        start=(isub == 0), stop=(isub == 3))
                            inst.then_inc(s_sh, 1)

                @blk.scalar
                def _(sc):
                    for ic in range(16):
                        for tau in range(NT):
                            G = ic * NT + tau
                            sc.wait_ge(s_sh, 3 * G + 1)
                            if G >= 1:
                                sc.wait_ge(s_hm, G)
                            sc.activation(gtile[:], pg[:], AF.Silu
                                          ).then_inc(s_sil, 1)

                @blk.vector
                def _(v):
                    tr_base = TRSH
                    for ic in range(16):
                        for tau in range(NT):
                            G = ic * NT + tau
                            v.wait_ge(s_sil, G + 1)
                            v.wait_ge(s_sh, 3 * G + 2)
                            if G >= 1:
                                v.wait_ge(s_tr, tr_base + 4 * G)
                            vfence(v, v.tensor_tensor(htile[:], gtile[:],
                                                      pu[:], ALU.mult))
                            v.engine_nop().then_inc(s_hm, 1)
                            for isub in range(4):
                                j = 4 * G + isub
                                v.wait_ge(s_tr, tr_base + j + 1)
                                if G >= 1 and isub == 0:
                                    v.wait_ge(s_sh, 3 * G)
                                v.tensor_copy(
                                    hTt[:, isub * 128:(isub + 1) * 128],
                                    (ptA if j % 2 == 0 else ptB)[:]
                                ).then_inc(s_trv2, 1)
                            v.wait_ge(s_sh, 3 * G + 3)
                            v.wait_ge(s_ldx, 16)
                            for ncn in range(2):
                                v.tensor_tensor(
                                    acc[:, tau * D + ncn * 512:
                                        tau * D + (ncn + 1) * 512],
                                    acc[:, tau * D + ncn * 512:
                                        tau * D + (ncn + 1) * 512],
                                    pdn[:, ncn * 512:(ncn + 1) * 512],
                                    ALU.add).then_inc(s_acc, 1)
                    for tau in range(NT):
                        v.tensor_tensor(acc[:, tau * D:(tau + 1) * D],
                                        acc[:, tau * D:(tau + 1) * D],
                                        racc[:, tau * D:(tau + 1) * D],
                                        ALU.add).then_inc(s_fin, 1)

                @blk.sync
                def _(sy):
                    sy.wait_ge(s_fin, NT)
                    sy.dma_start(out=o_out.rearrange("(t p) d -> p t d", p=128),
                                 in_=acc[:].rearrange("p (t d) -> p t d", d=D)
                                 ).then_inc(s_st, 16)
                    sy.wait_ge(s_st, 16)


class SemRef:
    def __init__(self, h):
        self.h = h



def declare_io(nc):
    io = {}
    ext = [
        ("xs", [TPC, D], F32), ("wqp", [HP, D, 128], F32),
        ("wkp", [HP, D, 128], F32), ("wvf", [D, D], F32),
        ("bqT", [DH, H], F32), ("bkT", [DH, H], F32), ("bvf", [1, D], F32),
        ("wop", [HP, 128, D], F32), ("bo", [128, D], F32),
        ("wqT", [D, D], F32), ("rtT", [D, 255], F32), ("lv", [256, D], F32),
        ("woutT", [D, D], F32), ("wob", [128, D], F32),
        ("mk", [8, 128, 128], BF16),
        ("bsel", [128, NT], F32),
        ("wvT", [D, D], F32), ("wkT", [D, D], F32), ("wrT", [D, D], F32),
        ("tmkT", [128, 8], F32), ("tmvT", [128, 8], F32),
        ("tmrT", [128, 8], F32), ("tdT", [128, 8], F32),
        ("tfT", [128, 8], F32), ("woT", [D, D], F32), ("rwT", [D, 8], F32),
        ("selci", [64, 8], F32), ("selpre", [128, NT], F32),
        ("rgi", [8, D, 4096], F32), ("rui", [8, D, 4096], F32),
        ("rdi", [8, 4096, D], F32), ("sgi", [D, 8192], F32),
        ("sui", [D, 8192], F32), ("sdi", [8192, D], F32),
    ]
    for n, s, d in ext:
        io[n] = nc.dram_tensor(n, s, d, kind="ExternalInput").ap()
    io["out"] = nc.dram_tensor("out", [TPC, D], F32, kind="ExternalOutput").ap()
    scr = [
        ("qS", [HP, 128, TPC], BF16),
        ("agk_in", [HP, 128, TPC], BF16),
        ("agk_out", [NC * HP, 128, TPC], BF16),
        ("agv_in", [TPC, D], BF16), ("agv_out", [NC * TPC, D], BF16),
        ("x2S", [TPC, D], F32),
        ("agb_in", [NT, D], F32), ("agb_out", [NC * NT, D], F32),
        ("kvS", [8, 128, TPC], F32), ("kcS", [8, 128, TPC], F32),
        ("srS", [8, 128, TPC], F32),
        ("agc_in", [32, 1024], F32), ("agc_out", [NC * 32, 1024], F32),
        ("x3S", [TPC, D], F32), ("ln3S", [TPC, D], BF16),
        ("wdS", [NT, 128, 8], F32), ("selS", [NT, 128, 8], F32),
        ("agn_in", [NT, 8], F32), ("agn_out", [NC * NT, 8], F32),
    ]
    for n, s, d in scr:
        io[n] = nc.dram_tensor(n, s, d).ap()
    return io


def build_F():
    nc = bass.Bass("TRN2", target_bir_lowering=False, debug=False)
    io = declare_io(nc)
    io["ccv"] = nc.alloc_semaphore("ccv")
    for emit in (emit_A, emit_B, emit_C, emit_D, emit_E):
        with nc.cleanup_on_exit():
            emit(nc, io)
            nc.all_engine_barrier()
    return nc



def prep_shared(inputs):
    s = {}
    WQ = inputs["L0_attn_W_Q"]
    WK = inputs["L0_attn_W_K"]
    WV = inputs["L0_attn_W_V"]
    s["wqp"] = np.ascontiguousarray(
        WQ.reshape(HP, 2, D, DH).transpose(0, 2, 1, 3).reshape(HP, D, 128))
    s["wkp"] = np.ascontiguousarray(
        WK.reshape(HP, 2, D, DH).transpose(0, 2, 1, 3).reshape(HP, D, 128))
    s["wvf"] = np.ascontiguousarray(WV.transpose(1, 0, 2).reshape(D, D))
    s["bqT"] = np.ascontiguousarray(inputs["L0_attn_b_Q"].T)
    s["bkT"] = np.ascontiguousarray(inputs["L0_attn_b_K"].T)
    s["bvf"] = np.ascontiguousarray(inputs["L0_attn_b_V"].reshape(1, D))
    WO = np.asarray(inputs["L0_attn_W_O"])
    s["wop"] = np.ascontiguousarray(WO.reshape(HP, 128, D))
    s["bo"] = np.ascontiguousarray(np.broadcast_to(
        np.asarray(inputs["L0_attn_b_O"]).reshape(1, D), (128, D)))
    s["wqT"] = np.ascontiguousarray(np.asarray(inputs["L0_ffn_Wq"]).T)
    s["rtT"] = np.ascontiguousarray(np.asarray(inputs["L0_ffn_routers"]).T)
    s["lv"] = np.ascontiguousarray(np.asarray(inputs["L0_ffn_leaf_values"]))
    s["woutT"] = np.ascontiguousarray(np.asarray(inputs["L0_ffn_Wout_w"]).T)
    s["wob"] = np.ascontiguousarray(np.broadcast_to(
        np.asarray(inputs["L0_ffn_Wout_b"]).reshape(1, D), (128, D)))
    s["wvT"] = np.ascontiguousarray(np.asarray(inputs["L1_attn_Wv"]).T)
    s["wkT"] = np.ascontiguousarray(np.asarray(inputs["L1_attn_Wk"]).T)
    s["wrT"] = np.ascontiguousarray(np.asarray(inputs["L1_attn_Wr"]).T)
    s["tmkT"] = np.ascontiguousarray(
        np.asarray(inputs["L1_attn_time_mix_k"]).reshape(8, 128).T)
    s["tmvT"] = np.ascontiguousarray(
        np.asarray(inputs["L1_attn_time_mix_v"]).reshape(8, 128).T)
    s["tmrT"] = np.ascontiguousarray(
        np.asarray(inputs["L1_attn_time_mix_r"]).reshape(8, 128).T)
    s["tdT"] = np.ascontiguousarray(
        np.asarray(inputs["L1_attn_time_decay"]).reshape(8, 128).T)
    s["tfT"] = np.ascontiguousarray(
        np.asarray(inputs["L1_attn_time_first"]).reshape(8, 128).T)
    s["woT"] = np.ascontiguousarray(np.asarray(inputs["L1_attn_Wo"]).T)
    s["rwT"] = np.ascontiguousarray(np.asarray(inputs["L1_ffn_router_w"]).T)
    s["rgi"] = np.ascontiguousarray(np.asarray(inputs["L1_ffn_rg"]))
    s["rui"] = np.ascontiguousarray(np.asarray(inputs["L1_ffn_ru"]))
    s["rdi"] = np.ascontiguousarray(np.asarray(inputs["L1_ffn_rd"]))
    s["sgi"] = np.ascontiguousarray(np.asarray(inputs["L1_ffn_sg"]))
    s["sui"] = np.ascontiguousarray(np.asarray(inputs["L1_ffn_su"]))
    s["sdi"] = np.ascontiguousarray(np.asarray(inputs["L1_ffn_sd"]))
    return s


def make_mask(core):
    tri = np.triu(np.ones((128, 128))).astype(BF)
    m = np.zeros((8, 128, 128), dtype=BF)
    for r in range(8):
        if r < core:
            m[r] = 1.0
        elif r == core:
            m[r] = tri
    return m


def make_bsel(core):
    bs = np.zeros((128, NT), np.float32)
    for b in range(B):
        for k in range(8):
            tau = b * 8 + k
            if core > 0:
                bs[(core - 1) * 16 + tau, tau] = 1.0
            elif k > 0:
                bs[7 * 16 + tau - 1, tau] = 1.0
    return bs


def make_selc(core):
    s = np.zeros((64, 8), np.float32)
    for k in range(8):
        j = 8 * k + core - 1
        if j >= 0:
            s[j, k] = 1.0
    return s


def make_selpre(core):
    sp = np.zeros((128, NT), np.float32)
    for r in range(128):
        c, tp = divmod(r, 16)
        br, kr = divmod(tp, 8)
        gr = br * 64 + 8 * kr + c
        for b in range(B):
            for k in range(8):
                if gr < b * 64 + 8 * k + core:
                    sp[r, b * 8 + k] = 1.0
    return sp


def kernel(**inputs):
    x = np.asarray(inputs["L0_attn_normalized_resid_pre"], dtype=np.float32)
    xsh = shard_rows(x)
    shared = prep_shared(inputs)
    maps = [dict(xs=xsh[i], mk=make_mask(i), bsel=make_bsel(i),
                 selci=make_selc(i), selpre=make_selpre(i), **shared)
            for i in range(NC)]
    nc = build_F()
    res = run_bass_kernel_spmd(nc, maps, core_ids=list(range(NC)))
    out = unshard_rows([np.asarray(r["out"]) for r in res.results])
    return np.ascontiguousarray(out.reshape(B, SQ, D).astype(np.float32))
